# version 14
# baseline (speedup 1.0000x reference)
import math
import numpy as np
import ml_dtypes
import concourse.bass as bass
import concourse.mybir as mybir
from concourse.bass_utils import run_bass_kernel_spmd

F32 = mybir.dt.float32
BF16 = mybir.dt.bfloat16
I32 = mybir.dt.int32
AF = mybir.ActivationFunctionType
OP = mybir.AluOpType
AX = mybir.AxisListType

D = 2048
T = 2048
TO = 1024
H = 16
INTER = 5632
NDC = D // 128
NHC = INTER // 128
EPS = 1e-5
RS = 1.0 / math.sqrt(128.0)
LN1E4 = math.log(10000.0)
TWO_PI = 2.0 * math.pi
C1 = 6.28125
C2 = TWO_PI - C1
PI_F32 = float(np.float32(np.pi))
PI_LO = float(np.nextafter(np.float32(np.pi), np.float32(0.0)))

ENGINES = ("sync", "vector", "scalar", "tensor", "gpsimd")
ESEM = {"vector": "dv", "scalar": "ac", "tensor": "pe", "gpsimd": "gp", "sync": "sp"}

DEBUG_TAPS = set()

_INV_FREQ_BITS = np.array([1065353216,1063104471,1061157137,1059470816,1058010522,1056527311,1054337179,1052440603,1050798235,1049376004,1047712803,1045579757,1043732615,1042133056,1040747895,1038909398,1036831949,1035032952,1033475085,1032126029,1030116802,1028093503,1026341397,1024824136,1023510243,1021334738,1019364175,1017657739,1016180025,1014779187,1012562930,1010643729,1008981770,1007542573,1005959600,1003801109,1001931932,1000313292,998911607,997151246,995049016,993228559,991652109,990286959,988353832,986306396,984533390,982998030,981668463,979567074,977573004,975846212,974350870,973033388,970790693,968848599,967166816,965710450,964208661,962024420,960132947,958494998,957076594,955395298], dtype=np.uint32)


_TABLE_CACHE = {}


def _rope_tables_f32():
    if "t" in _TABLE_CACHE:
        return _TABLE_CACHE["t"]
    import os
    import subprocess
    import sys
    import tempfile
    inv = _INV_FREQ_BITS.view(np.float32)
    t = np.arange(T, dtype=np.float32)
    fr = np.einsum("i,j->ij", t, inv).astype(np.float32)
    emb = np.concatenate([fr, fr], axis=-1)
    cos = sin = None
    try:
        with tempfile.TemporaryDirectory() as td:
            np.save(os.path.join(td, "emb.npy"), emb)
            prog = ("import numpy as np, jax.numpy as jnp, sys, os;"
                    "td = sys.argv[1];"
                    "e = jnp.asarray(np.load(os.path.join(td, 'emb.npy')));"
                    "np.save(os.path.join(td, 'cos.npy'), np.asarray(jnp.cos(e)));"
                    "np.save(os.path.join(td, 'sin.npy'), np.asarray(jnp.sin(e)));"
                    "print('TBLOK')")
            env = {k: v for k, v in os.environ.items() if k != "PYTHONPATH"}
            env["JAX_PLATFORMS"] = "cpu"
            r = subprocess.run([sys.executable, "-c", prog, td], env=env,
                               capture_output=True, text=True, timeout=600)
            if "TBLOK" in r.stdout:
                cos = np.load(os.path.join(td, "cos.npy")).astype(np.float32)
                sin = np.load(os.path.join(td, "sin.npy")).astype(np.float32)
    except Exception:
        cos = None
    if cos is None:
        cos = np.cos(emb.astype(np.float64)).astype(np.float32)
        sin = np.sin(emb.astype(np.float64)).astype(np.float32)
    _TABLE_CACHE["t"] = (cos, sin)
    return cos, sin


class Prog:
    def __init__(self, nc, ctx):
        self.nc = nc
        self.ctx = ctx
        self.ops = {e: [] for e in ENGINES}
        self.sem_handles = {}
        self.counts = {}
        self.writers = {}
        self.readers = {}
        self.waited = {e: {} for e in ENGINES}

    def sem(self, name):
        if name not in self.sem_handles:
            self.sem_handles[name] = self.ctx.enter_context(self.nc.semaphore(name))
            self.counts[name] = 0
        return self.sem_handles[name]

    def _wait(self, eng, sname, val):
        if val <= 0 or self.waited[eng].get(sname, 0) >= val:
            return
        self.waited[eng][sname] = val
        h = self.sem_handles[sname]
        self.ops[eng].append(lambda E, h=h, v=val: E.wait_ge(h, v))

    def emit(self, eng, fn, reads=(), writes=(), sem=None, inc=1):
        for k in reads:
            for s, c in self.writers.get(k, {}).items():
                self._wait(eng, s, c)
        for k in writes:
            for s, c in self.writers.get(k, {}).items():
                self._wait(eng, s, c)
            for s, c in self.readers.get(k, {}).items():
                self._wait(eng, s, c)
        sname = sem or ESEM[eng]
        self.sem(sname)
        self.counts[sname] += inc
        cnt = self.counts[sname]
        h = self.sem_handles[sname]

        def run(E, fn=fn, h=h, inc=inc):
            fn(E).then_inc(h, inc)
        self.ops[eng].append(run)
        for k in writes:
            self.writers.setdefault(k, {})[sname] = cnt
            self.readers.pop(k, None)
        for k in reads:
            self.readers.setdefault(k, {})[sname] = cnt
        return sname, cnt

    def final_wait(self, eng, sname, val):
        h = self.sem(sname)
        self.ops[eng].append(lambda E, h=h, v=val: E.wait_ge(h, v))

    def realize(self):
        with self.nc.Block() as block:
            @block.sync
            def _(sync):
                for f in self.ops["sync"]:
                    f(sync)

            @block.vector
            def _(vector):
                for f in self.ops["vector"]:
                    f(vector)

            @block.scalar
            def _(scalar):
                for f in self.ops["scalar"]:
                    f(scalar)

            @block.tensor
            def _(tensor):
                for f in self.ops["tensor"]:
                    f(tensor)

            @block.gpsimd
            def _(gpsimd):
                for f in self.ops["gpsimd"]:
                    f(gpsimd)


def build():
    import contextlib
    nc = bass.Bass("TRN2", target_bir_lowering=False, debug=False)
    ctx = contextlib.ExitStack()
    p = Prog(nc, ctx)
    E = p.emit

    xT = nc.dram_tensor("xT", [D, T], F32, kind="ExternalInput").ap()
    wqkv_b = nc.dram_tensor("wqkv_b", [32 * 128, NDC * 128], F32, kind="ExternalInput").ap()
    wv_b = nc.dram_tensor("wv_b", [8 * 128, NDC * 256], F32, kind="ExternalInput").ap()
    wout_b = nc.dram_tensor("wout_b", [NDC * 128, NDC * 128], F32, kind="ExternalInput").ap()
    gw_b = nc.dram_tensor("gw_b", [NHC * 128, NDC * 256], F32, kind="ExternalInput").ap()
    dwT = nc.dram_tensor("dwT", [INTER, D], F32, kind="ExternalInput").ap()
    g1t = nc.dram_tensor("g1t", [128, NDC], F32, kind="ExternalInput").ap()
    g2t = nc.dram_tensor("g2t", [128, NHC], F32, kind="ExternalInput").ap()
    onesb16 = nc.dram_tensor("onesb16", [128, 128], BF16, kind="ExternalInput").ap()
    identf = nc.dram_tensor("identf", [128, 128], F32, kind="ExternalInput").ap()
    cosTin = nc.dram_tensor("cosTin", [128, T], F32, kind="ExternalInput").ap()
    sinTin = nc.dram_tensor("sinTin", [128, T], F32, kind="ExternalInput").ap()
    cvec = nc.dram_tensor("cvec", [128, 2], F32, kind="ExternalInput").ap()

    outT = nc.dram_tensor("outT", [D, TO], F32, kind="ExternalOutput").ap()
    h_scr = nc.dram_tensor("h_scr", [INTER, TO], F32, kind="Internal").ap()
    xqh_scr = nc.dram_tensor("xqh_scr", [INTER, TO], BF16, kind="Internal").ap()
    sc_scr = nc.dram_tensor("sc_scr", [1, 4], F32, kind="Internal").ap()

    akey = {}

    def place(name, shape, dt, off, keys):
        t = nc.alloc_sbuf_tensor_at(name, shape, dt, offset=off)
        akey[name] = (keys,) if isinstance(keys, str) else tuple(keys)
        return t

    Z0 = 16544
    Z1 = Z0 + 81920
    Z2 = Z1 + 32768
    Z3 = Z2 + 32768
    Z4 = Z3 + 32768
    Z5 = Z4 + 24576
    XN16 = place("XN16", [128, NDC * T], BF16, Z0, "z0a")
    cosT = place("cosT", [128, T], F32, Z0 + 65536, "z0b")
    sinT = place("sinT", [128, T], F32, Z0 + 73728, "z0c")
    GWF = [place("GWF0", [128, NDC * 256], F32, Z1, ("z1a", "z1b")),
           place("GWF1", [128, NDC * 256], F32, Z1 + 16384,
                 ("z1c", "z1d", "z1e", "z1f"))]
    WQ16 = [place("WQ16_0", [128, NDC * 256], BF16, Z4, ("z4a", "z4b")),
            place("WQ16_1", [128, NDC * 256], BF16, Z4 + 16384, ("z4e", "z4f"))]
    U1 = place("U1", [128, 2048], F32, Z0 + 65536, "z0b")
    U2 = place("U2", [128, 2048], F32, Z0 + 73728, "z0c")
    RSTD = place("RSTD", [128, T], F32, Z1, "z1a")
    MUSC = place("MUSC", [128, T], F32, Z1 + 8192, "z1b")
    RRMS = place("RRMS", [128, TO], F32, Z1 + 16384, "z1c")
    CG = place("CG", [128, TO], F32, Z1 + 20480, "z1d")
    MM1 = place("MM1", [128, TO], F32, Z1 + 24576, "z1e")
    K16 = place("K16", [128, T], BF16, Z1, "z1a")
    VT16 = place("VT16", [128, 2 * T], BF16, Z1 + 4096, ("z1a", "z1b"))
    Q16 = place("Q16", [128, TO], BF16, Z1 + 12288, "z1b")
    SQH = place("SQH", [128, T], BF16, Z1 + 14336, ("z1b", "z1c"))
    RECB = place("RECB", [128, TO], F32, Z1 + 18432, ("z1c", "z1d"))
    LNS = place("LNS", [128, TO], F32, Z1 + 22528, ("z1d", "z1e"))
    PB = [place("PB0", [128, TO], BF16, Z1 + 26624, "z1e"),
          place("PB1", [128, TO], BF16, Z1 + 28672, "z1f")]
    S16 = place("S16", [128, NDC * TO], BF16, Z1,
                ("z1a", "z1b", "z1c", "z1d", "z1e", "z1f"))
    XB = place("XB", [128, T], F32, Z2, "z2a")
    XB1 = place("XB1", [128, T], F32, Z3, ("z3a",))
    SQB = place("SQB", [128, T], F32, Z2 + 8192, "z2b")
    TMP1 = place("TMP1", [128, T], F32, Z2 + 16384, "z2c")
    TMP2 = place("TMP2", [128, T], F32, Z2 + 24576, "z2d")
    QT = place("QT", [128, TO], F32, Z2 + 8192, "z2b")
    QI = place("QI", [128, TO], I32, Z2 + 12288, "z2b")
    QC = place("QC", [128, TO], F32, Z2 + 16384, "z2c")
    QN = place("QN", [128, TO], F32, Z2 + 20480, "z2c")
    QF = place("QF", [128, TO], F32, Z2 + 24576, "z2d")
    QD = place("QD", [128, TO], F32, Z2 + 28672, "z2d")
    WQK = [place("WQK0", [128, NDC * 128], BF16, Z2, "z2a"),
           place("WQK1", [128, NDC * 128], BF16, Z2 + 4096, "z2a")]
    LQD = place("LQD", [128, TO], F32, Z2 + 8192, "z2b")
    LQC = place("LQC", [128, TO], F32, Z2 + 12288, "z2b")
    WVG = [place("WVG0", [128, NDC * 256], BF16, Z2 + 16384, "z2c"),
           place("WVG1", [128, NDC * 256], BF16, Z2 + 24576, "z2d")]
    WOB = [place("WOB0", [128, NDC * 128], BF16, Z2, "z2a"),
           place("WOB1", [128, NDC * 128], BF16, Z2 + 4096, "z2a")]
    EV1 = place("EV1", [128, TO], F32, Z2, "z2a")
    EV2 = place("EV2", [128, TO], F32, Z2 + 4096, "z2a")
    EV3 = [place("EV3a", [128, TO], F32, Z2 + 8192, "z2b"),
           place("EV3b", [128, TO], F32, Z2 + 12288, "z2b")]
    HB = place("HB", [128, TO], F32, Z4 + 4096, "z4b")
    LYH1 = place("LYH1", [128, TO], F32, Z4 + 8192, "z4c")
    LQF = place("LQF", [128, TO], F32, Z4 + 12288, "z4d")
    YH1 = place("YH1", [128, TO], F32, Z2 + 8192, "z2b")
    YH2 = place("YH2", [128, TO], F32, Z2 + 12288, "z2b")
    HS1 = place("HS1", [128, TO], F32, Z2 + 16384, "z2c")
    HS2 = place("HS2", [128, TO], F32, Z2 + 20480, "z2c")
    YH2p = place("YH2p", [128, TO], F32, Z2 + 16384, "z2c")
    HSQA = place("HSQA", [128, TO], F32, Z2 + 20480, "z2c")
    SQF = place("SQF", [128, TO], F32, Z2 + 24576, "z2d")
    DWF = [place("DWF0", [128, 512], F32, Z3 + 4096, "z3a"),
           place("DWF1", [128, 512], F32, Z3 + 6144, "z3a")]
    U1D = place("U1D", [128, 512], F32, Z3 + 8192, "z3a")
    U2D = place("U2D", [128, 512], F32, Z3 + 10240, "z3a")
    WQDh = [place("WQDh0", [128, 512], BF16, Z3 + 12288, "z3a"),
            place("WQDh1", [128, 512], BF16, Z3 + 13312, "z3a")]
    XQHR = [place("XQHR0", [128, TO], BF16, Z3, "z3a"),
            place("XQHR1", [128, TO], BF16, Z3 + 2048, "z3a")]
    OBA = place("OBA", [128, TO], F32, Z2 + 22528, ("z2c", "z2d"))
    OBB = place("OBB", [128, TO], F32, Z2 + 26624, "z2d")
    XR = place("XR", [128, TO], F32, Z3 + 16384, "z3b")
    XQ = place("XQ", [128, NDC * TO], BF16, Z3, ("z3a", "z3b"))
    O16 = place("O16", [128, H * TO], BF16, Z3, ("z3a", "z3b"))
    SQ16 = place("SQ16", [128, T], BF16, Z4, "z4a")
    XB16 = place("XB16", [128, T], BF16, Z4 + 4096, "z4b")
    WRB = [place("WRB0", [128, 2048], BF16, Z4 + 8192, "z4c"),
           place("WRB1", [128, 2048], BF16, Z4 + 12288, "z4d")]
    RTa = place("RTa", [64, TO], F32, Z4 + 16384, "z4e")
    RTb = place("RTb", [64, TO], F32, Z1 + 22528, ("z1d", "z1e"))
    RTc = place("RTc", [64, TO], F32, Z1 + 18432, ("z1c", "z1d"))
    RTd = place("RTd", [64, TO], F32, Z1 + 14336, ("z1b", "z1c"))
    SCG = place("SCG", [128, TO], F32, Z4 + 8192, "z4c")
    MMH = place("MMH", [128, TO], F32, Z4 + 12288, "z4d")
    CCH = place("CCH", [128, TO], F32, Z4, "z4a")
    SCD = place("SCD", [128, TO], F32, Z4 + 20480, "z4f")
    z5 = [Z5]

    def small(name, shape, dt=F32):
        t = place(name, shape, dt, z5[0], name)
        sz = shape[1] * {F32: 4, BF16: 2, I32: 4}[dt]
        z5[0] += (sz + 31) // 32 * 32
        return t

    g1s = small("g1s", [128, NDC])
    g2s = small("g2s", [128, NHC])
    ones16 = small("ones16", [128, 128], BF16)
    idf = small("idf", [128, 128])
    epsv = small("epsv", [128, 2])
    SCAL = small("SCAL", [128, 4])
    SCLT = small("SCLT", [1, 4])
    MCOMP = small("MCOMP", [128, 8])
    MQ = small("MQ", [128, 1])
    MK = small("MK", [128, 1])
    NEGM = small("NEGM", [128, 1])
    WACC = small("WACC", [128, 2])
    WS1 = small("WS1", [128, 1])
    WS2 = small("WS2", [128, 1])
    WS1D = small("WS1D", [128, 1])
    WS2D = small("WS2D", [128, 1])
    BCH = small("BCH", [128, 128])
    HLF = small("HLF", [128, 2])
    ONESF = small("ONESF", [128, 128])
    LADB = small("LADB", [128, 14])
    DEN_G = small("DEN_G", [128, TO])
    XQHT = [place("XQHT0", [128, TO], BF16, z5[0] - 4096, "DEN_G"),
            place("XQHT1", [128, TO], BF16, z5[0] - 2048, "DEN_G")]
    assert z5[0] <= 229376, z5[0]

    PS = [nc.alloc_psum_tensor(f"PS{i}", [128, TO], F32) for i in range(4)]

    def K(*names):
        out = []
        for n in names:
            out.extend(akey.get(n, (n,)))
        return out

    taps = []

    def tapk(name, tile, keys, rows=128, cols=None):
        if name not in DEBUG_TAPS:
            return
        c = cols if cols is not None else tile.shape[1]
        dt = tile.dtype
        o = nc.dram_tensor("tap_" + name, [rows, c], dt, kind="ExternalOutput").ap()
        taps.append(name)
        E("gpsimd", lambda G, o=o, t=tile, r=rows, c=c: G.dma_start(out=o, in_=t[0:r, 0:c]),
          reads=K(*keys), sem=f"tp{len(taps)}", inc=16)

    for ci, (dst, srcp, nm) in enumerate([(g1s, g1t, "g1s"), (g2s, g2t, "g2s"),
                                          (ones16, onesb16, "ones16"), (idf, identf, "idf"),
                                          (epsv, cvec, "epsv")]):
        E("sync", lambda S, d=dst, s=srcp: S.dma_start(out=d[:, :], in_=s),
          writes=K(nm), sem=f"cs{ci}", inc=16)

    for nm in ("WS1", "WS2", "WS1D", "WS2D"):
        t = {"WS1": WS1, "WS2": WS2, "WS1D": WS1D, "WS2D": WS2D}[nm]
        E("vector", lambda V, t=t: V.memset(t[:, :], 0.0), writes=K(nm))
    gw_flat = gw_b.rearrange("a (b c) -> (a b) c", b=2)
    wred_list = [(gw_flat[rg * 128:(rg + 1) * 128, :], 2048) for rg in range(NHC * 2)]
    wred_list += [(dwT[rg * 128:(rg + 1) * 128, :], 2048) for rg in range(NHC)]
    NGW = NHC * 2

    for idxw, (apw, w) in enumerate(wred_list):
        b = idxw % 2
        E("gpsimd", lambda G, b=b, apw=apw, w=w: G.dma_start(out=WRB[b][:, 0:w], in_=apw),
          writes=K(f"WRB{b}"), sem=f"wr{b}", inc=16)
        awt = SQ16 if b == 0 else XB16
        awk = "SQ16" if b == 0 else "XB16"
        E("scalar", lambda S, b=b, w=w, awt=awt: S.activation(out=awt[:, 0:w],
                                                              in_=WRB[b][:, 0:w],
                                                              func=AF.Abs),
          reads=K(f"WRB{b}"), writes=K(awk))
        isgw = idxw < NGW
        ps_lo = 0 if isgw else 2
        st = (idxw == 0) if isgw else (idxw == NGW)
        sp = (idxw == NGW - 1) if isgw else (idxw == len(wred_list) - 1)

        def mm_wred(TE, w=w, ps_lo=ps_lo, st=st, sp=sp, awt=awt):
            nj = (w + 511) // 512
            for j in range(nj):
                wj = min(512, w - j * 512)
                TE.ldweights(ones16[:, :])
                i = TE.matmul(out=PS[ps_lo + j // 2][:, (j % 2) * 512:(j % 2) * 512 + wj],
                              lhsT=ones16[:, :], rhs=awt[:, j * 512:j * 512 + wj],
                              start=st, stop=sp)
            return i
        E("tensor", mm_wred, reads=K(awk, "ones16"),
          writes=[f"PS{ps_lo}", f"PS{ps_lo + 1}"])
    E("vector", lambda V: V.reduce_sum(out=WS1[:, :], in_=PS[0][:, :], axis=AX.X),
      reads=["PS0"], writes=K("WS1"))
    E("vector", lambda V: V.reduce_sum(out=WS2[:, :], in_=PS[1][:, :], axis=AX.X),
      reads=["PS1"], writes=K("WS2"))
    E("vector", lambda V: V.reduce_sum(out=WS1D[:, :], in_=PS[2][:, :], axis=AX.X),
      reads=["PS2"], writes=K("WS1D"))
    E("vector", lambda V: V.reduce_sum(out=WS2D[:, :], in_=PS[3][:, :], axis=AX.X),
      reads=["PS3"], writes=K("WS2D"))
    E("vector", lambda V: V.tensor_tensor(out=WACC[:, 0:1], in0=WS1[:, :], in1=WS2[:, :],
                                          op=OP.add),
      reads=K("WS1", "WS2"), writes=K("WACC"))
    E("vector", lambda V: V.tensor_tensor(out=WACC[:, 1:2], in0=WS1D[:, :], in1=WS2D[:, :],
                                          op=OP.add),
      reads=K("WS1D", "WS2D"), writes=K("WACC"))
    E("vector", lambda V: V.tensor_scalar(out=SCAL[:, 0:1], in0=WACC[:, 0:1],
                                          scalar1=1.0 / (D * 2 * INTER), scalar2=EPS,
                                          op0=OP.mult, op1=OP.max),
      reads=K("WACC"), writes=K("SCAL"))
    E("vector", lambda V: V.tensor_scalar(out=SCAL[:, 1:2], in0=WACC[:, 1:2],
                                          scalar1=1.0 / (INTER * D), scalar2=EPS,
                                          op0=OP.mult, op1=OP.max),
      reads=K("WACC"), writes=K("SCAL"))
    E("vector", lambda V: V.reciprocal(out=SCAL[:, 2:3], in_=SCAL[:, 0:1]),
      reads=K("SCAL"), writes=K("SCAL"))
    E("vector", lambda V: V.reciprocal(out=SCAL[:, 3:4], in_=SCAL[:, 1:2]),
      reads=K("SCAL"), writes=K("SCAL"))

    E("vector", lambda V: V.memset(HLF[:, 0:1], 0.5), writes=K("HLF"))
    E("vector", lambda V: V.memset(ONESF[:, :], 1.0), writes=K("ONESF"))
    E("vector", lambda V: V.memset(HLF[:, 1:2], -0.5), writes=K("HLF"))
    for i, m in enumerate((64.0, 32.0, 16.0, 8.0, 4.0, 2.0, 1.0)):
        E("vector", lambda V, i=i, m=m: V.memset(LADB[:, 2 * i:2 * i + 1], m - 0.5),
          writes=K("LADB"))
        E("vector", lambda V, i=i, m=m: V.memset(LADB[:, 2 * i + 1:2 * i + 2], -(m - 0.5)),
          writes=K("LADB"))

    def xchunk(dc):
        return xT[dc * 128:(dc + 1) * 128, :]

    for dc in range(NDC):
        xb = XB if dc % 2 == 0 else XB1
        xbn = "XB" if dc % 2 == 0 else "XB1"
        E("sync", lambda S, dc=dc, xb=xb: S.dma_start(out=xb[:, :], in_=xchunk(dc)),
          writes=K(xbn), sem=f"xb{dc % 2}", inc=16)
        E("scalar", lambda S, xb=xb: S.activation(out=SQB[:, :], in_=xb[:, :], func=AF.Square),
          reads=K(xbn), writes=K("SQB"))
        E("vector", lambda V, xb=xb: V.tensor_copy(out=XB16[:, :], in_=xb[:, :]),
          reads=K(xbn), writes=K("XB16"))
        E("vector", lambda V: V.tensor_copy(out=SQ16[:, :], in_=SQB[:, :]),
          reads=K("SQB"), writes=K("SQ16"))
        st, sp = dc == 0, dc == NDC - 1

        def mm_stats(TE, st=st, sp=sp):
            for half in range(2):
                for j in range(2):
                    TE.ldweights(ones16[:, :])
                    TE.matmul(out=PS[half][:, 512 * j:512 * (j + 1)], lhsT=ones16[:, :],
                              rhs=XB16[:, half * TO + 512 * j: half * TO + 512 * (j + 1)],
                              start=st, stop=sp)
                    TE.ldweights(ones16[:, :])
                    i = TE.matmul(out=PS[2 + half][:, 512 * j:512 * (j + 1)], lhsT=ones16[:, :],
                                  rhs=SQ16[:, half * TO + 512 * j: half * TO + 512 * (j + 1)],
                                  start=st, stop=sp)
            return i
        E("tensor", mm_stats, reads=K("XB16", "SQ16", "ones16"),
          writes=["PS0", "PS1", "PS2", "PS3"])

    MUt, VARt = XB, SQB
    for half in range(2):
        E("vector", lambda V, h=half: V.tensor_scalar(out=MUt[:, h * TO:(h + 1) * TO],
                                                      in0=PS[h][:, :], scalar1=1.0 / D,
                                                      scalar2=None, op0=OP.mult),
          reads=[f"PS{half}"], writes=K("XB"))
        E("vector", lambda V, h=half: V.tensor_scalar(out=VARt[:, h * TO:(h + 1) * TO],
                                                      in0=PS[2 + h][:, :], scalar1=1.0 / D,
                                                      scalar2=None, op0=OP.mult),
          reads=[f"PS{2 + half}"], writes=K("SQB"))
    E("vector", lambda V: V.tensor_tensor(out=TMP1[:, :], in0=MUt[:, :], in1=MUt[:, :], op=OP.mult),
      reads=K("XB"), writes=K("TMP1"))
    E("vector", lambda V: V.tensor_tensor(out=VARt[:, :], in0=VARt[:, :], in1=TMP1[:, :],
                                          op=OP.subtract),
      reads=K("TMP1", "SQB"), writes=K("SQB"))
    E("scalar", lambda S: S.activation(out=TMP1[:, :], in_=VARt[:, :], func=AF.Ln,
                                       bias=epsv[:, 0:1]),
      reads=K("SQB", "epsv"), writes=K("TMP1"))
    E("scalar", lambda S: S.activation(out=RSTD[:, :], in_=TMP1[:, :], func=AF.Exp, scale=-0.5),
      reads=K("TMP1"), writes=K("RSTD"))
    E("vector", lambda V: V.tensor_tensor(out=MUSC[:, :], in0=MUt[:, :], in1=RSTD[:, :],
                                          op=OP.mult),
      reads=K("XB", "RSTD"), writes=K("MUSC"))
    E("vector", lambda V: V.tensor_scalar(out=TMP1[:, 0:TO], in0=VARt[:, 0:TO], scalar1=EPS,
                                          scalar2=None, op0=OP.add),
      reads=K("SQB"), writes=K("TMP1"))
    E("vector", lambda V: V.reciprocal(out=MM1[:, :], in_=TMP1[:, 0:TO]),
      reads=K("TMP1"), writes=K("MM1"))
    E("vector", lambda V: V.tensor_tensor(out=TMP2[:, 0:TO], in0=VARt[:, 0:TO], in1=MM1[:, :],
                                          op=OP.mult),
      reads=K("SQB", "MM1"), writes=K("TMP2"))
    E("scalar", lambda S: S.activation(out=TMP1[:, 0:TO], in_=TMP2[:, 0:TO], func=AF.Ln,
                                       bias=epsv[:, 0:1]),
      reads=K("TMP2", "epsv"), writes=K("TMP1"))
    E("scalar", lambda S: S.activation(out=RRMS[:, :], in_=TMP1[:, 0:TO], func=AF.Exp, scale=-0.5),
      reads=K("TMP1"), writes=K("RRMS"))
    tapk("rstd", RSTD, ["RSTD"])
    tapk("rrms", RRMS, ["RRMS"])

    for dc in range(NDC):
        xb = XB if dc % 2 == 0 else XB1
        xbn = "XB" if dc % 2 == 0 else "XB1"
        E("sync", lambda S, dc=dc, xb=xb: S.dma_start(out=xb[:, :], in_=xchunk(dc)),
          writes=K(xbn), sem=f"xb{dc % 2}", inc=16)
        E("vector", lambda V, xb=xb: V.tensor_tensor(out=TMP1[:, :], in0=xb[:, :], in1=RSTD[:, :],
                                              op=OP.mult),
          reads=K(xbn, "RSTD"), writes=K("TMP1"))
        E("vector", lambda V: V.tensor_tensor(out=TMP2[:, :], in0=TMP1[:, :], in1=MUSC[:, :],
                                              op=OP.subtract),
          reads=K("TMP1", "MUSC"), writes=K("TMP2"))
        E("vector", lambda V, dc=dc: V.tensor_copy(out=XN16[:, dc * T:(dc + 1) * T],
                                                   in_=TMP2[:, :]),
          reads=K("TMP2"), writes=K("XN16"))
        E("vector", lambda V, dc=dc: V.tensor_scalar(out=QT[:, :], in0=TMP2[:, 0:TO],
                                                     scalar1=g1s[:, dc:dc + 1], scalar2=None,
                                                     op0=OP.mult),
          reads=K("TMP2", "g1s"), writes=K("QT"))
        E("scalar", lambda S: S.activation(out=QF[:, :], in_=QT[:, :], func=AF.Abs),
          reads=K("QT"), writes=K("QF"))
        if dc == 0:
            E("vector", lambda V: V.tensor_copy(out=MM1[:, :], in_=QF[:, :]),
              reads=K("QF"), writes=K("MM1"))
        else:
            E("vector", lambda V: V.tensor_tensor(out=MM1[:, :], in0=MM1[:, :], in1=QF[:, :],
                                                  op=OP.max),
              reads=K("QF", "MM1"), writes=K("MM1"))
    tapk("xn16", XN16, ["XN16"])

    def compact_max(src, srckeys, dst, dstnm):
        for blk in range(8):
            E("tensor", lambda TE, blk=blk: TE.transpose(
                out=PS[0][:, 0:128], in_=src[:, blk * 128:(blk + 1) * 128], identity=idf[:, :]),
              reads=K(*srckeys) + K("idf"), writes=["PS0"])
            E("vector", lambda V, blk=blk: V.reduce_max(out=dst[:, blk:blk + 1],
                                                        in_=PS[0][:, 0:128], axis=AX.X),
              reads=["PS0"], writes=K(dstnm))

    def bcast_cols(srccomp, srcnm, dst, dstkeys, colbase, nblk=8):
        for blk in range(nblk):
            E("vector", lambda V, blk=blk: V.tensor_copy(
                out=BCH[:, :], in_=srccomp[:, blk:blk + 1].broadcast_to([128, 128])),
              reads=K(srcnm), writes=K("BCH"))
            E("tensor", lambda TE: TE.transpose(out=PS[0][:, 0:128], in_=BCH[:, :],
                                                identity=idf[:, :]),
              reads=K("BCH", "idf"), writes=["PS0"])
            E("vector", lambda V, blk=blk: V.tensor_copy(
                out=dst[:, colbase + blk * 128: colbase + (blk + 1) * 128], in_=PS[0][:, 0:128]),
              reads=["PS0"], writes=K(*dstkeys))

    compact_max(MM1, ["MM1"], MCOMP, "MCOMP")
    bcast_cols(MCOMP, "MCOMP", TMP1, ["TMP1"], 0)
    E("vector", lambda V: V.tensor_tensor(out=QT[:, :], in0=RRMS[:, :], in1=TMP1[:, 0:TO],
                                          op=OP.mult),
      reads=K("RRMS", "TMP1"), writes=K("QT"))
    E("vector", lambda V: V.tensor_scalar(out=DEN_G[:, :], in0=QT[:, :], scalar1=EPS,
                                          scalar2=None, op0=OP.max),
      reads=K("QT"), writes=K("DEN_G"))
    E("vector", lambda V: V.reciprocal(out=QF[:, :], in_=DEN_G[:, :]),
      reads=K("DEN_G"), writes=K("QF"))
    E("vector", lambda V: V.tensor_tensor(out=QT[:, :], in0=RRMS[:, :], in1=QF[:, :], op=OP.mult),
      reads=K("RRMS", "QF"), writes=K("QT"))
    E("vector", lambda V: V.tensor_scalar(out=CG[:, :], in0=QT[:, :], scalar1=127.0,
                                          scalar2=None, op0=OP.mult),
      reads=K("QT"), writes=K("CG"))
    tapk("cg", CG, ["CG"])
    tapk("deng", DEN_G, ["DEN_G"])

    def cast_correct(src_t, srckeys, dst, dstkeys, dst_c0):
        E("vector", lambda V: V.tensor_copy(out=QI[:, :], in_=src_t[:, 0:TO]),
          reads=K(*srckeys), writes=K("QI"))
        E("vector", lambda V: V.tensor_scalar(out=QF[:, :], in0=QI[:, :], scalar1=-128.0,
                                              scalar2=127.0, op0=OP.max, op1=OP.min),
          reads=K("QI"), writes=K("QF"))
        E("gpsimd", lambda G: G.tensor_tensor(out=QD[:, :], in0=src_t[:, 0:TO], in1=QF[:, :],
                                              op=OP.subtract),
          reads=K(*srckeys) + K("QF"), writes=K("QD"))
        E("vector", lambda V: V.tensor_scalar(out=QC[:, :], in0=QD[:, :], scalar1=0.5,
                                              scalar2=None, op0=OP.is_ge),
          reads=K("QD"), writes=K("QC"))
        E("vector", lambda V: V.scalar_tensor_tensor(out=QN[:, :], in0=QD[:, :], scalar=-0.5,
                                                     in1=QC[:, :], op0=OP.is_le, op1=OP.subtract),
          reads=K("QD", "QC"), writes=K("QN"))
        E("gpsimd", lambda G: G.tensor_tensor(out=dst[:, dst_c0:dst_c0 + TO], in0=QF[:, :],
                                              in1=QN[:, :], op=OP.subtract),
          reads=K("QF", "QN"), writes=K(*dstkeys))

    def ladder_round(src_t, srckeys, dst, dstkeys, dst_c0, eng2="vector"):
        E("vector", lambda V: V.tensor_copy(out=QD[:, :], in_=src_t[:, 0:TO]),
          reads=K(*srckeys), writes=K("QD"))
        for i, m in enumerate((64.0, 32.0, 16.0, 8.0, 4.0, 2.0, 1.0)):
            E(eng2, lambda X, m=m: X.tensor_scalar(out=QC[:, :], in0=QD[:, :], scalar1=m - 0.5,
                                                   scalar2=None, op0=OP.is_ge),
              reads=K("QD"), writes=K("QC"))
            E("vector", lambda X, m=m: X.scalar_tensor_tensor(out=QC[:, :], in0=QD[:, :],
                                                              scalar=-(m - 0.5), in1=QC[:, :],
                                                              op0=OP.is_le, op1=OP.subtract),
              reads=K("QD", "QC"), writes=K("QC"))
            E("vector", lambda X, m=m: X.scalar_tensor_tensor(out=QD[:, :], in0=QC[:, :], scalar=m,
                                                              in1=QD[:, :], op0=OP.mult, op1=OP.add),
              reads=K("QC", "QD"), writes=K("QD"))
        E("vector", lambda V: V.tensor_tensor(out=QF[:, :], in0=src_t[:, 0:TO], in1=QD[:, :],
                                              op=OP.subtract),
          reads=K(*srckeys) + K("QD"), writes=K("QF"))
        E("vector", lambda V: V.tensor_scalar(out=dst[:, dst_c0:dst_c0 + TO], in0=QF[:, :],
                                              scalar1=-128.0, scalar2=127.0, op0=OP.max,
                                              op1=OP.min),
          reads=K("QF"), writes=K(*dstkeys))

    for dc in range(NDC):
        E("sync", lambda S, dc=dc: S.dma_start(out=XB[:, 0:TO], in_=xchunk(dc)[:, 0:TO]),
          writes=K("XB"), sem="xb", inc=16)
        E("vector", lambda V: V.tensor_tensor(out=TMP1[:, 0:TO], in0=XB[:, 0:TO],
                                              in1=RSTD[:, 0:TO], op=OP.mult),
          reads=K("XB", "RSTD"), writes=K("TMP1"))
        E("vector", lambda V: V.tensor_tensor(out=TMP2[:, 0:TO], in0=TMP1[:, 0:TO],
                                              in1=MUSC[:, 0:TO], op=OP.subtract),
          reads=K("TMP1", "MUSC"), writes=K("TMP2"))
        E("vector", lambda V, dc=dc: V.tensor_scalar(out=QT[:, :], in0=TMP2[:, 0:TO],
                                                     scalar1=g1s[:, dc:dc + 1], scalar2=None,
                                                     op0=OP.mult),
          reads=K("TMP2", "g1s"), writes=K("QT"))
        E("vector", lambda V: V.tensor_tensor(out=QT[:, :], in0=QT[:, :], in1=CG[:, :],
                                              op=OP.mult),
          reads=K("CG", "QT"), writes=K("QT"))
        cast_correct(QT, ["QT"], XQ, ["XQ"], dc * TO)
    tapk("xq", XQ, ["XQ"])

    E("vector", lambda V: V.tensor_scalar(out=SCG[:, :], in0=DEN_G[:, :], scalar1=SCAL[:, 0:1],
                                          scalar2=0.5 / 127.0, op0=OP.mult, op1=OP.mult),
      reads=K("DEN_G", "SCAL"), writes=K("SCG"))

    def hw_tile(jp):
        return h_scr[jp * 128:(jp + 1) * 128, :]

    for jp in range(NHC):
        gb = jp % 2
        psg, psgn = PS[2 * gb], f"PS{2 * gb}"
        psu, psun = PS[2 * gb + 1], f"PS{2 * gb + 1}"
        ev = EV3[gb]
        evn = "EV3a" if gb == 0 else "EV3b"
        E("sync", lambda S, jp=jp, gb=gb: S.dma_start(
            out=GWF[gb][:, :], in_=gw_b[jp * 128:(jp + 1) * 128, :]),
          writes=K(f"GWF{gb}"), sem=f"gw{gb}", inc=16)
        for hf in range(2):
            E("scalar", lambda S, gb=gb, hf=hf: S.activation(
                out=U1[:, :], in_=GWF[gb][:, hf * 2048:(hf + 1) * 2048],
                func=AF.Sign, scale=SCAL[:, 2:3], bias=HLF[:, 1:2]),
              reads=K(f"GWF{gb}", "SCAL", "HLF"), writes=K("U1"))
            E("scalar", lambda S, gb=gb, hf=hf: S.activation(
                out=U2[:, :], in_=GWF[gb][:, hf * 2048:(hf + 1) * 2048],
                func=AF.Sign, scale=SCAL[:, 2:3], bias=HLF[:, 0:1]),
              reads=K(f"GWF{gb}", "SCAL", "HLF"), writes=K("U2"))
            E("vector", lambda V, gb=gb, hf=hf: V.tensor_tensor(
                out=WQ16[gb][:, hf * 2048:(hf + 1) * 2048], in0=U1[:, :], in1=U2[:, :],
                op=OP.add),
              reads=K("U1", "U2"), writes=K(f"WQ16_{gb}"))

        def mm_gate(TE, gb=gb, psg=psg, psu=psu):
            for dc in range(NDC):
                TE.ldweights(WQ16[gb][:, dc * 256: dc * 256 + 128])
                for j in range(2):
                    TE.matmul(out=psg[:, j * 512:(j + 1) * 512],
                              lhsT=WQ16[gb][:, dc * 256: dc * 256 + 128],
                              rhs=XQ[:, dc * TO + j * 512: dc * TO + (j + 1) * 512],
                              start=(dc == 0), stop=(dc == NDC - 1))
                TE.ldweights(WQ16[gb][:, dc * 256 + 128:(dc + 1) * 256])
                for j in range(2):
                    i = TE.matmul(out=psu[:, j * 512:(j + 1) * 512],
                                  lhsT=WQ16[gb][:, dc * 256 + 128:(dc + 1) * 256],
                                  rhs=XQ[:, dc * TO + j * 512: dc * TO + (j + 1) * 512],
                                  start=(dc == 0), stop=(dc == NDC - 1))
            return i
        E("tensor", mm_gate, reads=K(f"WQ16_{gb}", "XQ"), writes=[psgn, psun])
        E("vector", lambda V, psg=psg: V.tensor_tensor(out=EV1[:, :], in0=psg[:, 0:TO],
                                                       in1=SCG[:, :], op=OP.mult),
          reads=[psgn] + K("SCG"), writes=K("EV1"))
        E("scalar", lambda S: S.activation(out=EV2[:, :], in_=EV1[:, :], func=AF.Silu),
          reads=K("EV1"), writes=K("EV2"))
        E("vector", lambda V, psu=psu, ev=ev: V.tensor_tensor(out=ev[:, :], in0=EV2[:, :],
                                                              in1=psu[:, 0:TO], op=OP.mult),
          reads=[psun] + K("EV2"), writes=K(evn))
        E("gpsimd", lambda G, jp=jp, ev=ev: G.dma_start(out=hw_tile(jp), in_=ev[:, :]),
          reads=K(evn), writes=["h_scr"], sem=f"hw{jp % 2}", inc=16)
        E("scalar", lambda S, jp=jp, ev=ev: S.activation(out=YH2p[:, :], in_=ev[:, :],
                                                         func=AF.Abs,
                                                         scale=g2s[:, jp:jp + 1]),
          reads=K(evn, "g2s"), writes=K("YH2p"))
        if jp == 0:
            E("vector", lambda V: V.tensor_copy(out=MMH[:, :], in_=YH2p[:, :]),
              reads=K("YH2p"), writes=K("MMH"))
        else:
            E("vector", lambda V: V.tensor_tensor(out=MMH[:, :], in0=MMH[:, :],
                                                  in1=YH2p[:, :], op=OP.max),
              reads=K("YH2p", "MMH"), writes=K("MMH"))
        if jp == 5:
            tapk("yh2p5", YH2p, ["YH2p"])
            tapk("ev5", ev, [evn])
        E("scalar", lambda S, ev=ev: S.activation(out=SQF[:, :], in_=ev[:, :],
                                                  func=AF.Square),
          reads=K(evn), writes=K("SQF"))
        if jp == 0:
            E("vector", lambda V: V.tensor_copy(out=HSQA[:, :], in_=SQF[:, :]),
              reads=K("SQF"), writes=K("HSQA"))
        else:
            E("vector", lambda V: V.tensor_tensor(out=HSQA[:, :], in0=HSQA[:, :],
                                                  in1=SQF[:, :], op=OP.add),
              reads=K("SQF", "HSQA"), writes=K("HSQA"))

    def mm_hsum(TE):
        for j in range(2):
            i = TE.matmul(out=PS[3][:, j * 512:(j + 1) * 512], lhsT=ONESF[:, :],
                          rhs=HSQA[:, j * 512:(j + 1) * 512], start=True, stop=True)
        return i
    E("tensor", mm_hsum, reads=K("HSQA", "ONESF"), writes=["PS3"])
    E("vector", lambda V: V.tensor_scalar(out=HS1[:, :], in0=PS[3][:, 0:TO],
                                          scalar1=1.0 / INTER, scalar2=None, op0=OP.mult),
      reads=["PS3"], writes=K("HS1"))
    E("vector", lambda V: V.tensor_tensor(out=HS2[:, :], in0=SCG[:, :], in1=SCG[:, :],
                                          op=OP.mult),
      reads=K("SCG"), writes=K("HS2"))
    E("vector", lambda V: V.tensor_tensor(out=HS1[:, :], in0=HS1[:, :], in1=HS2[:, :],
                                          op=OP.mult),
      reads=K("HS2", "HS1"), writes=K("HS1"))
    E("scalar", lambda S: S.activation(out=HS2[:, :], in_=HS1[:, :], func=AF.Ln,
                                       bias=epsv[:, 0:1]),
      reads=K("HS1", "epsv"), writes=K("HS2"))
    E("scalar", lambda S: S.activation(out=HS1[:, :], in_=HS2[:, :], func=AF.Exp, scale=-0.5),
      reads=K("HS2"), writes=K("HS1"))
    compact_max(MMH, ["MMH"], MCOMP, "MCOMP")
    bcast_cols(MCOMP, "MCOMP", HS2, ["HS2"], 0)
    E("vector", lambda V: V.tensor_tensor(out=YH1[:, :], in0=SCG[:, :], in1=HS2[:, :],
                                          op=OP.mult),
      reads=K("SCG", "HS2"), writes=K("YH1"))
    E("vector", lambda V: V.tensor_tensor(out=YH1[:, :], in0=YH1[:, :], in1=HS1[:, :],
                                          op=OP.mult),
      reads=K("HS1", "YH1"), writes=K("YH1"))
    E("vector", lambda V: V.tensor_scalar(out=YH2[:, :], in0=YH1[:, :], scalar1=EPS,
                                          scalar2=None, op0=OP.max),
      reads=K("YH1"), writes=K("YH2"))
    E("vector", lambda V: V.reciprocal(out=YH1[:, :], in_=YH2[:, :]),
      reads=K("YH2"), writes=K("YH1"))
    E("vector", lambda V: V.tensor_tensor(out=CCH[:, :], in0=HS1[:, :], in1=YH1[:, :],
                                          op=OP.mult),
      reads=K("HS1", "YH1"), writes=K("CCH"))
    E("vector", lambda V: V.tensor_tensor(out=CCH[:, :], in0=CCH[:, :], in1=SCG[:, :],
                                          op=OP.mult),
      reads=K("SCG", "CCH"), writes=K("CCH"))
    E("vector", lambda V: V.tensor_scalar(out=CCH[:, :], in0=CCH[:, :], scalar1=127.0,
                                          scalar2=None, op0=OP.mult),
      reads=K("CCH"), writes=K("CCH"))
    E("vector", lambda V: V.tensor_scalar(out=SCD[:, :], in0=YH2[:, :], scalar1=SCAL[:, 1:2],
                                          scalar2=0.5 / 127.0, op0=OP.mult, op1=OP.mult),
      reads=K("YH2", "SCAL"), writes=K("SCD"))
    tapk("cch", CCH, ["CCH"])
    tapk("scd", SCD, ["SCD"])


    pb_state = [0]

    def passB_chunk(jp):
        b = jp % 2
        xt = XQHT[b]
        xtn = f"XQHT{b}"
        E("sync", lambda S, jp=jp: S.dma_start(out=HB[:, :], in_=hw_tile(jp)),
          reads=["h_scr"], writes=K("HB"), sem="hr0", inc=16)
        E("vector", lambda V, jp=jp: V.scalar_tensor_tensor(
            out=LYH1[:, :], in0=HB[:, :], scalar=g2s[:, jp:jp + 1], in1=CCH[:, :],
            op0=OP.mult, op1=OP.mult),
          reads=K("HB", "g2s", "CCH"), writes=K("LYH1"))
        E("vector", lambda V: V.tensor_copy(out=LQD[:, :], in_=LYH1[:, :]),
          reads=K("LYH1"), writes=K("LQD"))
        for i in range(7):
            m = (64.0, 32.0, 16.0, 8.0, 4.0, 2.0, 1.0)[i]
            E("scalar", lambda S, i=i: S.activation(out=LQC[:, :], in_=LQD[:, :],
                                                    func=AF.Sign,
                                                    bias=LADB[:, 2 * i + 1:2 * i + 2]),
              reads=K("LQD", "LADB"), writes=K("LQC"))
            E("scalar", lambda S, i=i: S.activation(out=LQF[:, :], in_=LQD[:, :],
                                                    func=AF.Sign,
                                                    bias=LADB[:, 2 * i:2 * i + 1]),
              reads=K("LQD", "LADB"), writes=K("LQF"))
            E("vector", lambda V: V.tensor_tensor(out=LQC[:, :], in0=LQC[:, :],
                                                  in1=LQF[:, :], op=OP.add),
              reads=K("LQC", "LQF"), writes=K("LQC"))
            E("vector", lambda V, m=m: V.scalar_tensor_tensor(
                out=LQD[:, :], in0=LQC[:, :], scalar=-0.5 * m, in1=LQD[:, :],
                op0=OP.mult, op1=OP.add),
              reads=K("LQC", "LQD"), writes=K("LQD"))
        E("vector", lambda V: V.tensor_tensor(out=LQF[:, :], in0=LYH1[:, :], in1=LQD[:, :],
                                              op=OP.subtract),
          reads=K("LYH1", "LQD"), writes=K("LQF"))
        E("vector", lambda V, xt=xt: V.tensor_scalar(out=xt[:, :], in0=LQF[:, :],
                                                     scalar1=-128.0, scalar2=127.0,
                                                     op0=OP.max, op1=OP.min),
          reads=K("LQF"), writes=K(xtn))
        E("gpsimd", lambda G, jp=jp, xt=xt: G.dma_start(
            out=xqh_scr[jp * 128:(jp + 1) * 128, :], in_=xt[:, :]),
          reads=K(xtn), writes=["xqh_scr"], sem=f"xqo{b}", inc=16)

    def emit_passB(upto):
        while pb_state[0] < min(upto, NHC):
            passB_chunk(pb_state[0])
            pb_state[0] += 1

    for ci, (dst, srcp, nm) in enumerate([(cosT, cosTin, "cosT"), (sinT, sinTin, "sinT")]):
        E("sync", lambda S, d=dst, s=srcp: S.dma_start(out=d[:, :], in_=s),
          writes=K(nm), sem=f"ct{ci}", inc=16)

    for g in range(H // 2):
        gb = g % 2
        E("gpsimd", lambda G, g=g, gb=gb: G.dma_start(
            out=WVG[gb][:, :], in_=wv_b[g * 128:(g + 1) * 128, :]),
          writes=K(f"WVG{gb}"), sem=f"wv{gb}", inc=16)
        for tt in range(16):
            psn = 2 + (tt % 2)

            def mm_vt(TE, tt=tt, gb=gb, psn=psn):
                for dc in range(NDC):
                    TE.ldweights(XN16[:, dc * T + tt * 128: dc * T + (tt + 1) * 128])
                    i = TE.matmul(out=PS[psn][:, 0:256],
                                  lhsT=XN16[:, dc * T + tt * 128: dc * T + (tt + 1) * 128],
                                  rhs=WVG[gb][:, dc * 256:(dc + 1) * 256],
                                  start=(dc == 0), stop=(dc == NDC - 1))
                return i
            E("tensor", mm_vt, reads=K("XN16", f"WVG{gb}"), writes=[f"PS{psn}"])
            E("vector", lambda V, tt=tt, psn=psn: V.tensor_copy(
                out=VT16[:, tt * 128:(tt + 1) * 128], in_=PS[psn][:, 0:128]),
              reads=[f"PS{psn}"], writes=K("VT16"))
            E("vector", lambda V, tt=tt, psn=psn: V.tensor_copy(
                out=VT16[:, T + tt * 128: T + (tt + 1) * 128], in_=PS[psn][:, 128:256]),
              reads=[f"PS{psn}"], writes=K("VT16"))

        for hh in range(2):
            head = 2 * g + hh
            E("gpsimd", lambda G, head=head: G.dma_start(
                out=WQK[0][:, :], in_=wqkv_b[head * 128:(head + 1) * 128, :]),
              writes=K("WQK0"), sem="wq0", inc=16)
            E("gpsimd", lambda G, head=head: G.dma_start(
                out=WQK[1][:, :],
                in_=wqkv_b[(16 + head) * 128:(16 + head + 1) * 128, :]),
              writes=K("WQK1"), sem="wk0", inc=16)

            def mm_q(TE):
                for dc in range(NDC):
                    TE.ldweights(WQK[0][:, dc * 128:(dc + 1) * 128])
                    for j in range(2):
                        i = TE.matmul(out=PS[0][:, j * 512:(j + 1) * 512],
                                      lhsT=WQK[0][:, dc * 128:(dc + 1) * 128],
                                      rhs=XN16[:, dc * T + j * 512: dc * T + (j + 1) * 512],
                                      start=(dc == 0), stop=(dc == NDC - 1))
                return i
            E("tensor", mm_q, reads=K("WQK0", "XN16"), writes=["PS0"])

            def mm_k(TE):
                for dc in range(NDC):
                    TE.ldweights(WQK[1][:, dc * 128:(dc + 1) * 128])
                    for half in range(2):
                        for j in range(2):
                            i = TE.matmul(
                                out=PS[1 + half][:, j * 512:(j + 1) * 512],
                                lhsT=WQK[1][:, dc * 128:(dc + 1) * 128],
                                rhs=XN16[:, dc * T + half * TO + j * 512:
                                         dc * T + half * TO + (j + 1) * 512],
                                start=(dc == 0), stop=(dc == NDC - 1))
                return i
            E("tensor", mm_k, reads=K("WQK1", "XN16"), writes=["PS1", "PS2"])

            def rope_half(psn, dst, dstnm, dst_c0, n, tab_c0):
                E("vector", lambda V: V.tensor_copy(out=RTa[:, 0:n], in_=PS[psn][0:64, 0:n]),
                  reads=[f"PS{psn}"], writes=K("RTa"))
                E("vector", lambda V: V.tensor_copy(out=RTb[:, 0:n], in_=PS[psn][64:128, 0:n]),
                  reads=[f"PS{psn}"], writes=K("RTb"))
                E("vector", lambda V: V.tensor_tensor(
                    out=RTc[:, 0:n], in0=RTa[:, 0:n],
                    in1=cosT[0:64, tab_c0:tab_c0 + n], op=OP.mult),
                  reads=K("RTa", "cosT"), writes=K("RTc"))
                E("vector", lambda V: V.tensor_tensor(
                    out=RTd[:, 0:n], in0=RTb[:, 0:n],
                    in1=sinT[0:64, tab_c0:tab_c0 + n], op=OP.mult),
                  reads=K("RTb", "sinT"), writes=K("RTd"))
                E("vector", lambda V: V.tensor_tensor(
                    out=dst[0:64, dst_c0:dst_c0 + n], in0=RTc[:, 0:n], in1=RTd[:, 0:n],
                    op=OP.subtract),
                  reads=K("RTc", "RTd"), writes=K(dstnm))
                E("vector", lambda V: V.tensor_tensor(
                    out=RTc[:, 0:n], in0=RTb[:, 0:n],
                    in1=cosT[0:64, tab_c0:tab_c0 + n], op=OP.mult),
                  reads=K("RTb", "cosT"), writes=K("RTc"))
                E("vector", lambda V: V.tensor_tensor(
                    out=RTd[:, 0:n], in0=RTa[:, 0:n],
                    in1=sinT[0:64, tab_c0:tab_c0 + n], op=OP.mult),
                  reads=K("RTa", "sinT"), writes=K("RTd"))
                E("vector", lambda V: V.tensor_tensor(
                    out=dst[64:128, dst_c0:dst_c0 + n], in0=RTc[:, 0:n], in1=RTd[:, 0:n],
                    op=OP.add),
                  reads=K("RTc", "RTd"), writes=K(dstnm))

            rope_half(0, Q16, "Q16", 0, TO, 0)
            rope_half(1, K16, "K16", 0, TO, 0)
            rope_half(2, K16, "K16", TO, TO, TO)

            E("scalar", lambda S: S.activation(out=SQH[:, 0:TO], in_=Q16[:, :], func=AF.Square),
              reads=K("Q16"), writes=K("SQH"))

            def mm_mq(TE):
                for j in range(2):
                    TE.ldweights(ones16[:, :])
                    i = TE.matmul(out=PS[3][:, j * 512:(j + 1) * 512], lhsT=ones16[:, :],
                                  rhs=SQH[:, j * 512:(j + 1) * 512], start=True, stop=True)
                return i
            E("tensor", mm_mq, reads=K("SQH", "ones16"), writes=["PS3"])
            E("vector", lambda V: V.reduce_max(out=MQ[:, :], in_=PS[3][:, 0:TO], axis=AX.X),
              reads=["PS3"], writes=K("MQ"))
            E("scalar", lambda S: S.activation(out=SQH[:, :], in_=K16[:, :], func=AF.Square),
              reads=K("K16"), writes=K("SQH"))

            def mm_mk(TE):
                for half in range(2):
                    for j in range(2):
                        TE.ldweights(ones16[:, :])
                        i = TE.matmul(out=PS[3][:, j * 512:(j + 1) * 512], lhsT=ones16[:, :],
                                      rhs=SQH[:, half * TO + j * 512: half * TO + (j + 1) * 512],
                                      start=(half == 0), stop=(half == 1))
                return i
            E("tensor", mm_mk, reads=K("SQH", "ones16"), writes=["PS3"])
            E("vector", lambda V: V.reduce_max(out=MK[:, :], in_=PS[3][:, 0:TO], axis=AX.X),
              reads=["PS3"], writes=K("MK"))
            E("vector", lambda V: V.tensor_tensor(out=MQ[:, :], in0=MQ[:, :], in1=MK[:, :],
                                                  op=OP.mult),
              reads=K("MK", "MQ"), writes=K("MQ"))
            E("scalar", lambda S: S.activation(out=MK[:, :], in_=MQ[:, :], func=AF.Sqrt),
              reads=K("MQ"), writes=K("MK"))
            E("vector", lambda V: V.tensor_scalar(out=NEGM[:, :], in0=MK[:, :], scalar1=-RS,
                                                  scalar2=None, op0=OP.mult),
              reads=K("MK"), writes=K("NEGM"))

            def mm_s(TE, kt):
                TE.ldweights(K16[:, kt * 128:(kt + 1) * 128])
                for j in range(2):
                    i = TE.matmul(out=PS[1 + (kt % 2)][:, j * 512:(j + 1) * 512],
                                  lhsT=K16[:, kt * 128:(kt + 1) * 128],
                                  rhs=Q16[:, j * 512:(j + 1) * 512], start=True, stop=True)
                return i
            emit_passB((head + 1) * NHC // H)
            E("tensor", lambda TE: mm_s(TE, 0), reads=K("K16", "Q16"), writes=["PS1"])
            for kt in range(16):
                psn = 1 + (kt % 2)
                pb = PB[kt % 2]
                pbn = f"PB{kt % 2}"
                E("scalar", lambda S, psn=psn, pb=pb: S.activation(
                    out=pb[:, :], in_=PS[psn][:, 0:TO], func=AF.Exp, bias=NEGM[:, 0:1], scale=RS),
                  reads=[f"PS{psn}"] + K("NEGM"), writes=K(pbn))
                if kt < 15:
                    E("tensor", lambda TE, kt=kt: mm_s(TE, kt + 1), reads=K("K16", "Q16"),
                      writes=[f"PS{1 + ((kt + 1) % 2)}"])

                def mm_sum(TE, kt=kt, pb=pb):
                    TE.ldweights(ones16[:, :])
                    for j in range(2):
                        i = TE.matmul(out=PS[3][:, j * 512:(j + 1) * 512], lhsT=ones16[:, :],
                                      rhs=pb[:, j * 512:(j + 1) * 512],
                                      start=(kt == 0), stop=(kt == 15))
                    return i
                E("tensor", mm_sum, reads=K(pbn, "ones16"), writes=["PS3"])

                def mm_o(TE, kt=kt, hh=hh, pb=pb):
                    TE.ldweights(VT16[:, hh * T + kt * 128: hh * T + (kt + 1) * 128])
                    for j in range(2):
                        i = TE.matmul(out=PS[0][:, j * 512:(j + 1) * 512],
                                      lhsT=VT16[:, hh * T + kt * 128: hh * T + (kt + 1) * 128],
                                      rhs=pb[:, j * 512:(j + 1) * 512],
                                      start=(kt == 0), stop=(kt == 15))
                    return i
                E("tensor", mm_o, reads=K(pbn, "VT16"), writes=["PS0"])

            E("vector", lambda V: V.reciprocal(out=RECB[:, :], in_=PS[3][:, 0:TO]),
              reads=["PS3"], writes=K("RECB"))
            E("vector", lambda V, head=head: V.tensor_tensor(
                out=O16[:, head * TO:(head + 1) * TO], in0=PS[0][:, 0:TO], in1=RECB[:, :],
                op=OP.mult),
              reads=["PS0"] + K("RECB"), writes=K("O16"))
    tapk("o16", O16, ["O16"])

    emit_passB(NHC)

    for dco in range(NDC):
        wb = dco % 2
        E("gpsimd", lambda G, dc=dco, wb=wb: G.dma_start(
            out=WOB[wb][:, :], in_=wout_b[dc * 128:(dc + 1) * 128, :]),
          writes=K(f"WOB{wb}"), sem=f"wo{wb}", inc=16)

        def mm_sp(TE, wb=wb):
            for oc in range(NDC):
                TE.ldweights(WOB[wb][:, oc * 128:(oc + 1) * 128])
                for j in range(2):
                    i = TE.matmul(out=PS[0][:, j * 512:(j + 1) * 512],
                                  lhsT=WOB[wb][:, oc * 128:(oc + 1) * 128],
                                  rhs=O16[:, oc * TO + j * 512: oc * TO + (j + 1) * 512],
                                  start=(oc == 0), stop=(oc == NDC - 1))
            return i
        E("tensor", mm_sp, reads=K(f"WOB{wb}", "O16"), writes=["PS0"])
        E("vector", lambda V, dc=dco: V.tensor_copy(out=S16[:, dc * TO:(dc + 1) * TO],
                                                    in_=PS[0][:, 0:TO]),
          reads=["PS0"], writes=K("S16"))
    tapk("s16", S16, ["S16"])

    for g in range(4):
        for hc in range(NHC):
            b = hc % 2
            E("gpsimd", lambda G, hc=hc, b=b: G.dma_start(
                out=XQHR[b][:, :], in_=xqh_scr[hc * 128:(hc + 1) * 128, :]),
              reads=["xqh_scr"], writes=K(f"XQHR{b}"), sem=f"hq{b}", inc=16)
            E("sync", lambda S, g=g, hc=hc, b=b: S.dma_start(
                out=DWF[b][:, :], in_=dwT[hc * 128:(hc + 1) * 128, g * 512:(g + 1) * 512]),
              writes=K(f"DWF{b}"), sem=f"dq{b}", inc=16)
            E("scalar", lambda S, b=b: S.activation(out=U1D[:, :], in_=DWF[b][:, :],
                                                    func=AF.Sign, scale=SCAL[:, 3:4],
                                                    bias=HLF[:, 1:2]),
              reads=K(f"DWF{b}", "SCAL", "HLF"), writes=K("U1D"))
            E("scalar", lambda S, b=b: S.activation(out=U2D[:, :], in_=DWF[b][:, :],
                                                    func=AF.Sign, scale=SCAL[:, 3:4],
                                                    bias=HLF[:, 0:1]),
              reads=K(f"DWF{b}", "SCAL", "HLF"), writes=K("U2D"))
            E("vector", lambda V, b=b: V.tensor_tensor(out=WQDh[b][:, :], in0=U1D[:, :],
                                                       in1=U2D[:, :], op=OP.add),
              reads=K("U1D", "U2D"), writes=K(f"WQDh{b}"))

            def mm_dn(TE, hc=hc, b=b):
                for dcl in range(4):
                    TE.ldweights(WQDh[b][:, dcl * 128:(dcl + 1) * 128])
                    for j in range(2):
                        i = TE.matmul(out=PS[dcl][:, j * 512:(j + 1) * 512],
                                      lhsT=WQDh[b][:, dcl * 128:(dcl + 1) * 128],
                                      rhs=XQHR[b][:, j * 512:(j + 1) * 512],
                                      start=(hc == 0), stop=(hc == NHC - 1))
                return i
            E("tensor", mm_dn, reads=K(f"WQDh{b}", f"XQHR{b}"),
              writes=["PS0", "PS1", "PS2", "PS3"])
        for dcl in range(4):
            dc = 4 * g + dcl
            E("sync", lambda S, dc=dc: S.dma_start(out=XR[:, :], in_=xchunk(dc)[:, 0:TO]),
              writes=K("XR"), sem="xr", inc=16)
            E("vector", lambda V, dcl=dcl: V.tensor_tensor(out=OBA[:, :],
                                                           in0=PS[dcl][:, 0:TO],
                                                           in1=SCD[:, :], op=OP.mult),
              reads=[f"PS{dcl}"] + K("SCD"), writes=K("OBA"))
            E("scalar", lambda S: S.activation(out=OBB[:, :], in_=OBA[:, :], func=AF.Silu),
              reads=K("OBA"), writes=K("OBB"))
            E("gpsimd", lambda G, dc=dc: G.tensor_tensor(out=OBB[:, :], in0=OBB[:, :],
                                                         in1=S16[:, dc * TO:(dc + 1) * TO],
                                                         op=OP.mult),
              reads=K("S16", "OBB"), writes=K("OBB"))
            E("vector", lambda V: V.tensor_tensor(out=OBA[:, :], in0=OBB[:, :], in1=XR[:, :],
                                                  op=OP.add),
              reads=K("XR", "OBB"), writes=K("OBA"))
            E("gpsimd", lambda G, dc=dc: G.dma_start(
                out=outT[dc * 128:(dc + 1) * 128, :], in_=OBA[:, :]),
              reads=K("OBA"), sem=f"ost{dc % 2}", inc=16)

    p.final_wait("gpsimd", "ost0", 16 * (NDC // 2))
    p.final_wait("gpsimd", "ost1", 16 * (NDC // 2))
    p.realize()
    return nc, taps


_BUILD_CACHE = {}


def _get_nc():
    key = tuple(sorted(DEBUG_TAPS))
    if key not in _BUILD_CACHE:
        _BUILD_CACHE[key] = build()
    return _BUILD_CACHE[key]


def stage_inputs(L0_attn_x, L0_attn_w_qkv, L0_attn_w_out, L0_ffn_gate_w, L0_ffn_down_w,
                 L0_ffn_gate_norm_w, L0_ffn_down_norm_w):
    x = np.asarray(L0_attn_x, np.float32)
    wqkvT = np.asarray(L0_attn_w_qkv, np.float32).T
    wqkv_b = np.ascontiguousarray(
        wqkvT[:, :2 * D].reshape(NDC, 128, 32, 128).transpose(2, 1, 0, 3).reshape(32 * 128, NDC * 128))
    wv_b = np.ascontiguousarray(
        wqkvT[:, 2 * D:].reshape(NDC, 128, 8, 256).transpose(2, 1, 0, 3).reshape(8 * 128, NDC * 256))
    woutT = np.asarray(L0_attn_w_out, np.float32).T
    wout_b = np.ascontiguousarray(
        woutT.reshape(NDC, 128, NDC, 128).transpose(2, 1, 0, 3).reshape(NDC * 128, NDC * 128))
    gwT = np.asarray(L0_ffn_gate_w, np.float32).T
    idx = []
    for j in range(NHC):
        idx.extend(range(j * 128, (j + 1) * 128))
        idx.extend(range(INTER + j * 128, INTER + (j + 1) * 128))
    gwTp = gwT[:, idx]
    gw_b = np.ascontiguousarray(
        gwTp.reshape(NDC, 128, NHC, 256).transpose(2, 1, 0, 3).reshape(NHC * 128, NDC * 256))
    dwTc = np.ascontiguousarray(np.asarray(L0_ffn_down_w, np.float32).T)
    g1t = np.ascontiguousarray(np.asarray(L0_ffn_gate_norm_w, np.float32).reshape(NDC, 128).T)
    g2t = np.ascontiguousarray(np.asarray(L0_ffn_down_norm_w, np.float32).reshape(NHC, 128).T)
    onesb16 = np.ones((128, 128), ml_dtypes.bfloat16)
    identf = np.eye(128, dtype=np.float32)
    cvec = np.tile(np.array([[EPS, 0.0]], np.float32), (128, 1))
    cosE, sinE = _rope_tables_f32()
    in_maps = []
    for c in range(8):
        pair, half = c // 2, c % 2
        xp = x[pair].T
        perm = np.concatenate([np.arange(half * TO, (half + 1) * TO),
                               np.arange((1 - half) * TO, (2 - half) * TO)])
        in_maps.append(dict(
            xT=np.ascontiguousarray(xp[:, perm]),
            wqkv_b=wqkv_b, wv_b=wv_b, wout_b=wout_b, gw_b=gw_b, dwT=dwTc, g1t=g1t, g2t=g2t,
            onesb16=onesb16, identf=identf,
            cosTin=np.ascontiguousarray(cosE[perm].T),
            sinTin=np.ascontiguousarray(sinE[perm].T),
            cvec=cvec,
        ))
    return in_maps


def kernel(L0_attn_x, L0_attn_w_qkv, L0_attn_w_out, L0_ffn_gate_w, L0_ffn_down_w,
           L0_ffn_gate_norm_w, L0_ffn_down_norm_w):
    in_maps = stage_inputs(L0_attn_x, L0_attn_w_qkv, L0_attn_w_out, L0_ffn_gate_w,
                           L0_ffn_down_w, L0_ffn_gate_norm_w, L0_ffn_down_norm_w)
    nc, taps = _get_nc()
    res = run_bass_kernel_spmd(nc, in_maps, core_ids=list(range(8)))
    out = np.empty((4, T, D), np.float32)
    for c in range(8):
        pair, half = c // 2, c % 2
        out[pair, half * TO:(half + 1) * TO, :] = np.asarray(res.results[c]["outT"]).T
    if DEBUG_TAPS:
        kernel.tap_results = [
            {("tap_" + n): np.asarray(r.get("tap_" + n)) for n in taps} for r in res.results]
    return out


# version 15
# speedup vs baseline: 1.0603x; 1.0603x over previous
import math
import numpy as np
import ml_dtypes
import concourse.bass as bass
import concourse.mybir as mybir
from concourse.bass_utils import run_bass_kernel_spmd

F32 = mybir.dt.float32
BF16 = mybir.dt.bfloat16
I32 = mybir.dt.int32
AF = mybir.ActivationFunctionType
OP = mybir.AluOpType
AX = mybir.AxisListType

D = 2048
T = 2048
TO = 1024
H = 16
INTER = 5632
NDC = D // 128
NHC = INTER // 128
EPS = 1e-5
RS = 1.0 / math.sqrt(128.0)
LN1E4 = math.log(10000.0)
TWO_PI = 2.0 * math.pi
C1 = 6.28125
C2 = TWO_PI - C1
PI_F32 = float(np.float32(np.pi))
PI_LO = float(np.nextafter(np.float32(np.pi), np.float32(0.0)))

ENGINES = ("sync", "vector", "scalar", "tensor", "gpsimd")
ESEM = {"vector": "dv", "scalar": "ac", "tensor": "pe", "gpsimd": "gp", "sync": "sp"}

DEBUG_TAPS = set()

_INV_FREQ_BITS = np.array([1065353216,1063104471,1061157137,1059470816,1058010522,1056527311,1054337179,1052440603,1050798235,1049376004,1047712803,1045579757,1043732615,1042133056,1040747895,1038909398,1036831949,1035032952,1033475085,1032126029,1030116802,1028093503,1026341397,1024824136,1023510243,1021334738,1019364175,1017657739,1016180025,1014779187,1012562930,1010643729,1008981770,1007542573,1005959600,1003801109,1001931932,1000313292,998911607,997151246,995049016,993228559,991652109,990286959,988353832,986306396,984533390,982998030,981668463,979567074,977573004,975846212,974350870,973033388,970790693,968848599,967166816,965710450,964208661,962024420,960132947,958494998,957076594,955395298], dtype=np.uint32)


_TABLE_CACHE = {}


def _rope_tables_f32():
    if "t" in _TABLE_CACHE:
        return _TABLE_CACHE["t"]
    import os
    import subprocess
    import sys
    import tempfile
    inv = _INV_FREQ_BITS.view(np.float32)
    t = np.arange(T, dtype=np.float32)
    fr = np.einsum("i,j->ij", t, inv).astype(np.float32)
    emb = np.concatenate([fr, fr], axis=-1)
    cos = sin = None
    try:
        with tempfile.TemporaryDirectory() as td:
            np.save(os.path.join(td, "emb.npy"), emb)
            prog = ("import numpy as np, jax.numpy as jnp, sys, os;"
                    "td = sys.argv[1];"
                    "e = jnp.asarray(np.load(os.path.join(td, 'emb.npy')));"
                    "np.save(os.path.join(td, 'cos.npy'), np.asarray(jnp.cos(e)));"
                    "np.save(os.path.join(td, 'sin.npy'), np.asarray(jnp.sin(e)));"
                    "print('TBLOK')")
            env = {k: v for k, v in os.environ.items() if k != "PYTHONPATH"}
            env["JAX_PLATFORMS"] = "cpu"
            r = subprocess.run([sys.executable, "-c", prog, td], env=env,
                               capture_output=True, text=True, timeout=600)
            if "TBLOK" in r.stdout:
                cos = np.load(os.path.join(td, "cos.npy")).astype(np.float32)
                sin = np.load(os.path.join(td, "sin.npy")).astype(np.float32)
    except Exception:
        cos = None
    if cos is None:
        cos = np.cos(emb.astype(np.float64)).astype(np.float32)
        sin = np.sin(emb.astype(np.float64)).astype(np.float32)
    _TABLE_CACHE["t"] = (cos, sin)
    return cos, sin


class Prog:
    def __init__(self, nc, ctx):
        self.nc = nc
        self.ctx = ctx
        self.ops = {e: [] for e in ENGINES}
        self.sem_handles = {}
        self.counts = {}
        self.writers = {}
        self.readers = {}
        self.waited = {e: {} for e in ENGINES}

    def sem(self, name):
        if name not in self.sem_handles:
            self.sem_handles[name] = self.ctx.enter_context(self.nc.semaphore(name))
            self.counts[name] = 0
        return self.sem_handles[name]

    def _wait(self, eng, sname, val):
        if val <= 0 or self.waited[eng].get(sname, 0) >= val:
            return
        self.waited[eng][sname] = val
        h = self.sem_handles[sname]
        self.ops[eng].append(lambda E, h=h, v=val: E.wait_ge(h, v))

    def emit(self, eng, fn, reads=(), writes=(), sem=None, inc=1):
        for k in reads:
            for s, c in self.writers.get(k, {}).items():
                self._wait(eng, s, c)
        for k in writes:
            for s, c in self.writers.get(k, {}).items():
                self._wait(eng, s, c)
            for s, c in self.readers.get(k, {}).items():
                self._wait(eng, s, c)
        sname = sem or ESEM[eng]
        self.sem(sname)
        self.counts[sname] += inc
        cnt = self.counts[sname]
        h = self.sem_handles[sname]

        def run(E, fn=fn, h=h, inc=inc):
            fn(E).then_inc(h, inc)
        self.ops[eng].append(run)
        for k in writes:
            self.writers.setdefault(k, {})[sname] = cnt
            self.readers.pop(k, None)
        for k in reads:
            self.readers.setdefault(k, {})[sname] = cnt
        return sname, cnt

    def final_wait(self, eng, sname, val):
        h = self.sem(sname)
        self.ops[eng].append(lambda E, h=h, v=val: E.wait_ge(h, v))

    def realize(self):
        with self.nc.Block() as block:
            @block.sync
            def _(sync):
                for f in self.ops["sync"]:
                    f(sync)

            @block.vector
            def _(vector):
                for f in self.ops["vector"]:
                    f(vector)

            @block.scalar
            def _(scalar):
                for f in self.ops["scalar"]:
                    f(scalar)

            @block.tensor
            def _(tensor):
                for f in self.ops["tensor"]:
                    f(tensor)

            @block.gpsimd
            def _(gpsimd):
                for f in self.ops["gpsimd"]:
                    f(gpsimd)


def build():
    import contextlib
    nc = bass.Bass("TRN2", target_bir_lowering=False, debug=False)
    ctx = contextlib.ExitStack()
    p = Prog(nc, ctx)
    E = p.emit

    xT = nc.dram_tensor("xT", [D, T], F32, kind="ExternalInput").ap()
    wqkv_b = nc.dram_tensor("wqkv_b", [32 * 128, NDC * 128], F32, kind="ExternalInput").ap()
    wv_b = nc.dram_tensor("wv_b", [8 * 128, NDC * 256], F32, kind="ExternalInput").ap()
    wout_b = nc.dram_tensor("wout_b", [NDC * 128, NDC * 128], F32, kind="ExternalInput").ap()
    gw_b = nc.dram_tensor("gw_b", [NHC * 128, NDC * 256], F32, kind="ExternalInput").ap()
    dwT = nc.dram_tensor("dwT", [INTER, D], F32, kind="ExternalInput").ap()
    g1t = nc.dram_tensor("g1t", [128, NDC], F32, kind="ExternalInput").ap()
    g2t = nc.dram_tensor("g2t", [128, NHC], F32, kind="ExternalInput").ap()
    onesb16 = nc.dram_tensor("onesb16", [128, 128], BF16, kind="ExternalInput").ap()
    identf = nc.dram_tensor("identf", [128, 128], F32, kind="ExternalInput").ap()
    cosTin = nc.dram_tensor("cosTin", [128, T], F32, kind="ExternalInput").ap()
    sinTin = nc.dram_tensor("sinTin", [128, T], F32, kind="ExternalInput").ap()
    cvec = nc.dram_tensor("cvec", [128, 2], F32, kind="ExternalInput").ap()

    outT = nc.dram_tensor("outT", [D, TO], F32, kind="ExternalOutput").ap()
    h_scr = nc.dram_tensor("h_scr", [INTER, TO], F32, kind="Internal").ap()
    xqh_scr = nc.dram_tensor("xqh_scr", [INTER, TO], BF16, kind="Internal").ap()
    sc_scr = nc.dram_tensor("sc_scr", [1, 4], F32, kind="Internal").ap()

    akey = {}

    def place(name, shape, dt, off, keys):
        t = nc.alloc_sbuf_tensor_at(name, shape, dt, offset=off)
        akey[name] = (keys,) if isinstance(keys, str) else tuple(keys)
        return t

    Z0 = 16544
    Z1 = Z0 + 81920
    Z2 = Z1 + 32768
    Z3 = Z2 + 32768
    Z4 = Z3 + 32768
    Z5 = Z4 + 24576
    XN16 = place("XN16", [128, NDC * T], BF16, Z0, "z0a")
    cosT = place("cosT", [128, T], F32, Z0 + 65536, "z0b")
    sinT = place("sinT", [128, T], F32, Z0 + 73728, "z0c")
    GWF = [place("GWF0", [128, NDC * 256], F32, Z1, ("z1a", "z1b")),
           place("GWF1", [128, NDC * 256], F32, Z1 + 16384,
                 ("z1c", "z1d", "z1e", "z1f"))]
    WQ16 = [place("WQ16_0", [128, NDC * 256], BF16, Z4, ("z4a", "z4b")),
            place("WQ16_1", [128, NDC * 256], BF16, Z4 + 16384, ("z4e", "z4f"))]
    U1 = place("U1", [128, 2048], F32, Z0 + 65536, "z0b")
    U2 = place("U2", [128, 2048], F32, Z0 + 73728, "z0c")
    RSTD = place("RSTD", [128, T], F32, Z1, "z1a")
    MUSC = place("MUSC", [128, T], F32, Z1 + 8192, "z1b")
    RRMS = place("RRMS", [128, TO], F32, Z1 + 16384, "z1c")
    CG = place("CG", [128, TO], F32, Z1 + 20480, "z1d")
    MM1 = place("MM1", [128, TO], F32, Z1 + 24576, "z1e")
    K16 = place("K16", [128, T], BF16, Z1, "z1a")
    VT16 = place("VT16", [128, 2 * T], BF16, Z1 + 4096, ("z1a", "z1b"))
    Q16 = place("Q16", [128, TO], BF16, Z1 + 12288, "z1b")
    SQH = place("SQH", [128, T], BF16, Z1 + 14336, ("z1b", "z1c"))
    RECB = place("RECB", [128, TO], F32, Z1 + 18432, ("z1c", "z1d"))
    LNS = place("LNS", [128, TO], F32, Z1 + 22528, ("z1d", "z1e"))
    PB = [place("PB0", [128, TO], BF16, Z1 + 26624, "z1e"),
          place("PB1", [128, TO], BF16, Z1 + 28672, "z1f")]
    S16 = place("S16", [128, NDC * TO], BF16, Z1,
                ("z1a", "z1b", "z1c", "z1d", "z1e", "z1f"))
    XB = place("XB", [128, T], F32, Z2, "z2a")
    XB1 = place("XB1", [128, T], F32, Z3, ("z3a",))
    SQB = place("SQB", [128, T], F32, Z2 + 8192, "z2b")
    TMP1 = place("TMP1", [128, T], F32, Z2 + 16384, "z2c")
    TMP2 = place("TMP2", [128, T], F32, Z2 + 24576, "z2d")
    QT = place("QT", [128, TO], F32, Z2 + 8192, "z2b")
    QI = place("QI", [128, TO], I32, Z2 + 12288, "z2b")
    QC = place("QC", [128, TO], F32, Z2 + 16384, "z2c")
    QN = place("QN", [128, TO], F32, Z2 + 20480, "z2c")
    QF = place("QF", [128, TO], F32, Z2 + 24576, "z2d")
    QD = place("QD", [128, TO], F32, Z2 + 28672, "z2d")
    WQK = [place("WQK0", [128, NDC * 128], BF16, Z2, "z2a"),
           place("WQK1", [128, NDC * 128], BF16, Z2 + 4096, "z2a")]
    LQD = place("LQD", [128, TO], F32, Z2 + 8192, "z2b")
    LQC = place("LQC", [128, TO], F32, Z2 + 12288, "z2b")
    WVG = [place("WVG0", [128, NDC * 256], BF16, Z2 + 16384, "z2c"),
           place("WVG1", [128, NDC * 256], BF16, Z2 + 24576, "z2d")]
    WOB = [place("WOB0", [128, NDC * 128], BF16, Z2, "z2a"),
           place("WOB1", [128, NDC * 128], BF16, Z2 + 4096, "z2a")]
    EV1 = place("EV1", [128, TO], F32, Z2, "z2a")
    EV2 = place("EV2", [128, TO], F32, Z2 + 4096, "z2a")
    EV3 = [place("EV3a", [128, TO], F32, Z2 + 8192, "z2b"),
           place("EV3b", [128, TO], F32, Z2 + 12288, "z2b")]
    HB = place("HB", [128, TO], F32, Z4 + 4096, "z4b")
    LYH1 = place("LYH1", [128, TO], F32, Z4 + 8192, "z4c")
    LQF = place("LQF", [128, TO], F32, Z4 + 12288, "z4d")
    YH1 = place("YH1", [128, TO], F32, Z2 + 8192, "z2b")
    YH2 = place("YH2", [128, TO], F32, Z2 + 12288, "z2b")
    HS1 = place("HS1", [128, TO], F32, Z2 + 16384, "z2c")
    HS2 = place("HS2", [128, TO], F32, Z2 + 20480, "z2c")
    YH2p = place("YH2p", [128, TO], F32, Z2 + 16384, "z2c")
    HSQA = place("HSQA", [128, TO], F32, Z2 + 20480, "z2c")
    SQF = place("SQF", [128, TO], F32, Z2 + 24576, "z2d")
    DWF = [place("DWF0", [128, 512], F32, Z3 + 4096, "z3a"),
           place("DWF1", [128, 512], F32, Z3 + 6144, "z3a")]
    U1D = place("U1D", [128, 512], F32, Z3 + 8192, "z3a")
    U2D = place("U2D", [128, 512], F32, Z3 + 10240, "z3a")
    WQDh = [place("WQDh0", [128, 512], BF16, Z3 + 12288, "z3a"),
            place("WQDh1", [128, 512], BF16, Z3 + 13312, "z3a")]
    XQHR = [place("XQHR0", [128, TO], BF16, Z3, "z3a"),
            place("XQHR1", [128, TO], BF16, Z3 + 2048, "z3a")]
    OBA = place("OBA", [128, TO], F32, Z2 + 22528, ("z2c", "z2d"))
    OBB = place("OBB", [128, TO], F32, Z2 + 26624, "z2d")
    XR = place("XR", [128, TO], F32, Z3 + 16384, "z3b")
    XQ = place("XQ", [128, NDC * TO], BF16, Z3, ("z3a", "z3b"))
    O16 = place("O16", [128, H * TO], BF16, Z3, ("z3a", "z3b"))
    SQ16 = place("SQ16", [128, T], BF16, Z4, "z4a")
    XB16 = place("XB16", [128, T], BF16, Z4 + 4096, "z4b")
    WRB = [place("WRB0", [128, 2048], BF16, Z4 + 8192, "z4c"),
           place("WRB1", [128, 2048], BF16, Z4 + 12288, "z4d")]
    RTa = place("RTa", [64, TO], F32, Z4 + 16384, "z4e")
    RTb = place("RTb", [64, TO], F32, Z1 + 22528, ("z1d", "z1e"))
    RTc = place("RTc", [64, TO], F32, Z1 + 18432, ("z1c", "z1d"))
    RTd = place("RTd", [64, TO], F32, Z1 + 14336, ("z1b", "z1c"))
    SCG = place("SCG", [128, TO], F32, Z4 + 8192, "z4c")
    MMH = place("MMH", [128, TO], F32, Z4 + 12288, "z4d")
    CCH = place("CCH", [128, TO], F32, Z4, "z4a")
    SCD = place("SCD", [128, TO], F32, Z4 + 20480, "z4f")
    z5 = [Z5]

    def small(name, shape, dt=F32):
        t = place(name, shape, dt, z5[0], name)
        sz = shape[1] * {F32: 4, BF16: 2, I32: 4}[dt]
        z5[0] += (sz + 31) // 32 * 32
        return t

    g1s = small("g1s", [128, NDC])
    g2s = small("g2s", [128, NHC])
    ones16 = small("ones16", [128, 128], BF16)
    idf = small("idf", [128, 128])
    epsv = small("epsv", [128, 2])
    SCAL = small("SCAL", [128, 4])
    SCLT = small("SCLT", [1, 4])
    MCOMP = small("MCOMP", [128, 8])
    MQ = small("MQ", [128, 1])
    MK = small("MK", [128, 1])
    NEGM = small("NEGM", [128, 1])
    WACC = small("WACC", [128, 2])
    WS1 = small("WS1", [128, 1])
    WS2 = small("WS2", [128, 1])
    WS1D = small("WS1D", [128, 1])
    WS2D = small("WS2D", [128, 1])
    BCH = small("BCH", [128, 128])
    HLF = small("HLF", [128, 2])
    ONESF = small("ONESF", [128, 128])
    LADB = small("LADB", [128, 14])
    DEN_G = small("DEN_G", [128, TO])
    XQHT = [place("XQHT0", [128, TO], BF16, z5[0] - 4096, "DEN_G"),
            place("XQHT1", [128, TO], BF16, z5[0] - 2048, "DEN_G")]
    assert z5[0] <= 229376, z5[0]

    PS = [nc.alloc_psum_tensor(f"PS{i}", [128, TO], F32) for i in range(4)]

    def K(*names):
        out = []
        for n in names:
            out.extend(akey.get(n, (n,)))
        return out

    taps = []

    def tapk(name, tile, keys, rows=128, cols=None):
        if name not in DEBUG_TAPS:
            return
        c = cols if cols is not None else tile.shape[1]
        dt = tile.dtype
        o = nc.dram_tensor("tap_" + name, [rows, c], dt, kind="ExternalOutput").ap()
        taps.append(name)
        E("gpsimd", lambda G, o=o, t=tile, r=rows, c=c: G.dma_start(out=o, in_=t[0:r, 0:c]),
          reads=K(*keys), sem=f"tp{len(taps)}", inc=16)

    for ci, (dst, srcp, nm) in enumerate([(g1s, g1t, "g1s"), (g2s, g2t, "g2s"),
                                          (ones16, onesb16, "ones16"), (idf, identf, "idf"),
                                          (epsv, cvec, "epsv")]):
        E("sync", lambda S, d=dst, s=srcp: S.dma_start(out=d[:, :], in_=s),
          writes=K(nm), sem=f"cs{ci}", inc=16)

    for nm in ("WS1", "WS2", "WS1D", "WS2D"):
        t = {"WS1": WS1, "WS2": WS2, "WS1D": WS1D, "WS2D": WS2D}[nm]
        E("vector", lambda V, t=t: V.memset(t[:, :], 0.0), writes=K(nm))
    gw_flat = gw_b.rearrange("a (b c) -> (a b) c", b=2)
    wred_list = [(gw_flat[rg * 128:(rg + 1) * 128, :], 2048) for rg in range(NHC * 2)]
    wred_list += [(dwT[rg * 128:(rg + 1) * 128, :], 2048) for rg in range(NHC)]
    NGW = NHC * 2

    for idxw, (apw, w) in enumerate(wred_list):
        b = idxw % 2
        E("gpsimd", lambda G, b=b, apw=apw, w=w: G.dma_start(out=WRB[b][:, 0:w], in_=apw),
          writes=K(f"WRB{b}"), sem=f"wr{b}", inc=16)
        awt = SQ16 if b == 0 else XB16
        awk = "SQ16" if b == 0 else "XB16"
        E("scalar", lambda S, b=b, w=w, awt=awt: S.activation(out=awt[:, 0:w],
                                                              in_=WRB[b][:, 0:w],
                                                              func=AF.Abs),
          reads=K(f"WRB{b}"), writes=K(awk))
        isgw = idxw < NGW
        ps_lo = 0 if isgw else 2
        st = (idxw == 0) if isgw else (idxw == NGW)
        sp = (idxw == NGW - 1) if isgw else (idxw == len(wred_list) - 1)

        def mm_wred(TE, w=w, ps_lo=ps_lo, st=st, sp=sp, awt=awt):
            nj = (w + 511) // 512
            for j in range(nj):
                wj = min(512, w - j * 512)
                TE.ldweights(ones16[:, :])
                i = TE.matmul(out=PS[ps_lo + j // 2][:, (j % 2) * 512:(j % 2) * 512 + wj],
                              lhsT=ones16[:, :], rhs=awt[:, j * 512:j * 512 + wj],
                              start=st, stop=sp)
            return i
        E("tensor", mm_wred, reads=K(awk, "ones16"),
          writes=[f"PS{ps_lo}", f"PS{ps_lo + 1}"])
    E("vector", lambda V: V.reduce_sum(out=WS1[:, :], in_=PS[0][:, :], axis=AX.X),
      reads=["PS0"], writes=K("WS1"))
    E("vector", lambda V: V.reduce_sum(out=WS2[:, :], in_=PS[1][:, :], axis=AX.X),
      reads=["PS1"], writes=K("WS2"))
    E("vector", lambda V: V.reduce_sum(out=WS1D[:, :], in_=PS[2][:, :], axis=AX.X),
      reads=["PS2"], writes=K("WS1D"))
    E("vector", lambda V: V.reduce_sum(out=WS2D[:, :], in_=PS[3][:, :], axis=AX.X),
      reads=["PS3"], writes=K("WS2D"))
    E("vector", lambda V: V.tensor_tensor(out=WACC[:, 0:1], in0=WS1[:, :], in1=WS2[:, :],
                                          op=OP.add),
      reads=K("WS1", "WS2"), writes=K("WACC"))
    E("vector", lambda V: V.tensor_tensor(out=WACC[:, 1:2], in0=WS1D[:, :], in1=WS2D[:, :],
                                          op=OP.add),
      reads=K("WS1D", "WS2D"), writes=K("WACC"))
    E("vector", lambda V: V.tensor_scalar(out=SCAL[:, 0:1], in0=WACC[:, 0:1],
                                          scalar1=1.0 / (D * 2 * INTER), scalar2=EPS,
                                          op0=OP.mult, op1=OP.max),
      reads=K("WACC"), writes=K("SCAL"))
    E("vector", lambda V: V.tensor_scalar(out=SCAL[:, 1:2], in0=WACC[:, 1:2],
                                          scalar1=1.0 / (INTER * D), scalar2=EPS,
                                          op0=OP.mult, op1=OP.max),
      reads=K("WACC"), writes=K("SCAL"))
    E("vector", lambda V: V.reciprocal(out=SCAL[:, 2:3], in_=SCAL[:, 0:1]),
      reads=K("SCAL"), writes=K("SCAL"))
    E("vector", lambda V: V.reciprocal(out=SCAL[:, 3:4], in_=SCAL[:, 1:2]),
      reads=K("SCAL"), writes=K("SCAL"))

    E("vector", lambda V: V.memset(HLF[:, 0:1], 0.5), writes=K("HLF"))
    E("vector", lambda V: V.memset(ONESF[:, :], 1.0), writes=K("ONESF"))
    E("vector", lambda V: V.memset(HLF[:, 1:2], -0.5), writes=K("HLF"))
    for i, m in enumerate((64.0, 32.0, 16.0, 8.0, 4.0, 2.0, 1.0)):
        E("vector", lambda V, i=i, m=m: V.memset(LADB[:, 2 * i:2 * i + 1], m - 0.5),
          writes=K("LADB"))
        E("vector", lambda V, i=i, m=m: V.memset(LADB[:, 2 * i + 1:2 * i + 2], -(m - 0.5)),
          writes=K("LADB"))

    def xchunk(dc):
        return xT[dc * 128:(dc + 1) * 128, :]

    for dc in range(NDC):
        xb = XB if dc % 2 == 0 else XB1
        xbn = "XB" if dc % 2 == 0 else "XB1"
        E("sync", lambda S, dc=dc, xb=xb: S.dma_start(out=xb[:, :], in_=xchunk(dc)),
          writes=K(xbn), sem=f"xb{dc % 2}", inc=16)
        E("scalar", lambda S, xb=xb: S.activation(out=SQB[:, :], in_=xb[:, :], func=AF.Square),
          reads=K(xbn), writes=K("SQB"))
        E("vector", lambda V, xb=xb: V.tensor_copy(out=XB16[:, :], in_=xb[:, :]),
          reads=K(xbn), writes=K("XB16"))
        E("vector", lambda V: V.tensor_copy(out=SQ16[:, :], in_=SQB[:, :]),
          reads=K("SQB"), writes=K("SQ16"))
        st, sp = dc == 0, dc == NDC - 1

        def mm_stats(TE, st=st, sp=sp):
            for half in range(2):
                for j in range(2):
                    TE.ldweights(ones16[:, :])
                    TE.matmul(out=PS[half][:, 512 * j:512 * (j + 1)], lhsT=ones16[:, :],
                              rhs=XB16[:, half * TO + 512 * j: half * TO + 512 * (j + 1)],
                              start=st, stop=sp)
                    TE.ldweights(ones16[:, :])
                    i = TE.matmul(out=PS[2 + half][:, 512 * j:512 * (j + 1)], lhsT=ones16[:, :],
                                  rhs=SQ16[:, half * TO + 512 * j: half * TO + 512 * (j + 1)],
                                  start=st, stop=sp)
            return i
        E("tensor", mm_stats, reads=K("XB16", "SQ16", "ones16"),
          writes=["PS0", "PS1", "PS2", "PS3"])

    MUt, VARt = XB, SQB
    for half in range(2):
        E("vector", lambda V, h=half: V.tensor_scalar(out=MUt[:, h * TO:(h + 1) * TO],
                                                      in0=PS[h][:, :], scalar1=1.0 / D,
                                                      scalar2=None, op0=OP.mult),
          reads=[f"PS{half}"], writes=K("XB"))
        E("vector", lambda V, h=half: V.tensor_scalar(out=VARt[:, h * TO:(h + 1) * TO],
                                                      in0=PS[2 + h][:, :], scalar1=1.0 / D,
                                                      scalar2=None, op0=OP.mult),
          reads=[f"PS{2 + half}"], writes=K("SQB"))
    E("vector", lambda V: V.tensor_tensor(out=TMP1[:, :], in0=MUt[:, :], in1=MUt[:, :], op=OP.mult),
      reads=K("XB"), writes=K("TMP1"))
    E("vector", lambda V: V.tensor_tensor(out=VARt[:, :], in0=VARt[:, :], in1=TMP1[:, :],
                                          op=OP.subtract),
      reads=K("TMP1", "SQB"), writes=K("SQB"))
    E("scalar", lambda S: S.activation(out=TMP1[:, :], in_=VARt[:, :], func=AF.Ln,
                                       bias=epsv[:, 0:1]),
      reads=K("SQB", "epsv"), writes=K("TMP1"))
    E("scalar", lambda S: S.activation(out=RSTD[:, :], in_=TMP1[:, :], func=AF.Exp, scale=-0.5),
      reads=K("TMP1"), writes=K("RSTD"))
    E("vector", lambda V: V.tensor_tensor(out=MUSC[:, :], in0=MUt[:, :], in1=RSTD[:, :],
                                          op=OP.mult),
      reads=K("XB", "RSTD"), writes=K("MUSC"))
    E("vector", lambda V: V.tensor_scalar(out=TMP1[:, 0:TO], in0=VARt[:, 0:TO], scalar1=EPS,
                                          scalar2=None, op0=OP.add),
      reads=K("SQB"), writes=K("TMP1"))
    E("vector", lambda V: V.reciprocal(out=MM1[:, :], in_=TMP1[:, 0:TO]),
      reads=K("TMP1"), writes=K("MM1"))
    E("vector", lambda V: V.tensor_tensor(out=TMP2[:, 0:TO], in0=VARt[:, 0:TO], in1=MM1[:, :],
                                          op=OP.mult),
      reads=K("SQB", "MM1"), writes=K("TMP2"))
    E("scalar", lambda S: S.activation(out=TMP1[:, 0:TO], in_=TMP2[:, 0:TO], func=AF.Ln,
                                       bias=epsv[:, 0:1]),
      reads=K("TMP2", "epsv"), writes=K("TMP1"))
    E("scalar", lambda S: S.activation(out=RRMS[:, :], in_=TMP1[:, 0:TO], func=AF.Exp, scale=-0.5),
      reads=K("TMP1"), writes=K("RRMS"))
    tapk("rstd", RSTD, ["RSTD"])
    tapk("rrms", RRMS, ["RRMS"])

    for dc in range(NDC):
        xb = XB if dc % 2 == 0 else XB1
        xbn = "XB" if dc % 2 == 0 else "XB1"
        E("sync", lambda S, dc=dc, xb=xb: S.dma_start(out=xb[:, :], in_=xchunk(dc)),
          writes=K(xbn), sem=f"xb{dc % 2}", inc=16)
        E("vector", lambda V, xb=xb: V.tensor_tensor(out=TMP1[:, :], in0=xb[:, :], in1=RSTD[:, :],
                                              op=OP.mult),
          reads=K(xbn, "RSTD"), writes=K("TMP1"))
        E("vector", lambda V: V.tensor_tensor(out=TMP2[:, :], in0=TMP1[:, :], in1=MUSC[:, :],
                                              op=OP.subtract),
          reads=K("TMP1", "MUSC"), writes=K("TMP2"))
        E("vector", lambda V, dc=dc: V.tensor_copy(out=XN16[:, dc * T:(dc + 1) * T],
                                                   in_=TMP2[:, :]),
          reads=K("TMP2"), writes=K("XN16"))
        E("vector", lambda V, dc=dc: V.tensor_scalar(out=QT[:, :], in0=TMP2[:, 0:TO],
                                                     scalar1=g1s[:, dc:dc + 1], scalar2=None,
                                                     op0=OP.mult),
          reads=K("TMP2", "g1s"), writes=K("QT"))
        E("scalar", lambda S: S.activation(out=QF[:, :], in_=QT[:, :], func=AF.Abs),
          reads=K("QT"), writes=K("QF"))
        if dc == 0:
            E("vector", lambda V: V.tensor_copy(out=MM1[:, :], in_=QF[:, :]),
              reads=K("QF"), writes=K("MM1"))
        else:
            E("vector", lambda V: V.tensor_tensor(out=MM1[:, :], in0=MM1[:, :], in1=QF[:, :],
                                                  op=OP.max),
              reads=K("QF", "MM1"), writes=K("MM1"))
    tapk("xn16", XN16, ["XN16"])

    def compact_max(src, srckeys, dst, dstnm):
        for blk in range(8):
            E("tensor", lambda TE, blk=blk: TE.transpose(
                out=PS[0][:, 0:128], in_=src[:, blk * 128:(blk + 1) * 128], identity=idf[:, :]),
              reads=K(*srckeys) + K("idf"), writes=["PS0"])
            E("vector", lambda V, blk=blk: V.reduce_max(out=dst[:, blk:blk + 1],
                                                        in_=PS[0][:, 0:128], axis=AX.X),
              reads=["PS0"], writes=K(dstnm))

    def bcast_cols(srccomp, srcnm, dst, dstkeys, colbase, nblk=8):
        for blk in range(nblk):
            E("vector", lambda V, blk=blk: V.tensor_copy(
                out=BCH[:, :], in_=srccomp[:, blk:blk + 1].broadcast_to([128, 128])),
              reads=K(srcnm), writes=K("BCH"))
            E("tensor", lambda TE: TE.transpose(out=PS[0][:, 0:128], in_=BCH[:, :],
                                                identity=idf[:, :]),
              reads=K("BCH", "idf"), writes=["PS0"])
            E("vector", lambda V, blk=blk: V.tensor_copy(
                out=dst[:, colbase + blk * 128: colbase + (blk + 1) * 128], in_=PS[0][:, 0:128]),
              reads=["PS0"], writes=K(*dstkeys))

    compact_max(MM1, ["MM1"], MCOMP, "MCOMP")
    bcast_cols(MCOMP, "MCOMP", TMP1, ["TMP1"], 0)
    E("vector", lambda V: V.tensor_tensor(out=QT[:, :], in0=RRMS[:, :], in1=TMP1[:, 0:TO],
                                          op=OP.mult),
      reads=K("RRMS", "TMP1"), writes=K("QT"))
    E("vector", lambda V: V.tensor_scalar(out=DEN_G[:, :], in0=QT[:, :], scalar1=EPS,
                                          scalar2=None, op0=OP.max),
      reads=K("QT"), writes=K("DEN_G"))
    E("vector", lambda V: V.reciprocal(out=QF[:, :], in_=DEN_G[:, :]),
      reads=K("DEN_G"), writes=K("QF"))
    E("vector", lambda V: V.tensor_tensor(out=QT[:, :], in0=RRMS[:, :], in1=QF[:, :], op=OP.mult),
      reads=K("RRMS", "QF"), writes=K("QT"))
    E("vector", lambda V: V.tensor_scalar(out=CG[:, :], in0=QT[:, :], scalar1=127.0,
                                          scalar2=None, op0=OP.mult),
      reads=K("QT"), writes=K("CG"))
    tapk("cg", CG, ["CG"])
    tapk("deng", DEN_G, ["DEN_G"])

    def cast_correct(src_t, srckeys, dst, dstkeys, dst_c0):
        E("vector", lambda V: V.tensor_copy(out=QI[:, :], in_=src_t[:, 0:TO]),
          reads=K(*srckeys), writes=K("QI"))
        E("vector", lambda V: V.tensor_scalar(out=QF[:, :], in0=QI[:, :], scalar1=-128.0,
                                              scalar2=127.0, op0=OP.max, op1=OP.min),
          reads=K("QI"), writes=K("QF"))
        E("gpsimd", lambda G: G.tensor_tensor(out=QD[:, :], in0=src_t[:, 0:TO], in1=QF[:, :],
                                              op=OP.subtract),
          reads=K(*srckeys) + K("QF"), writes=K("QD"))
        E("vector", lambda V: V.tensor_scalar(out=QC[:, :], in0=QD[:, :], scalar1=0.5,
                                              scalar2=None, op0=OP.is_ge),
          reads=K("QD"), writes=K("QC"))
        E("vector", lambda V: V.scalar_tensor_tensor(out=QN[:, :], in0=QD[:, :], scalar=-0.5,
                                                     in1=QC[:, :], op0=OP.is_le, op1=OP.subtract),
          reads=K("QD", "QC"), writes=K("QN"))
        E("gpsimd", lambda G: G.tensor_tensor(out=dst[:, dst_c0:dst_c0 + TO], in0=QF[:, :],
                                              in1=QN[:, :], op=OP.subtract),
          reads=K("QF", "QN"), writes=K(*dstkeys))

    def ladder_round(src_t, srckeys, dst, dstkeys, dst_c0, eng2="vector"):
        E("vector", lambda V: V.tensor_copy(out=QD[:, :], in_=src_t[:, 0:TO]),
          reads=K(*srckeys), writes=K("QD"))
        for i, m in enumerate((64.0, 32.0, 16.0, 8.0, 4.0, 2.0, 1.0)):
            E(eng2, lambda X, m=m: X.tensor_scalar(out=QC[:, :], in0=QD[:, :], scalar1=m - 0.5,
                                                   scalar2=None, op0=OP.is_ge),
              reads=K("QD"), writes=K("QC"))
            E("vector", lambda X, m=m: X.scalar_tensor_tensor(out=QC[:, :], in0=QD[:, :],
                                                              scalar=-(m - 0.5), in1=QC[:, :],
                                                              op0=OP.is_le, op1=OP.subtract),
              reads=K("QD", "QC"), writes=K("QC"))
            E("vector", lambda X, m=m: X.scalar_tensor_tensor(out=QD[:, :], in0=QC[:, :], scalar=m,
                                                              in1=QD[:, :], op0=OP.mult, op1=OP.add),
              reads=K("QC", "QD"), writes=K("QD"))
        E("vector", lambda V: V.tensor_tensor(out=QF[:, :], in0=src_t[:, 0:TO], in1=QD[:, :],
                                              op=OP.subtract),
          reads=K(*srckeys) + K("QD"), writes=K("QF"))
        E("vector", lambda V: V.tensor_scalar(out=dst[:, dst_c0:dst_c0 + TO], in0=QF[:, :],
                                              scalar1=-128.0, scalar2=127.0, op0=OP.max,
                                              op1=OP.min),
          reads=K("QF"), writes=K(*dstkeys))

    for dc in range(NDC):
        E("sync", lambda S, dc=dc: S.dma_start(out=XB[:, 0:TO], in_=xchunk(dc)[:, 0:TO]),
          writes=K("XB"), sem="xb", inc=16)
        E("vector", lambda V: V.tensor_tensor(out=TMP1[:, 0:TO], in0=XB[:, 0:TO],
                                              in1=RSTD[:, 0:TO], op=OP.mult),
          reads=K("XB", "RSTD"), writes=K("TMP1"))
        E("vector", lambda V: V.tensor_tensor(out=TMP2[:, 0:TO], in0=TMP1[:, 0:TO],
                                              in1=MUSC[:, 0:TO], op=OP.subtract),
          reads=K("TMP1", "MUSC"), writes=K("TMP2"))
        E("vector", lambda V, dc=dc: V.tensor_scalar(out=QT[:, :], in0=TMP2[:, 0:TO],
                                                     scalar1=g1s[:, dc:dc + 1], scalar2=None,
                                                     op0=OP.mult),
          reads=K("TMP2", "g1s"), writes=K("QT"))
        E("vector", lambda V: V.tensor_tensor(out=QT[:, :], in0=QT[:, :], in1=CG[:, :],
                                              op=OP.mult),
          reads=K("CG", "QT"), writes=K("QT"))
        cast_correct(QT, ["QT"], XQ, ["XQ"], dc * TO)
    tapk("xq", XQ, ["XQ"])

    E("vector", lambda V: V.tensor_scalar(out=SCG[:, :], in0=DEN_G[:, :], scalar1=SCAL[:, 0:1],
                                          scalar2=0.5 / 127.0, op0=OP.mult, op1=OP.mult),
      reads=K("DEN_G", "SCAL"), writes=K("SCG"))

    def hw_tile(jp):
        return h_scr[jp * 128:(jp + 1) * 128, :]

    for jp in range(NHC):
        gb = jp % 2
        psg, psgn = PS[2 * gb], f"PS{2 * gb}"
        psu, psun = PS[2 * gb + 1], f"PS{2 * gb + 1}"
        ev = EV3[gb]
        evn = "EV3a" if gb == 0 else "EV3b"
        E("sync", lambda S, jp=jp, gb=gb: S.dma_start(
            out=GWF[gb][:, :], in_=gw_b[jp * 128:(jp + 1) * 128, :]),
          writes=K(f"GWF{gb}"), sem=f"gw{gb}", inc=16)
        for hf in range(2):
            E("scalar", lambda S, gb=gb, hf=hf: S.activation(
                out=U1[:, :], in_=GWF[gb][:, hf * 2048:(hf + 1) * 2048],
                func=AF.Sign, scale=SCAL[:, 2:3], bias=HLF[:, 1:2]),
              reads=K(f"GWF{gb}", "SCAL", "HLF"), writes=K("U1"))
            E("scalar", lambda S, gb=gb, hf=hf: S.activation(
                out=U2[:, :], in_=GWF[gb][:, hf * 2048:(hf + 1) * 2048],
                func=AF.Sign, scale=SCAL[:, 2:3], bias=HLF[:, 0:1]),
              reads=K(f"GWF{gb}", "SCAL", "HLF"), writes=K("U2"))
            E("vector", lambda V, gb=gb, hf=hf: V.tensor_tensor(
                out=WQ16[gb][:, hf * 2048:(hf + 1) * 2048], in0=U1[:, :], in1=U2[:, :],
                op=OP.add),
              reads=K("U1", "U2"), writes=K(f"WQ16_{gb}"))

        def mm_gate(TE, gb=gb, psg=psg, psu=psu):
            for dc in range(NDC):
                TE.ldweights(WQ16[gb][:, dc * 256: dc * 256 + 128])
                for j in range(2):
                    TE.matmul(out=psg[:, j * 512:(j + 1) * 512],
                              lhsT=WQ16[gb][:, dc * 256: dc * 256 + 128],
                              rhs=XQ[:, dc * TO + j * 512: dc * TO + (j + 1) * 512],
                              start=(dc == 0), stop=(dc == NDC - 1))
                TE.ldweights(WQ16[gb][:, dc * 256 + 128:(dc + 1) * 256])
                for j in range(2):
                    i = TE.matmul(out=psu[:, j * 512:(j + 1) * 512],
                                  lhsT=WQ16[gb][:, dc * 256 + 128:(dc + 1) * 256],
                                  rhs=XQ[:, dc * TO + j * 512: dc * TO + (j + 1) * 512],
                                  start=(dc == 0), stop=(dc == NDC - 1))
            return i
        E("tensor", mm_gate, reads=K(f"WQ16_{gb}", "XQ"), writes=[psgn, psun])
        E("vector", lambda V, psg=psg: V.tensor_tensor(out=EV1[:, :], in0=psg[:, 0:TO],
                                                       in1=SCG[:, :], op=OP.mult),
          reads=[psgn] + K("SCG"), writes=K("EV1"))
        E("scalar", lambda S: S.activation(out=EV2[:, :], in_=EV1[:, :], func=AF.Silu),
          reads=K("EV1"), writes=K("EV2"))
        E("vector", lambda V, psu=psu, ev=ev: V.tensor_tensor(out=ev[:, :], in0=EV2[:, :],
                                                              in1=psu[:, 0:TO], op=OP.mult),
          reads=[psun] + K("EV2"), writes=K(evn))
        E("gpsimd", lambda G, jp=jp, ev=ev: G.dma_start(out=hw_tile(jp), in_=ev[:, :]),
          reads=K(evn), writes=["h_scr"], sem=f"hw{jp % 2}", inc=16)
        E("scalar", lambda S, jp=jp, ev=ev: S.activation(out=YH2p[:, :], in_=ev[:, :],
                                                         func=AF.Abs,
                                                         scale=g2s[:, jp:jp + 1]),
          reads=K(evn, "g2s"), writes=K("YH2p"))
        if jp == 0:
            E("vector", lambda V: V.tensor_copy(out=MMH[:, :], in_=YH2p[:, :]),
              reads=K("YH2p"), writes=K("MMH"))
        else:
            E("vector", lambda V: V.tensor_tensor(out=MMH[:, :], in0=MMH[:, :],
                                                  in1=YH2p[:, :], op=OP.max),
              reads=K("YH2p", "MMH"), writes=K("MMH"))
        if jp == 5:
            tapk("yh2p5", YH2p, ["YH2p"])
            tapk("ev5", ev, [evn])
        E("scalar", lambda S, ev=ev: S.activation(out=SQF[:, :], in_=ev[:, :],
                                                  func=AF.Square),
          reads=K(evn), writes=K("SQF"))
        if jp == 0:
            E("vector", lambda V: V.tensor_copy(out=HSQA[:, :], in_=SQF[:, :]),
              reads=K("SQF"), writes=K("HSQA"))
        else:
            E("vector", lambda V: V.tensor_tensor(out=HSQA[:, :], in0=HSQA[:, :],
                                                  in1=SQF[:, :], op=OP.add),
              reads=K("SQF", "HSQA"), writes=K("HSQA"))

    def mm_hsum(TE):
        for j in range(2):
            i = TE.matmul(out=PS[3][:, j * 512:(j + 1) * 512], lhsT=ONESF[:, :],
                          rhs=HSQA[:, j * 512:(j + 1) * 512], start=True, stop=True)
        return i
    E("tensor", mm_hsum, reads=K("HSQA", "ONESF"), writes=["PS3"])
    E("vector", lambda V: V.tensor_scalar(out=HS1[:, :], in0=PS[3][:, 0:TO],
                                          scalar1=1.0 / INTER, scalar2=None, op0=OP.mult),
      reads=["PS3"], writes=K("HS1"))
    E("vector", lambda V: V.tensor_tensor(out=HS2[:, :], in0=SCG[:, :], in1=SCG[:, :],
                                          op=OP.mult),
      reads=K("SCG"), writes=K("HS2"))
    E("vector", lambda V: V.tensor_tensor(out=HS1[:, :], in0=HS1[:, :], in1=HS2[:, :],
                                          op=OP.mult),
      reads=K("HS2", "HS1"), writes=K("HS1"))
    E("scalar", lambda S: S.activation(out=HS2[:, :], in_=HS1[:, :], func=AF.Ln,
                                       bias=epsv[:, 0:1]),
      reads=K("HS1", "epsv"), writes=K("HS2"))
    E("scalar", lambda S: S.activation(out=HS1[:, :], in_=HS2[:, :], func=AF.Exp, scale=-0.5),
      reads=K("HS2"), writes=K("HS1"))
    compact_max(MMH, ["MMH"], MCOMP, "MCOMP")
    bcast_cols(MCOMP, "MCOMP", HS2, ["HS2"], 0)
    E("vector", lambda V: V.tensor_tensor(out=YH1[:, :], in0=SCG[:, :], in1=HS2[:, :],
                                          op=OP.mult),
      reads=K("SCG", "HS2"), writes=K("YH1"))
    E("vector", lambda V: V.tensor_tensor(out=YH1[:, :], in0=YH1[:, :], in1=HS1[:, :],
                                          op=OP.mult),
      reads=K("HS1", "YH1"), writes=K("YH1"))
    E("vector", lambda V: V.tensor_scalar(out=YH2[:, :], in0=YH1[:, :], scalar1=EPS,
                                          scalar2=None, op0=OP.max),
      reads=K("YH1"), writes=K("YH2"))
    E("vector", lambda V: V.reciprocal(out=YH1[:, :], in_=YH2[:, :]),
      reads=K("YH2"), writes=K("YH1"))
    E("vector", lambda V: V.tensor_tensor(out=CCH[:, :], in0=HS1[:, :], in1=YH1[:, :],
                                          op=OP.mult),
      reads=K("HS1", "YH1"), writes=K("CCH"))
    E("vector", lambda V: V.tensor_tensor(out=CCH[:, :], in0=CCH[:, :], in1=SCG[:, :],
                                          op=OP.mult),
      reads=K("SCG", "CCH"), writes=K("CCH"))
    E("vector", lambda V: V.tensor_scalar(out=CCH[:, :], in0=CCH[:, :], scalar1=127.0,
                                          scalar2=None, op0=OP.mult),
      reads=K("CCH"), writes=K("CCH"))
    E("vector", lambda V: V.tensor_scalar(out=SCD[:, :], in0=YH2[:, :], scalar1=SCAL[:, 1:2],
                                          scalar2=0.5 / 127.0, op0=OP.mult, op1=OP.mult),
      reads=K("YH2", "SCAL"), writes=K("SCD"))
    tapk("cch", CCH, ["CCH"])
    tapk("scd", SCD, ["SCD"])


    pb_units = []

    def _pbA(jp):
        E("sync", lambda S, jp=jp: S.dma_start(out=HB[:, :], in_=hw_tile(jp)),
          reads=["h_scr"], writes=K("HB"), sem="hr0", inc=16)
        E("vector", lambda V, jp=jp: V.scalar_tensor_tensor(
            out=LYH1[:, :], in0=HB[:, :], scalar=g2s[:, jp:jp + 1], in1=CCH[:, :],
            op0=OP.mult, op1=OP.mult),
          reads=K("HB", "g2s", "CCH"), writes=K("LYH1"))
        E("vector", lambda V: V.tensor_copy(out=LQD[:, :], in_=LYH1[:, :]),
          reads=K("LYH1"), writes=K("LQD"))

    def _pbS(i):
        m = (64.0, 32.0, 16.0, 8.0, 4.0, 2.0, 1.0)[i]
        E("scalar", lambda S, i=i: S.activation(out=LQC[:, :], in_=LQD[:, :],
                                                func=AF.Sign,
                                                bias=LADB[:, 2 * i + 1:2 * i + 2]),
          reads=K("LQD", "LADB"), writes=K("LQC"))
        E("scalar", lambda S, i=i: S.activation(out=LQF[:, :], in_=LQD[:, :],
                                                func=AF.Sign,
                                                bias=LADB[:, 2 * i:2 * i + 1]),
          reads=K("LQD", "LADB"), writes=K("LQF"))
        E("vector", lambda V: V.tensor_tensor(out=LQC[:, :], in0=LQC[:, :],
                                              in1=LQF[:, :], op=OP.add),
          reads=K("LQC", "LQF"), writes=K("LQC"))
        E("vector", lambda V, m=m: V.scalar_tensor_tensor(
            out=LQD[:, :], in0=LQC[:, :], scalar=-0.5 * m, in1=LQD[:, :],
            op0=OP.mult, op1=OP.add),
          reads=K("LQC", "LQD"), writes=K("LQD"))

    def _pbC(jp):
        b = jp % 2
        xt = XQHT[b]
        xtn = f"XQHT{b}"
        E("vector", lambda V: V.tensor_tensor(out=LQF[:, :], in0=LYH1[:, :], in1=LQD[:, :],
                                              op=OP.subtract),
          reads=K("LYH1", "LQD"), writes=K("LQF"))
        E("vector", lambda V, xt=xt: V.tensor_scalar(out=xt[:, :], in0=LQF[:, :],
                                                     scalar1=-128.0, scalar2=127.0,
                                                     op0=OP.max, op1=OP.min),
          reads=K("LQF"), writes=K(xtn))
        E("gpsimd", lambda G, jp=jp, xt=xt: G.dma_start(
            out=xqh_scr[jp * 128:(jp + 1) * 128, :], in_=xt[:, :]),
          reads=K(xtn), writes=["xqh_scr"], sem=f"xqo{b}", inc=16)

    for jp in range(NHC):
        pb_units.append(lambda jp=jp: _pbA(jp))
        for i in range(7):
            pb_units.append(lambda i=i: _pbS(i))
        pb_units.append(lambda jp=jp: _pbC(jp))
    pb_state = [0]

    def pb_pump(n):
        while pb_state[0] < len(pb_units) and n > 0:
            pb_units[pb_state[0]]()
            pb_state[0] += 1
            n -= 1

    for ci, (dst, srcp, nm) in enumerate([(cosT, cosTin, "cosT"), (sinT, sinTin, "sinT")]):
        E("sync", lambda S, d=dst, s=srcp: S.dma_start(out=d[:, :], in_=s),
          writes=K(nm), sem=f"ct{ci}", inc=16)

    for g in range(H // 2):
        gb = g % 2
        E("gpsimd", lambda G, g=g, gb=gb: G.dma_start(
            out=WVG[gb][:, :], in_=wv_b[g * 128:(g + 1) * 128, :]),
          writes=K(f"WVG{gb}"), sem=f"wv{gb}", inc=16)
        for tt in range(16):
            psn = 2 + (tt % 2)

            def mm_vt(TE, tt=tt, gb=gb, psn=psn):
                for dc in range(NDC):
                    TE.ldweights(XN16[:, dc * T + tt * 128: dc * T + (tt + 1) * 128])
                    i = TE.matmul(out=PS[psn][:, 0:256],
                                  lhsT=XN16[:, dc * T + tt * 128: dc * T + (tt + 1) * 128],
                                  rhs=WVG[gb][:, dc * 256:(dc + 1) * 256],
                                  start=(dc == 0), stop=(dc == NDC - 1))
                return i
            E("tensor", mm_vt, reads=K("XN16", f"WVG{gb}"), writes=[f"PS{psn}"])
            E("vector", lambda V, tt=tt, psn=psn: V.tensor_copy(
                out=VT16[:, tt * 128:(tt + 1) * 128], in_=PS[psn][:, 0:128]),
              reads=[f"PS{psn}"], writes=K("VT16"))
            E("vector", lambda V, tt=tt, psn=psn: V.tensor_copy(
                out=VT16[:, T + tt * 128: T + (tt + 1) * 128], in_=PS[psn][:, 128:256]),
              reads=[f"PS{psn}"], writes=K("VT16"))
            pb_pump(1)

        for hh in range(2):
            head = 2 * g + hh
            E("gpsimd", lambda G, head=head: G.dma_start(
                out=WQK[0][:, :], in_=wqkv_b[head * 128:(head + 1) * 128, :]),
              writes=K("WQK0"), sem="wq0", inc=16)
            E("gpsimd", lambda G, head=head: G.dma_start(
                out=WQK[1][:, :],
                in_=wqkv_b[(16 + head) * 128:(16 + head + 1) * 128, :]),
              writes=K("WQK1"), sem="wk0", inc=16)

            def mm_q(TE):
                for dc in range(NDC):
                    TE.ldweights(WQK[0][:, dc * 128:(dc + 1) * 128])
                    for j in range(2):
                        i = TE.matmul(out=PS[0][:, j * 512:(j + 1) * 512],
                                      lhsT=WQK[0][:, dc * 128:(dc + 1) * 128],
                                      rhs=XN16[:, dc * T + j * 512: dc * T + (j + 1) * 512],
                                      start=(dc == 0), stop=(dc == NDC - 1))
                return i
            E("tensor", mm_q, reads=K("WQK0", "XN16"), writes=["PS0"])

            def mm_k(TE):
                for dc in range(NDC):
                    TE.ldweights(WQK[1][:, dc * 128:(dc + 1) * 128])
                    for half in range(2):
                        for j in range(2):
                            i = TE.matmul(
                                out=PS[1 + half][:, j * 512:(j + 1) * 512],
                                lhsT=WQK[1][:, dc * 128:(dc + 1) * 128],
                                rhs=XN16[:, dc * T + half * TO + j * 512:
                                         dc * T + half * TO + (j + 1) * 512],
                                start=(dc == 0), stop=(dc == NDC - 1))
                return i
            E("tensor", mm_k, reads=K("WQK1", "XN16"), writes=["PS1", "PS2"])

            def rope_half(psn, dst, dstnm, dst_c0, n, tab_c0):
                E("vector", lambda V: V.tensor_copy(out=RTa[:, 0:n], in_=PS[psn][0:64, 0:n]),
                  reads=[f"PS{psn}"], writes=K("RTa"))
                E("vector", lambda V: V.tensor_copy(out=RTb[:, 0:n], in_=PS[psn][64:128, 0:n]),
                  reads=[f"PS{psn}"], writes=K("RTb"))
                E("vector", lambda V: V.tensor_tensor(
                    out=RTc[:, 0:n], in0=RTa[:, 0:n],
                    in1=cosT[0:64, tab_c0:tab_c0 + n], op=OP.mult),
                  reads=K("RTa", "cosT"), writes=K("RTc"))
                E("vector", lambda V: V.tensor_tensor(
                    out=RTd[:, 0:n], in0=RTb[:, 0:n],
                    in1=sinT[0:64, tab_c0:tab_c0 + n], op=OP.mult),
                  reads=K("RTb", "sinT"), writes=K("RTd"))
                E("vector", lambda V: V.tensor_tensor(
                    out=dst[0:64, dst_c0:dst_c0 + n], in0=RTc[:, 0:n], in1=RTd[:, 0:n],
                    op=OP.subtract),
                  reads=K("RTc", "RTd"), writes=K(dstnm))
                E("vector", lambda V: V.tensor_tensor(
                    out=RTc[:, 0:n], in0=RTb[:, 0:n],
                    in1=cosT[0:64, tab_c0:tab_c0 + n], op=OP.mult),
                  reads=K("RTb", "cosT"), writes=K("RTc"))
                E("vector", lambda V: V.tensor_tensor(
                    out=RTd[:, 0:n], in0=RTa[:, 0:n],
                    in1=sinT[0:64, tab_c0:tab_c0 + n], op=OP.mult),
                  reads=K("RTa", "sinT"), writes=K("RTd"))
                E("vector", lambda V: V.tensor_tensor(
                    out=dst[64:128, dst_c0:dst_c0 + n], in0=RTc[:, 0:n], in1=RTd[:, 0:n],
                    op=OP.add),
                  reads=K("RTc", "RTd"), writes=K(dstnm))

            rope_half(0, Q16, "Q16", 0, TO, 0)
            rope_half(1, K16, "K16", 0, TO, 0)
            rope_half(2, K16, "K16", TO, TO, TO)

            E("scalar", lambda S: S.activation(out=SQH[:, 0:TO], in_=Q16[:, :], func=AF.Square),
              reads=K("Q16"), writes=K("SQH"))

            def mm_mq(TE):
                for j in range(2):
                    TE.ldweights(ones16[:, :])
                    i = TE.matmul(out=PS[3][:, j * 512:(j + 1) * 512], lhsT=ones16[:, :],
                                  rhs=SQH[:, j * 512:(j + 1) * 512], start=True, stop=True)
                return i
            E("tensor", mm_mq, reads=K("SQH", "ones16"), writes=["PS3"])
            E("vector", lambda V: V.reduce_max(out=MQ[:, :], in_=PS[3][:, 0:TO], axis=AX.X),
              reads=["PS3"], writes=K("MQ"))
            E("scalar", lambda S: S.activation(out=SQH[:, :], in_=K16[:, :], func=AF.Square),
              reads=K("K16"), writes=K("SQH"))

            def mm_mk(TE):
                for half in range(2):
                    for j in range(2):
                        TE.ldweights(ones16[:, :])
                        i = TE.matmul(out=PS[3][:, j * 512:(j + 1) * 512], lhsT=ones16[:, :],
                                      rhs=SQH[:, half * TO + j * 512: half * TO + (j + 1) * 512],
                                      start=(half == 0), stop=(half == 1))
                return i
            E("tensor", mm_mk, reads=K("SQH", "ones16"), writes=["PS3"])
            E("vector", lambda V: V.reduce_max(out=MK[:, :], in_=PS[3][:, 0:TO], axis=AX.X),
              reads=["PS3"], writes=K("MK"))
            E("vector", lambda V: V.tensor_tensor(out=MQ[:, :], in0=MQ[:, :], in1=MK[:, :],
                                                  op=OP.mult),
              reads=K("MK", "MQ"), writes=K("MQ"))
            E("scalar", lambda S: S.activation(out=MK[:, :], in_=MQ[:, :], func=AF.Sqrt),
              reads=K("MQ"), writes=K("MK"))
            E("vector", lambda V: V.tensor_scalar(out=NEGM[:, :], in0=MK[:, :], scalar1=-RS,
                                                  scalar2=None, op0=OP.mult),
              reads=K("MK"), writes=K("NEGM"))

            def mm_s(TE, kt):
                TE.ldweights(K16[:, kt * 128:(kt + 1) * 128])
                for j in range(2):
                    i = TE.matmul(out=PS[1 + (kt % 2)][:, j * 512:(j + 1) * 512],
                                  lhsT=K16[:, kt * 128:(kt + 1) * 128],
                                  rhs=Q16[:, j * 512:(j + 1) * 512], start=True, stop=True)
                return i
            E("tensor", lambda TE: mm_s(TE, 0), reads=K("K16", "Q16"), writes=["PS1"])
            for kt in range(16):
                psn = 1 + (kt % 2)
                pb = PB[kt % 2]
                pbn = f"PB{kt % 2}"
                E("scalar", lambda S, psn=psn, pb=pb: S.activation(
                    out=pb[:, :], in_=PS[psn][:, 0:TO], func=AF.Exp, bias=NEGM[:, 0:1], scale=RS),
                  reads=[f"PS{psn}"] + K("NEGM"), writes=K(pbn))
                if kt < 15:
                    E("tensor", lambda TE, kt=kt: mm_s(TE, kt + 1), reads=K("K16", "Q16"),
                      writes=[f"PS{1 + ((kt + 1) % 2)}"])

                def mm_sum(TE, kt=kt, pb=pb):
                    TE.ldweights(ones16[:, :])
                    for j in range(2):
                        i = TE.matmul(out=PS[3][:, j * 512:(j + 1) * 512], lhsT=ones16[:, :],
                                      rhs=pb[:, j * 512:(j + 1) * 512],
                                      start=(kt == 0), stop=(kt == 15))
                    return i
                E("tensor", mm_sum, reads=K(pbn, "ones16"), writes=["PS3"])

                def mm_o(TE, kt=kt, hh=hh, pb=pb):
                    TE.ldweights(VT16[:, hh * T + kt * 128: hh * T + (kt + 1) * 128])
                    for j in range(2):
                        i = TE.matmul(out=PS[0][:, j * 512:(j + 1) * 512],
                                      lhsT=VT16[:, hh * T + kt * 128: hh * T + (kt + 1) * 128],
                                      rhs=pb[:, j * 512:(j + 1) * 512],
                                      start=(kt == 0), stop=(kt == 15))
                    return i
                E("tensor", mm_o, reads=K(pbn, "VT16"), writes=["PS0"])
                pb_pump(1)

            E("vector", lambda V: V.reciprocal(out=RECB[:, :], in_=PS[3][:, 0:TO]),
              reads=["PS3"], writes=K("RECB"))
            E("vector", lambda V, head=head: V.tensor_tensor(
                out=O16[:, head * TO:(head + 1) * TO], in0=PS[0][:, 0:TO], in1=RECB[:, :],
                op=OP.mult),
              reads=["PS0"] + K("RECB"), writes=K("O16"))
            pb_pump(2)
    tapk("o16", O16, ["O16"])

    pb_pump(10 ** 9)

    for dco in range(NDC):
        wb = dco % 2
        E("gpsimd", lambda G, dc=dco, wb=wb: G.dma_start(
            out=WOB[wb][:, :], in_=wout_b[dc * 128:(dc + 1) * 128, :]),
          writes=K(f"WOB{wb}"), sem=f"wo{wb}", inc=16)

        def mm_sp(TE, wb=wb):
            for oc in range(NDC):
                TE.ldweights(WOB[wb][:, oc * 128:(oc + 1) * 128])
                for j in range(2):
                    i = TE.matmul(out=PS[0][:, j * 512:(j + 1) * 512],
                                  lhsT=WOB[wb][:, oc * 128:(oc + 1) * 128],
                                  rhs=O16[:, oc * TO + j * 512: oc * TO + (j + 1) * 512],
                                  start=(oc == 0), stop=(oc == NDC - 1))
            return i
        E("tensor", mm_sp, reads=K(f"WOB{wb}", "O16"), writes=["PS0"])
        E("vector", lambda V, dc=dco: V.tensor_copy(out=S16[:, dc * TO:(dc + 1) * TO],
                                                    in_=PS[0][:, 0:TO]),
          reads=["PS0"], writes=K("S16"))
    tapk("s16", S16, ["S16"])

    for g in range(4):
        for hc in range(NHC):
            b = hc % 2
            E("gpsimd", lambda G, hc=hc, b=b: G.dma_start(
                out=XQHR[b][:, :], in_=xqh_scr[hc * 128:(hc + 1) * 128, :]),
              reads=["xqh_scr"], writes=K(f"XQHR{b}"), sem=f"hq{b}", inc=16)
            E("sync", lambda S, g=g, hc=hc, b=b: S.dma_start(
                out=DWF[b][:, :], in_=dwT[hc * 128:(hc + 1) * 128, g * 512:(g + 1) * 512]),
              writes=K(f"DWF{b}"), sem=f"dq{b}", inc=16)
            E("scalar", lambda S, b=b: S.activation(out=U1D[:, :], in_=DWF[b][:, :],
                                                    func=AF.Sign, scale=SCAL[:, 3:4],
                                                    bias=HLF[:, 1:2]),
              reads=K(f"DWF{b}", "SCAL", "HLF"), writes=K("U1D"))
            E("scalar", lambda S, b=b: S.activation(out=U2D[:, :], in_=DWF[b][:, :],
                                                    func=AF.Sign, scale=SCAL[:, 3:4],
                                                    bias=HLF[:, 0:1]),
              reads=K(f"DWF{b}", "SCAL", "HLF"), writes=K("U2D"))
            E("vector", lambda V, b=b: V.tensor_tensor(out=WQDh[b][:, :], in0=U1D[:, :],
                                                       in1=U2D[:, :], op=OP.add),
              reads=K("U1D", "U2D"), writes=K(f"WQDh{b}"))

            def mm_dn(TE, hc=hc, b=b):
                for dcl in range(4):
                    TE.ldweights(WQDh[b][:, dcl * 128:(dcl + 1) * 128])
                    for j in range(2):
                        i = TE.matmul(out=PS[dcl][:, j * 512:(j + 1) * 512],
                                      lhsT=WQDh[b][:, dcl * 128:(dcl + 1) * 128],
                                      rhs=XQHR[b][:, j * 512:(j + 1) * 512],
                                      start=(hc == 0), stop=(hc == NHC - 1))
                return i
            E("tensor", mm_dn, reads=K(f"WQDh{b}", f"XQHR{b}"),
              writes=["PS0", "PS1", "PS2", "PS3"])
        for dcl in range(4):
            dc = 4 * g + dcl
            E("sync", lambda S, dc=dc: S.dma_start(out=XR[:, :], in_=xchunk(dc)[:, 0:TO]),
              writes=K("XR"), sem="xr", inc=16)
            E("vector", lambda V, dcl=dcl: V.tensor_tensor(out=OBA[:, :],
                                                           in0=PS[dcl][:, 0:TO],
                                                           in1=SCD[:, :], op=OP.mult),
              reads=[f"PS{dcl}"] + K("SCD"), writes=K("OBA"))
            E("scalar", lambda S: S.activation(out=OBB[:, :], in_=OBA[:, :], func=AF.Silu),
              reads=K("OBA"), writes=K("OBB"))
            E("gpsimd", lambda G, dc=dc: G.tensor_tensor(out=OBB[:, :], in0=OBB[:, :],
                                                         in1=S16[:, dc * TO:(dc + 1) * TO],
                                                         op=OP.mult),
              reads=K("S16", "OBB"), writes=K("OBB"))
            E("vector", lambda V: V.tensor_tensor(out=OBA[:, :], in0=OBB[:, :], in1=XR[:, :],
                                                  op=OP.add),
              reads=K("XR", "OBB"), writes=K("OBA"))
            E("gpsimd", lambda G, dc=dc: G.dma_start(
                out=outT[dc * 128:(dc + 1) * 128, :], in_=OBA[:, :]),
              reads=K("OBA"), sem=f"ost{dc % 2}", inc=16)

    p.final_wait("gpsimd", "ost0", 16 * (NDC // 2))
    p.final_wait("gpsimd", "ost1", 16 * (NDC // 2))
    p.realize()
    return nc, taps


_BUILD_CACHE = {}


def _get_nc():
    key = tuple(sorted(DEBUG_TAPS))
    if key not in _BUILD_CACHE:
        _BUILD_CACHE[key] = build()
    return _BUILD_CACHE[key]


def stage_inputs(L0_attn_x, L0_attn_w_qkv, L0_attn_w_out, L0_ffn_gate_w, L0_ffn_down_w,
                 L0_ffn_gate_norm_w, L0_ffn_down_norm_w):
    x = np.asarray(L0_attn_x, np.float32)
    wqkvT = np.asarray(L0_attn_w_qkv, np.float32).T
    wqkv_b = np.ascontiguousarray(
        wqkvT[:, :2 * D].reshape(NDC, 128, 32, 128).transpose(2, 1, 0, 3).reshape(32 * 128, NDC * 128))
    wv_b = np.ascontiguousarray(
        wqkvT[:, 2 * D:].reshape(NDC, 128, 8, 256).transpose(2, 1, 0, 3).reshape(8 * 128, NDC * 256))
    woutT = np.asarray(L0_attn_w_out, np.float32).T
    wout_b = np.ascontiguousarray(
        woutT.reshape(NDC, 128, NDC, 128).transpose(2, 1, 0, 3).reshape(NDC * 128, NDC * 128))
    gwT = np.asarray(L0_ffn_gate_w, np.float32).T
    idx = []
    for j in range(NHC):
        idx.extend(range(j * 128, (j + 1) * 128))
        idx.extend(range(INTER + j * 128, INTER + (j + 1) * 128))
    gwTp = gwT[:, idx]
    gw_b = np.ascontiguousarray(
        gwTp.reshape(NDC, 128, NHC, 256).transpose(2, 1, 0, 3).reshape(NHC * 128, NDC * 256))
    dwTc = np.ascontiguousarray(np.asarray(L0_ffn_down_w, np.float32).T)
    g1t = np.ascontiguousarray(np.asarray(L0_ffn_gate_norm_w, np.float32).reshape(NDC, 128).T)
    g2t = np.ascontiguousarray(np.asarray(L0_ffn_down_norm_w, np.float32).reshape(NHC, 128).T)
    onesb16 = np.ones((128, 128), ml_dtypes.bfloat16)
    identf = np.eye(128, dtype=np.float32)
    cvec = np.tile(np.array([[EPS, 0.0]], np.float32), (128, 1))
    cosE, sinE = _rope_tables_f32()
    in_maps = []
    for c in range(8):
        pair, half = c // 2, c % 2
        xp = x[pair].T
        perm = np.concatenate([np.arange(half * TO, (half + 1) * TO),
                               np.arange((1 - half) * TO, (2 - half) * TO)])
        in_maps.append(dict(
            xT=np.ascontiguousarray(xp[:, perm]),
            wqkv_b=wqkv_b, wv_b=wv_b, wout_b=wout_b, gw_b=gw_b, dwT=dwTc, g1t=g1t, g2t=g2t,
            onesb16=onesb16, identf=identf,
            cosTin=np.ascontiguousarray(cosE[perm].T),
            sinTin=np.ascontiguousarray(sinE[perm].T),
            cvec=cvec,
        ))
    return in_maps


def kernel(L0_attn_x, L0_attn_w_qkv, L0_attn_w_out, L0_ffn_gate_w, L0_ffn_down_w,
           L0_ffn_gate_norm_w, L0_ffn_down_norm_w):
    in_maps = stage_inputs(L0_attn_x, L0_attn_w_qkv, L0_attn_w_out, L0_ffn_gate_w,
                           L0_ffn_down_w, L0_ffn_gate_norm_w, L0_ffn_down_norm_w)
    nc, taps = _get_nc()
    res = run_bass_kernel_spmd(nc, in_maps, core_ids=list(range(8)))
    out = np.empty((4, T, D), np.float32)
    for c in range(8):
        pair, half = c // 2, c % 2
        out[pair, half * TO:(half + 1) * TO, :] = np.asarray(res.results[c]["outT"]).T
    if DEBUG_TAPS:
        kernel.tap_results = [
            {("tap_" + n): np.asarray(r.get("tap_" + n)) for n in taps} for r in res.results]
    return out


# version 16
# speedup vs baseline: 1.5144x; 1.4282x over previous
import math
import numpy as np
import ml_dtypes
import concourse.bass as bass
import concourse.mybir as mybir
from concourse.bass_utils import run_bass_kernel_spmd

F32 = mybir.dt.float32
BF16 = mybir.dt.bfloat16
I32 = mybir.dt.int32
AF = mybir.ActivationFunctionType
OP = mybir.AluOpType
AX = mybir.AxisListType

D = 2048
T = 2048
TO = 1024
H = 16
INTER = 5632
NDC = D // 128
NHC = INTER // 128
EPS = 1e-5
RS = 1.0 / math.sqrt(128.0)
LN1E4 = math.log(10000.0)
TWO_PI = 2.0 * math.pi
C1 = 6.28125
C2 = TWO_PI - C1
PI_F32 = float(np.float32(np.pi))
PI_LO = float(np.nextafter(np.float32(np.pi), np.float32(0.0)))

ENGINES = ("sync", "vector", "scalar", "tensor", "gpsimd")
ESEM = {"vector": "dv", "scalar": "ac", "tensor": "pe", "gpsimd": "gp", "sync": "sp"}

DEBUG_TAPS = set()

_INV_FREQ_BITS = np.array([1065353216,1063104471,1061157137,1059470816,1058010522,1056527311,1054337179,1052440603,1050798235,1049376004,1047712803,1045579757,1043732615,1042133056,1040747895,1038909398,1036831949,1035032952,1033475085,1032126029,1030116802,1028093503,1026341397,1024824136,1023510243,1021334738,1019364175,1017657739,1016180025,1014779187,1012562930,1010643729,1008981770,1007542573,1005959600,1003801109,1001931932,1000313292,998911607,997151246,995049016,993228559,991652109,990286959,988353832,986306396,984533390,982998030,981668463,979567074,977573004,975846212,974350870,973033388,970790693,968848599,967166816,965710450,964208661,962024420,960132947,958494998,957076594,955395298], dtype=np.uint32)


_TABLE_CACHE = {}


def _rope_tables_f32():
    if "t" in _TABLE_CACHE:
        return _TABLE_CACHE["t"]
    import os
    import subprocess
    import sys
    import tempfile
    inv = _INV_FREQ_BITS.view(np.float32)
    t = np.arange(T, dtype=np.float32)
    fr = np.einsum("i,j->ij", t, inv).astype(np.float32)
    emb = np.concatenate([fr, fr], axis=-1)
    cos = sin = None
    try:
        with tempfile.TemporaryDirectory() as td:
            np.save(os.path.join(td, "emb.npy"), emb)
            prog = ("import numpy as np, jax.numpy as jnp, sys, os;"
                    "td = sys.argv[1];"
                    "e = jnp.asarray(np.load(os.path.join(td, 'emb.npy')));"
                    "np.save(os.path.join(td, 'cos.npy'), np.asarray(jnp.cos(e)));"
                    "np.save(os.path.join(td, 'sin.npy'), np.asarray(jnp.sin(e)));"
                    "print('TBLOK')")
            env = {k: v for k, v in os.environ.items() if k != "PYTHONPATH"}
            env["JAX_PLATFORMS"] = "cpu"
            r = subprocess.run([sys.executable, "-c", prog, td], env=env,
                               capture_output=True, text=True, timeout=600)
            if "TBLOK" in r.stdout:
                cos = np.load(os.path.join(td, "cos.npy")).astype(np.float32)
                sin = np.load(os.path.join(td, "sin.npy")).astype(np.float32)
    except Exception:
        cos = None
    if cos is None:
        cos = np.cos(emb.astype(np.float64)).astype(np.float32)
        sin = np.sin(emb.astype(np.float64)).astype(np.float32)
    _TABLE_CACHE["t"] = (cos, sin)
    return cos, sin


class Prog:
    def __init__(self, nc, ctx):
        self.nc = nc
        self.ctx = ctx
        self.ops = {e: [] for e in ENGINES}
        self.sem_handles = {}
        self.counts = {}
        self.writers = {}
        self.readers = {}
        self.waited = {e: {} for e in ENGINES}

    def sem(self, name):
        if name not in self.sem_handles:
            self.sem_handles[name] = self.ctx.enter_context(self.nc.semaphore(name))
            self.counts[name] = 0
        return self.sem_handles[name]

    def _wait(self, eng, sname, val):
        if val <= 0 or self.waited[eng].get(sname, 0) >= val:
            return
        self.waited[eng][sname] = val
        h = self.sem_handles[sname]
        self.ops[eng].append(lambda E, h=h, v=val: E.wait_ge(h, v))

    def emit(self, eng, fn, reads=(), writes=(), sem=None, inc=1):
        for k in reads:
            for s, c in self.writers.get(k, {}).items():
                self._wait(eng, s, c)
        for k in writes:
            for s, c in self.writers.get(k, {}).items():
                self._wait(eng, s, c)
            for s, c in self.readers.get(k, {}).items():
                self._wait(eng, s, c)
        sname = sem or ESEM[eng]
        self.sem(sname)
        self.counts[sname] += inc
        cnt = self.counts[sname]
        h = self.sem_handles[sname]

        def run(E, fn=fn, h=h, inc=inc):
            fn(E).then_inc(h, inc)
        self.ops[eng].append(run)
        for k in writes:
            self.writers.setdefault(k, {})[sname] = cnt
            self.readers.pop(k, None)
        for k in reads:
            self.readers.setdefault(k, {})[sname] = cnt
        return sname, cnt

    def final_wait(self, eng, sname, val):
        h = self.sem(sname)
        self.ops[eng].append(lambda E, h=h, v=val: E.wait_ge(h, v))

    def realize(self):
        with self.nc.Block() as block:
            @block.sync
            def _(sync):
                for f in self.ops["sync"]:
                    f(sync)

            @block.vector
            def _(vector):
                for f in self.ops["vector"]:
                    f(vector)

            @block.scalar
            def _(scalar):
                for f in self.ops["scalar"]:
                    f(scalar)

            @block.tensor
            def _(tensor):
                for f in self.ops["tensor"]:
                    f(tensor)

            @block.gpsimd
            def _(gpsimd):
                for f in self.ops["gpsimd"]:
                    f(gpsimd)


def build():
    import contextlib
    nc = bass.Bass("TRN2", target_bir_lowering=False, debug=False)
    ctx = contextlib.ExitStack()
    p = Prog(nc, ctx)
    E = p.emit

    xT = nc.dram_tensor("xT", [D, T], F32, kind="ExternalInput").ap()
    wqkv_b = nc.dram_tensor("wqkv_b", [32 * 128, NDC * 128], F32, kind="ExternalInput").ap()
    wv_b = nc.dram_tensor("wv_b", [8 * 128, NDC * 256], F32, kind="ExternalInput").ap()
    wout_b = nc.dram_tensor("wout_b", [NDC * 128, NDC * 128], F32, kind="ExternalInput").ap()
    gw_b = nc.dram_tensor("gw_b", [NHC * 128, NDC * 256], F32, kind="ExternalInput").ap()
    dwA = nc.dram_tensor("dwA", [128, NDC * NHC * 128], F32, kind="ExternalInput").ap()
    g1t = nc.dram_tensor("g1t", [128, NDC], F32, kind="ExternalInput").ap()
    g2t = nc.dram_tensor("g2t", [128, NHC], F32, kind="ExternalInput").ap()
    onesb16 = nc.dram_tensor("onesb16", [128, 128], BF16, kind="ExternalInput").ap()
    identf = nc.dram_tensor("identf", [128, 128], F32, kind="ExternalInput").ap()
    cosTin = nc.dram_tensor("cosTin", [128, T], F32, kind="ExternalInput").ap()
    sinTin = nc.dram_tensor("sinTin", [128, T], F32, kind="ExternalInput").ap()
    cvec = nc.dram_tensor("cvec", [128, 2], F32, kind="ExternalInput").ap()

    outT = nc.dram_tensor("outT", [D, TO], F32, kind="ExternalOutput").ap()
    h_scr = nc.dram_tensor("h_scr", [INTER, TO], F32, kind="Internal").ap()
    xqh_scr = nc.dram_tensor("xqh_scr", [INTER, TO], BF16, kind="Internal").ap()
    sc_scr = nc.dram_tensor("sc_scr", [1, 4], F32, kind="Internal").ap()

    akey = {}

    def place(name, shape, dt, off, keys):
        t = nc.alloc_sbuf_tensor_at(name, shape, dt, offset=off)
        akey[name] = (keys,) if isinstance(keys, str) else tuple(keys)
        return t

    Z0 = 16544
    Z1 = Z0 + 81920
    Z2 = Z1 + 32768
    Z3 = Z2 + 32768
    Z4 = Z3 + 32768
    Z5 = Z4 + 24576
    XN16 = place("XN16", [128, NDC * T], BF16, Z0, "z0a")
    cosT = place("cosT", [128, T], F32, Z0 + 65536, "z0b")
    sinT = place("sinT", [128, T], F32, Z0 + 73728, "z0c")
    GWF = [place("GWF0", [128, NDC * 256], F32, Z1, ("z1a", "z1b")),
           place("GWF1", [128, NDC * 256], F32, Z1 + 16384,
                 ("z1c", "z1d", "z1e", "z1f"))]
    WQ16 = [place("WQ16_0", [128, NDC * 256], BF16, Z4, ("z4a", "z4b")),
            place("WQ16_1", [128, NDC * 256], BF16, Z4 + 16384, ("z4e", "z4f"))]
    U1 = place("U1", [128, 2048], F32, Z0 + 65536, "z0b")
    U2 = place("U2", [128, 2048], F32, Z0 + 73728, "z0c")
    RSTD = place("RSTD", [128, T], F32, Z1, "z1a")
    MUSC = place("MUSC", [128, T], F32, Z1 + 8192, "z1b")
    RRMS = place("RRMS", [128, TO], F32, Z1 + 16384, "z1c")
    CG = place("CG", [128, TO], F32, Z1 + 20480, "z1d")
    MM1 = place("MM1", [128, TO], F32, Z1 + 24576, "z1e")
    K16 = place("K16", [128, T], BF16, Z1, "z1a")
    VT16 = place("VT16", [128, 2 * T], BF16, Z1 + 4096, ("z1a", "z1b"))
    Q16 = place("Q16", [128, TO], BF16, Z1 + 12288, "z1b")
    SQH = place("SQH", [128, T], BF16, Z1 + 14336, ("z1b", "z1c"))
    RECB = place("RECB", [128, TO], F32, Z1 + 18432, ("z1c", "z1d"))
    LNS = place("LNS", [128, TO], F32, Z1 + 22528, ("z1d", "z1e"))
    PB = [place("PB0", [128, TO], BF16, Z1 + 26624, "z1e"),
          place("PB1", [128, TO], BF16, Z1 + 28672, "z1f")]
    S16 = place("S16", [128, NDC * TO], BF16, Z1,
                ("z1a", "z1b", "z1c", "z1d", "z1e", "z1f"))
    XB = place("XB", [128, T], F32, Z2, "z2a")
    XB1 = place("XB1", [128, T], F32, Z3, ("z3a",))
    SQB = place("SQB", [128, T], F32, Z2 + 8192, "z2b")
    TMP1 = place("TMP1", [128, T], F32, Z2 + 16384, "z2c")
    TMP2 = place("TMP2", [128, T], F32, Z2 + 24576, "z2d")
    QT = place("QT", [128, TO], F32, Z2 + 8192, "z2b")
    QI = place("QI", [128, TO], I32, Z2 + 12288, "z2b")
    QC = place("QC", [128, TO], F32, Z2 + 16384, "z2c")
    QN = place("QN", [128, TO], F32, Z2 + 20480, "z2c")
    QF = place("QF", [128, TO], F32, Z2 + 24576, "z2d")
    QD = place("QD", [128, TO], F32, Z2 + 28672, "z2d")
    WQK = [place("WQK0", [128, NDC * 128], BF16, Z2, "z2a"),
           place("WQK1", [128, NDC * 128], BF16, Z2 + 4096, "z2a")]
    LQD = place("LQD", [128, TO], F32, Z2 + 8192, "z2b")
    LQC = place("LQC", [128, TO], F32, Z2 + 12288, "z2b")
    WVG = [place("WVG0", [128, NDC * 256], BF16, Z2 + 16384, "z2c"),
           place("WVG1", [128, NDC * 256], BF16, Z2 + 24576, "z2d")]
    WOB = [place("WOB0", [128, NDC * 128], BF16, Z2, "z2a"),
           place("WOB1", [128, NDC * 128], BF16, Z2 + 4096, "z2a")]
    EV1 = place("EV1", [128, TO], F32, Z2, "z2a")
    EV2 = place("EV2", [128, TO], F32, Z2 + 4096, "z2a")
    EV3 = [place("EV3a", [128, TO], F32, Z2 + 8192, "z2b"),
           place("EV3b", [128, TO], F32, Z2 + 12288, "z2b")]
    HB = place("HB", [128, TO], F32, Z4 + 4096, "z4b")
    LYH1 = place("LYH1", [128, TO], F32, Z4 + 8192, "z4c")
    LQF = place("LQF", [128, TO], F32, Z4 + 12288, "z4d")
    YH1 = place("YH1", [128, TO], F32, Z2 + 8192, "z2b")
    YH2 = place("YH2", [128, TO], F32, Z2 + 12288, "z2b")
    HS1 = place("HS1", [128, TO], F32, Z2 + 16384, "z2c")
    HS2 = place("HS2", [128, TO], F32, Z2 + 20480, "z2c")
    YH2p = place("YH2p", [128, TO], F32, Z2 + 16384, "z2c")
    HSQA = place("HSQA", [128, TO], F32, Z2 + 20480, "z2c")
    SQF = place("SQF", [128, TO], F32, Z2 + 24576, "z2d")
    XQHC0 = place("XQHC0", [128, 40 * TO], BF16, Z0, ("z0a", "z0b", "z0c"))
    XQHC1 = place("XQHC1", [128, 4 * TO], BF16, Z3, "z3a")
    DWC = [place("DWC0", [128, 2816], F32, Z2, ("z2a", "z2b")),
           place("DWC1", [128, 2816], F32, Z2 + 11264, ("z2b", "z2c"))]
    UD1 = place("UD1", [128, 1024], F32, Z2 + 22528, ("z2c", "z2d"))
    UD2 = place("UD2", [128, 1024], F32, Z2 + 26624, "z2d")
    WQDC = [place("WQDC0", [128, 5632], BF16, Z4, ("z4a", "z4b", "z4c")),
            place("WQDC1", [128, 5632], BF16, Z3 + 20480, "z3b")]
    OBA = place("OBA", [128, TO], F32, Z4 + 12288, "z4d")
    OBB = place("OBB", [128, TO], F32, Z4 + 16384, "z4e")
    XR = place("XR", [128, TO], F32, Z3 + 16384, "z3b")
    XQ = place("XQ", [128, NDC * TO], BF16, Z3, ("z3a", "z3b"))
    O16 = place("O16", [128, H * TO], BF16, Z3, ("z3a", "z3b"))
    SQ16 = place("SQ16", [128, T], BF16, Z4, "z4a")
    XB16 = place("XB16", [128, T], BF16, Z4 + 4096, "z4b")
    WRB = [place("WRB0", [128, 2048], BF16, Z4 + 8192, "z4c"),
           place("WRB1", [128, 2048], BF16, Z4 + 12288, "z4d")]
    RTa = place("RTa", [64, TO], F32, Z4 + 16384, "z4e")
    RTb = place("RTb", [64, TO], F32, Z1 + 22528, ("z1d", "z1e"))
    RTc = place("RTc", [64, TO], F32, Z1 + 18432, ("z1c", "z1d"))
    RTd = place("RTd", [64, TO], F32, Z1 + 14336, ("z1b", "z1c"))
    SCG = place("SCG", [128, TO], F32, Z4 + 8192, "z4c")
    MMH = place("MMH", [128, TO], F32, Z4 + 12288, "z4d")
    CCH = place("CCH", [128, TO], F32, Z4, "z4a")
    SCD = place("SCD", [128, TO], F32, Z4 + 20480, "z4f")
    z5 = [Z5]

    def small(name, shape, dt=F32):
        t = place(name, shape, dt, z5[0], name)
        sz = shape[1] * {F32: 4, BF16: 2, I32: 4}[dt]
        z5[0] += (sz + 31) // 32 * 32
        return t

    g1s = small("g1s", [128, NDC])
    g2s = small("g2s", [128, NHC])
    ones16 = small("ones16", [128, 128], BF16)
    idf = small("idf", [128, 128])
    epsv = small("epsv", [128, 2])
    SCAL = small("SCAL", [128, 4])
    SCLT = small("SCLT", [1, 4])
    MCOMP = small("MCOMP", [128, 8])
    MQ = small("MQ", [128, 1])
    MK = small("MK", [128, 1])
    NEGM = small("NEGM", [128, 1])
    WACC = small("WACC", [128, 2])
    WS1 = small("WS1", [128, 1])
    WS2 = small("WS2", [128, 1])
    WS1D = small("WS1D", [128, 1])
    WS2D = small("WS2D", [128, 1])
    BCH = small("BCH", [128, 128])
    HLF = small("HLF", [128, 2])
    ONESF = small("ONESF", [128, 128])
    LADB = small("LADB", [128, 14])
    DEN_G = small("DEN_G", [128, TO])
    XQHT = [place("XQHT0", [128, TO], BF16, z5[0] - 4096, "DEN_G"),
            place("XQHT1", [128, TO], BF16, z5[0] - 2048, "DEN_G")]
    assert z5[0] <= 229376, z5[0]

    PS = [nc.alloc_psum_tensor(f"PS{i}", [128, TO], F32) for i in range(4)]

    def K(*names):
        out = []
        for n in names:
            out.extend(akey.get(n, (n,)))
        return out

    taps = []

    def tapk(name, tile, keys, rows=128, cols=None):
        if name not in DEBUG_TAPS:
            return
        c = cols if cols is not None else tile.shape[1]
        dt = tile.dtype
        o = nc.dram_tensor("tap_" + name, [rows, c], dt, kind="ExternalOutput").ap()
        taps.append(name)
        E("gpsimd", lambda G, o=o, t=tile, r=rows, c=c: G.dma_start(out=o, in_=t[0:r, 0:c]),
          reads=K(*keys), sem=f"tp{len(taps)}", inc=16)

    for ci, (dst, srcp, nm) in enumerate([(g1s, g1t, "g1s"), (g2s, g2t, "g2s"),
                                          (ones16, onesb16, "ones16"), (idf, identf, "idf"),
                                          (epsv, cvec, "epsv")]):
        E("sync", lambda S, d=dst, s=srcp: S.dma_start(out=d[:, :], in_=s),
          writes=K(nm), sem=f"cs{ci}", inc=16)

    for nm in ("WS1", "WS2", "WS1D", "WS2D"):
        t = {"WS1": WS1, "WS2": WS2, "WS1D": WS1D, "WS2D": WS2D}[nm]
        E("vector", lambda V, t=t: V.memset(t[:, :], 0.0), writes=K(nm))
    gw_flat = gw_b.rearrange("a (b c) -> (a b) c", b=2)
    wred_list = [(gw_flat[rg * 128:(rg + 1) * 128, :], 2048) for rg in range(NHC * 2)]
    wred_list += [(dwA[:, rg * 2048:(rg + 1) * 2048], 2048) for rg in range(NHC)]
    NGW = NHC * 2

    for idxw, (apw, w) in enumerate(wred_list):
        b = idxw % 2
        E("gpsimd", lambda G, b=b, apw=apw, w=w: G.dma_start(out=WRB[b][:, 0:w], in_=apw),
          writes=K(f"WRB{b}"), sem=f"wr{b}", inc=16)
        awt = SQ16 if b == 0 else XB16
        awk = "SQ16" if b == 0 else "XB16"
        E("scalar", lambda S, b=b, w=w, awt=awt: S.activation(out=awt[:, 0:w],
                                                              in_=WRB[b][:, 0:w],
                                                              func=AF.Abs),
          reads=K(f"WRB{b}"), writes=K(awk))
        isgw = idxw < NGW
        ps_lo = 0 if isgw else 2
        st = (idxw == 0) if isgw else (idxw == NGW)
        sp = (idxw == NGW - 1) if isgw else (idxw == len(wred_list) - 1)

        def mm_wred(TE, w=w, ps_lo=ps_lo, st=st, sp=sp, awt=awt):
            nj = (w + 511) // 512
            for j in range(nj):
                wj = min(512, w - j * 512)
                TE.ldweights(ones16[:, :])
                i = TE.matmul(out=PS[ps_lo + j // 2][:, (j % 2) * 512:(j % 2) * 512 + wj],
                              lhsT=ones16[:, :], rhs=awt[:, j * 512:j * 512 + wj],
                              start=st, stop=sp)
            return i
        E("tensor", mm_wred, reads=K(awk, "ones16"),
          writes=[f"PS{ps_lo}", f"PS{ps_lo + 1}"])
    E("vector", lambda V: V.reduce_sum(out=WS1[:, :], in_=PS[0][:, :], axis=AX.X),
      reads=["PS0"], writes=K("WS1"))
    E("vector", lambda V: V.reduce_sum(out=WS2[:, :], in_=PS[1][:, :], axis=AX.X),
      reads=["PS1"], writes=K("WS2"))
    E("vector", lambda V: V.reduce_sum(out=WS1D[:, :], in_=PS[2][:, :], axis=AX.X),
      reads=["PS2"], writes=K("WS1D"))
    E("vector", lambda V: V.reduce_sum(out=WS2D[:, :], in_=PS[3][:, :], axis=AX.X),
      reads=["PS3"], writes=K("WS2D"))
    E("vector", lambda V: V.tensor_tensor(out=WACC[:, 0:1], in0=WS1[:, :], in1=WS2[:, :],
                                          op=OP.add),
      reads=K("WS1", "WS2"), writes=K("WACC"))
    E("vector", lambda V: V.tensor_tensor(out=WACC[:, 1:2], in0=WS1D[:, :], in1=WS2D[:, :],
                                          op=OP.add),
      reads=K("WS1D", "WS2D"), writes=K("WACC"))
    E("vector", lambda V: V.tensor_scalar(out=SCAL[:, 0:1], in0=WACC[:, 0:1],
                                          scalar1=1.0 / (D * 2 * INTER), scalar2=EPS,
                                          op0=OP.mult, op1=OP.max),
      reads=K("WACC"), writes=K("SCAL"))
    E("vector", lambda V: V.tensor_scalar(out=SCAL[:, 1:2], in0=WACC[:, 1:2],
                                          scalar1=1.0 / (INTER * D), scalar2=EPS,
                                          op0=OP.mult, op1=OP.max),
      reads=K("WACC"), writes=K("SCAL"))
    E("vector", lambda V: V.reciprocal(out=SCAL[:, 2:3], in_=SCAL[:, 0:1]),
      reads=K("SCAL"), writes=K("SCAL"))
    E("vector", lambda V: V.reciprocal(out=SCAL[:, 3:4], in_=SCAL[:, 1:2]),
      reads=K("SCAL"), writes=K("SCAL"))

    E("vector", lambda V: V.memset(HLF[:, 0:1], 0.5), writes=K("HLF"))
    E("vector", lambda V: V.memset(ONESF[:, :], 1.0), writes=K("ONESF"))
    E("vector", lambda V: V.memset(HLF[:, 1:2], -0.5), writes=K("HLF"))
    for i, m in enumerate((64.0, 32.0, 16.0, 8.0, 4.0, 2.0, 1.0)):
        E("vector", lambda V, i=i, m=m: V.memset(LADB[:, 2 * i:2 * i + 1], m - 0.5),
          writes=K("LADB"))
        E("vector", lambda V, i=i, m=m: V.memset(LADB[:, 2 * i + 1:2 * i + 2], -(m - 0.5)),
          writes=K("LADB"))

    def xchunk(dc):
        return xT[dc * 128:(dc + 1) * 128, :]

    for dc in range(NDC):
        xb = XB if dc % 2 == 0 else XB1
        xbn = "XB" if dc % 2 == 0 else "XB1"
        E("sync", lambda S, dc=dc, xb=xb: S.dma_start(out=xb[:, :], in_=xchunk(dc)),
          writes=K(xbn), sem=f"xb{dc % 2}", inc=16)
        E("scalar", lambda S, xb=xb: S.activation(out=SQB[:, :], in_=xb[:, :], func=AF.Square),
          reads=K(xbn), writes=K("SQB"))
        E("vector", lambda V, xb=xb: V.tensor_copy(out=XB16[:, :], in_=xb[:, :]),
          reads=K(xbn), writes=K("XB16"))
        E("vector", lambda V: V.tensor_copy(out=SQ16[:, :], in_=SQB[:, :]),
          reads=K("SQB"), writes=K("SQ16"))
        st, sp = dc == 0, dc == NDC - 1

        def mm_stats(TE, st=st, sp=sp):
            for half in range(2):
                for j in range(2):
                    TE.ldweights(ones16[:, :])
                    TE.matmul(out=PS[half][:, 512 * j:512 * (j + 1)], lhsT=ones16[:, :],
                              rhs=XB16[:, half * TO + 512 * j: half * TO + 512 * (j + 1)],
                              start=st, stop=sp)
                    TE.ldweights(ones16[:, :])
                    i = TE.matmul(out=PS[2 + half][:, 512 * j:512 * (j + 1)], lhsT=ones16[:, :],
                                  rhs=SQ16[:, half * TO + 512 * j: half * TO + 512 * (j + 1)],
                                  start=st, stop=sp)
            return i
        E("tensor", mm_stats, reads=K("XB16", "SQ16", "ones16"),
          writes=["PS0", "PS1", "PS2", "PS3"])

    MUt, VARt = XB, SQB
    for half in range(2):
        E("vector", lambda V, h=half: V.tensor_scalar(out=MUt[:, h * TO:(h + 1) * TO],
                                                      in0=PS[h][:, :], scalar1=1.0 / D,
                                                      scalar2=None, op0=OP.mult),
          reads=[f"PS{half}"], writes=K("XB"))
        E("vector", lambda V, h=half: V.tensor_scalar(out=VARt[:, h * TO:(h + 1) * TO],
                                                      in0=PS[2 + h][:, :], scalar1=1.0 / D,
                                                      scalar2=None, op0=OP.mult),
          reads=[f"PS{2 + half}"], writes=K("SQB"))
    E("vector", lambda V: V.tensor_tensor(out=TMP1[:, :], in0=MUt[:, :], in1=MUt[:, :], op=OP.mult),
      reads=K("XB"), writes=K("TMP1"))
    E("vector", lambda V: V.tensor_tensor(out=VARt[:, :], in0=VARt[:, :], in1=TMP1[:, :],
                                          op=OP.subtract),
      reads=K("TMP1", "SQB"), writes=K("SQB"))
    E("scalar", lambda S: S.activation(out=TMP1[:, :], in_=VARt[:, :], func=AF.Ln,
                                       bias=epsv[:, 0:1]),
      reads=K("SQB", "epsv"), writes=K("TMP1"))
    E("scalar", lambda S: S.activation(out=RSTD[:, :], in_=TMP1[:, :], func=AF.Exp, scale=-0.5),
      reads=K("TMP1"), writes=K("RSTD"))
    E("vector", lambda V: V.tensor_tensor(out=MUSC[:, :], in0=MUt[:, :], in1=RSTD[:, :],
                                          op=OP.mult),
      reads=K("XB", "RSTD"), writes=K("MUSC"))
    E("vector", lambda V: V.tensor_scalar(out=TMP1[:, 0:TO], in0=VARt[:, 0:TO], scalar1=EPS,
                                          scalar2=None, op0=OP.add),
      reads=K("SQB"), writes=K("TMP1"))
    E("vector", lambda V: V.reciprocal(out=MM1[:, :], in_=TMP1[:, 0:TO]),
      reads=K("TMP1"), writes=K("MM1"))
    E("vector", lambda V: V.tensor_tensor(out=TMP2[:, 0:TO], in0=VARt[:, 0:TO], in1=MM1[:, :],
                                          op=OP.mult),
      reads=K("SQB", "MM1"), writes=K("TMP2"))
    E("scalar", lambda S: S.activation(out=TMP1[:, 0:TO], in_=TMP2[:, 0:TO], func=AF.Ln,
                                       bias=epsv[:, 0:1]),
      reads=K("TMP2", "epsv"), writes=K("TMP1"))
    E("scalar", lambda S: S.activation(out=RRMS[:, :], in_=TMP1[:, 0:TO], func=AF.Exp, scale=-0.5),
      reads=K("TMP1"), writes=K("RRMS"))
    tapk("rstd", RSTD, ["RSTD"])
    tapk("rrms", RRMS, ["RRMS"])

    for dc in range(NDC):
        xb = XB if dc % 2 == 0 else XB1
        xbn = "XB" if dc % 2 == 0 else "XB1"
        E("sync", lambda S, dc=dc, xb=xb: S.dma_start(out=xb[:, :], in_=xchunk(dc)),
          writes=K(xbn), sem=f"xb{dc % 2}", inc=16)
        E("vector", lambda V, xb=xb: V.tensor_tensor(out=TMP1[:, :], in0=xb[:, :], in1=RSTD[:, :],
                                              op=OP.mult),
          reads=K(xbn, "RSTD"), writes=K("TMP1"))
        E("vector", lambda V: V.tensor_tensor(out=TMP2[:, :], in0=TMP1[:, :], in1=MUSC[:, :],
                                              op=OP.subtract),
          reads=K("TMP1", "MUSC"), writes=K("TMP2"))
        E("vector", lambda V, dc=dc: V.tensor_copy(out=XN16[:, dc * T:(dc + 1) * T],
                                                   in_=TMP2[:, :]),
          reads=K("TMP2"), writes=K("XN16"))
        E("vector", lambda V, dc=dc: V.tensor_scalar(out=QT[:, :], in0=TMP2[:, 0:TO],
                                                     scalar1=g1s[:, dc:dc + 1], scalar2=None,
                                                     op0=OP.mult),
          reads=K("TMP2", "g1s"), writes=K("QT"))
        E("scalar", lambda S: S.activation(out=QF[:, :], in_=QT[:, :], func=AF.Abs),
          reads=K("QT"), writes=K("QF"))
        if dc == 0:
            E("vector", lambda V: V.tensor_copy(out=MM1[:, :], in_=QF[:, :]),
              reads=K("QF"), writes=K("MM1"))
        else:
            E("vector", lambda V: V.tensor_tensor(out=MM1[:, :], in0=MM1[:, :], in1=QF[:, :],
                                                  op=OP.max),
              reads=K("QF", "MM1"), writes=K("MM1"))
    tapk("xn16", XN16, ["XN16"])

    def compact_max(src, srckeys, dst, dstnm):
        for blk in range(8):
            E("tensor", lambda TE, blk=blk: TE.transpose(
                out=PS[0][:, 0:128], in_=src[:, blk * 128:(blk + 1) * 128], identity=idf[:, :]),
              reads=K(*srckeys) + K("idf"), writes=["PS0"])
            E("vector", lambda V, blk=blk: V.reduce_max(out=dst[:, blk:blk + 1],
                                                        in_=PS[0][:, 0:128], axis=AX.X),
              reads=["PS0"], writes=K(dstnm))

    def bcast_cols(srccomp, srcnm, dst, dstkeys, colbase, nblk=8):
        for blk in range(nblk):
            E("vector", lambda V, blk=blk: V.tensor_copy(
                out=BCH[:, :], in_=srccomp[:, blk:blk + 1].broadcast_to([128, 128])),
              reads=K(srcnm), writes=K("BCH"))
            E("tensor", lambda TE: TE.transpose(out=PS[0][:, 0:128], in_=BCH[:, :],
                                                identity=idf[:, :]),
              reads=K("BCH", "idf"), writes=["PS0"])
            E("vector", lambda V, blk=blk: V.tensor_copy(
                out=dst[:, colbase + blk * 128: colbase + (blk + 1) * 128], in_=PS[0][:, 0:128]),
              reads=["PS0"], writes=K(*dstkeys))

    compact_max(MM1, ["MM1"], MCOMP, "MCOMP")
    bcast_cols(MCOMP, "MCOMP", TMP1, ["TMP1"], 0)
    E("vector", lambda V: V.tensor_tensor(out=QT[:, :], in0=RRMS[:, :], in1=TMP1[:, 0:TO],
                                          op=OP.mult),
      reads=K("RRMS", "TMP1"), writes=K("QT"))
    E("vector", lambda V: V.tensor_scalar(out=DEN_G[:, :], in0=QT[:, :], scalar1=EPS,
                                          scalar2=None, op0=OP.max),
      reads=K("QT"), writes=K("DEN_G"))
    E("vector", lambda V: V.reciprocal(out=QF[:, :], in_=DEN_G[:, :]),
      reads=K("DEN_G"), writes=K("QF"))
    E("vector", lambda V: V.tensor_tensor(out=QT[:, :], in0=RRMS[:, :], in1=QF[:, :], op=OP.mult),
      reads=K("RRMS", "QF"), writes=K("QT"))
    E("vector", lambda V: V.tensor_scalar(out=CG[:, :], in0=QT[:, :], scalar1=127.0,
                                          scalar2=None, op0=OP.mult),
      reads=K("QT"), writes=K("CG"))
    tapk("cg", CG, ["CG"])
    tapk("deng", DEN_G, ["DEN_G"])

    def cast_correct(src_t, srckeys, dst, dstkeys, dst_c0):
        E("vector", lambda V: V.tensor_copy(out=QI[:, :], in_=src_t[:, 0:TO]),
          reads=K(*srckeys), writes=K("QI"))
        E("vector", lambda V: V.tensor_scalar(out=QF[:, :], in0=QI[:, :], scalar1=-128.0,
                                              scalar2=127.0, op0=OP.max, op1=OP.min),
          reads=K("QI"), writes=K("QF"))
        E("gpsimd", lambda G: G.tensor_tensor(out=QD[:, :], in0=src_t[:, 0:TO], in1=QF[:, :],
                                              op=OP.subtract),
          reads=K(*srckeys) + K("QF"), writes=K("QD"))
        E("vector", lambda V: V.tensor_scalar(out=QC[:, :], in0=QD[:, :], scalar1=0.5,
                                              scalar2=None, op0=OP.is_ge),
          reads=K("QD"), writes=K("QC"))
        E("vector", lambda V: V.scalar_tensor_tensor(out=QN[:, :], in0=QD[:, :], scalar=-0.5,
                                                     in1=QC[:, :], op0=OP.is_le, op1=OP.subtract),
          reads=K("QD", "QC"), writes=K("QN"))
        E("gpsimd", lambda G: G.tensor_tensor(out=dst[:, dst_c0:dst_c0 + TO], in0=QF[:, :],
                                              in1=QN[:, :], op=OP.subtract),
          reads=K("QF", "QN"), writes=K(*dstkeys))

    def ladder_round(src_t, srckeys, dst, dstkeys, dst_c0, eng2="vector"):
        E("vector", lambda V: V.tensor_copy(out=QD[:, :], in_=src_t[:, 0:TO]),
          reads=K(*srckeys), writes=K("QD"))
        for i, m in enumerate((64.0, 32.0, 16.0, 8.0, 4.0, 2.0, 1.0)):
            E(eng2, lambda X, m=m: X.tensor_scalar(out=QC[:, :], in0=QD[:, :], scalar1=m - 0.5,
                                                   scalar2=None, op0=OP.is_ge),
              reads=K("QD"), writes=K("QC"))
            E("vector", lambda X, m=m: X.scalar_tensor_tensor(out=QC[:, :], in0=QD[:, :],
                                                              scalar=-(m - 0.5), in1=QC[:, :],
                                                              op0=OP.is_le, op1=OP.subtract),
              reads=K("QD", "QC"), writes=K("QC"))
            E("vector", lambda X, m=m: X.scalar_tensor_tensor(out=QD[:, :], in0=QC[:, :], scalar=m,
                                                              in1=QD[:, :], op0=OP.mult, op1=OP.add),
              reads=K("QC", "QD"), writes=K("QD"))
        E("vector", lambda V: V.tensor_tensor(out=QF[:, :], in0=src_t[:, 0:TO], in1=QD[:, :],
                                              op=OP.subtract),
          reads=K(*srckeys) + K("QD"), writes=K("QF"))
        E("vector", lambda V: V.tensor_scalar(out=dst[:, dst_c0:dst_c0 + TO], in0=QF[:, :],
                                              scalar1=-128.0, scalar2=127.0, op0=OP.max,
                                              op1=OP.min),
          reads=K("QF"), writes=K(*dstkeys))

    for dc in range(NDC):
        E("sync", lambda S, dc=dc: S.dma_start(out=XB[:, 0:TO], in_=xchunk(dc)[:, 0:TO]),
          writes=K("XB"), sem="xb", inc=16)
        E("vector", lambda V: V.tensor_tensor(out=TMP1[:, 0:TO], in0=XB[:, 0:TO],
                                              in1=RSTD[:, 0:TO], op=OP.mult),
          reads=K("XB", "RSTD"), writes=K("TMP1"))
        E("vector", lambda V: V.tensor_tensor(out=TMP2[:, 0:TO], in0=TMP1[:, 0:TO],
                                              in1=MUSC[:, 0:TO], op=OP.subtract),
          reads=K("TMP1", "MUSC"), writes=K("TMP2"))
        E("vector", lambda V, dc=dc: V.tensor_scalar(out=QT[:, :], in0=TMP2[:, 0:TO],
                                                     scalar1=g1s[:, dc:dc + 1], scalar2=None,
                                                     op0=OP.mult),
          reads=K("TMP2", "g1s"), writes=K("QT"))
        E("vector", lambda V: V.tensor_tensor(out=QT[:, :], in0=QT[:, :], in1=CG[:, :],
                                              op=OP.mult),
          reads=K("CG", "QT"), writes=K("QT"))
        cast_correct(QT, ["QT"], XQ, ["XQ"], dc * TO)
    tapk("xq", XQ, ["XQ"])

    E("vector", lambda V: V.tensor_scalar(out=SCG[:, :], in0=DEN_G[:, :], scalar1=SCAL[:, 0:1],
                                          scalar2=0.5 / 127.0, op0=OP.mult, op1=OP.mult),
      reads=K("DEN_G", "SCAL"), writes=K("SCG"))

    def hw_tile(jp):
        return h_scr[jp * 128:(jp + 1) * 128, :]

    def gate_load_quant(jp):
        gb = jp % 2
        E("sync", lambda S, jp=jp, gb=gb: S.dma_start(
            out=GWF[gb][:, :], in_=gw_b[jp * 128:(jp + 1) * 128, :]),
          writes=K(f"GWF{gb}"), sem=f"gw{gb}", inc=16)
        for hf in range(2):
            E("scalar", lambda S, gb=gb, hf=hf: S.activation(
                out=U1[:, :], in_=GWF[gb][:, hf * 2048:(hf + 1) * 2048],
                func=AF.Sign, scale=SCAL[:, 2:3], bias=HLF[:, 1:2]),
              reads=K(f"GWF{gb}", "SCAL", "HLF"), writes=K("U1"))
            E("scalar", lambda S, gb=gb, hf=hf: S.activation(
                out=U2[:, :], in_=GWF[gb][:, hf * 2048:(hf + 1) * 2048],
                func=AF.Sign, scale=SCAL[:, 2:3], bias=HLF[:, 0:1]),
              reads=K(f"GWF{gb}", "SCAL", "HLF"), writes=K("U2"))
            E("vector", lambda V, gb=gb, hf=hf: V.tensor_tensor(
                out=WQ16[gb][:, hf * 2048:(hf + 1) * 2048], in0=U1[:, :], in1=U2[:, :],
                op=OP.add),
              reads=K("U1", "U2"), writes=K(f"WQ16_{gb}"))

    gate_load_quant(0)
    for jp in range(NHC):
        gb = jp % 2
        psg, psgn = PS[2 * gb], f"PS{2 * gb}"
        psu, psun = PS[2 * gb + 1], f"PS{2 * gb + 1}"
        ev = EV3[gb]
        evn = "EV3a" if gb == 0 else "EV3b"
        if jp + 1 < NHC:
            gate_load_quant(jp + 1)

        def mm_gate(TE, gb=gb, psg=psg, psu=psu):
            for dc in range(NDC):
                TE.ldweights(WQ16[gb][:, dc * 256: dc * 256 + 128])
                for j in range(2):
                    TE.matmul(out=psg[:, j * 512:(j + 1) * 512],
                              lhsT=WQ16[gb][:, dc * 256: dc * 256 + 128],
                              rhs=XQ[:, dc * TO + j * 512: dc * TO + (j + 1) * 512],
                              start=(dc == 0), stop=(dc == NDC - 1))
                TE.ldweights(WQ16[gb][:, dc * 256 + 128:(dc + 1) * 256])
                for j in range(2):
                    i = TE.matmul(out=psu[:, j * 512:(j + 1) * 512],
                                  lhsT=WQ16[gb][:, dc * 256 + 128:(dc + 1) * 256],
                                  rhs=XQ[:, dc * TO + j * 512: dc * TO + (j + 1) * 512],
                                  start=(dc == 0), stop=(dc == NDC - 1))
            return i
        E("tensor", mm_gate, reads=K(f"WQ16_{gb}", "XQ"), writes=[psgn, psun])
        E("vector", lambda V, psg=psg: V.tensor_tensor(out=EV1[:, :], in0=psg[:, 0:TO],
                                                       in1=SCG[:, :], op=OP.mult),
          reads=[psgn] + K("SCG"), writes=K("EV1"))
        E("scalar", lambda S: S.activation(out=EV2[:, :], in_=EV1[:, :], func=AF.Silu),
          reads=K("EV1"), writes=K("EV2"))
        E("vector", lambda V, psu=psu, ev=ev: V.tensor_tensor(out=ev[:, :], in0=EV2[:, :],
                                                              in1=psu[:, 0:TO], op=OP.mult),
          reads=[psun] + K("EV2"), writes=K(evn))
        E("gpsimd", lambda G, jp=jp, ev=ev: G.dma_start(out=hw_tile(jp), in_=ev[:, :]),
          reads=K(evn), writes=["h_scr"], sem=f"hw{jp % 2}", inc=16)
        E("scalar", lambda S, jp=jp, ev=ev: S.activation(out=YH2p[:, :], in_=ev[:, :],
                                                         func=AF.Abs,
                                                         scale=g2s[:, jp:jp + 1]),
          reads=K(evn, "g2s"), writes=K("YH2p"))
        if jp == 0:
            E("vector", lambda V: V.tensor_copy(out=MMH[:, :], in_=YH2p[:, :]),
              reads=K("YH2p"), writes=K("MMH"))
        else:
            E("vector", lambda V: V.tensor_tensor(out=MMH[:, :], in0=MMH[:, :],
                                                  in1=YH2p[:, :], op=OP.max),
              reads=K("YH2p", "MMH"), writes=K("MMH"))
        E("scalar", lambda S, ev=ev: S.activation(out=SQF[:, :], in_=ev[:, :],
                                                  func=AF.Square),
          reads=K(evn), writes=K("SQF"))
        if jp == 0:
            E("vector", lambda V: V.tensor_copy(out=HSQA[:, :], in_=SQF[:, :]),
              reads=K("SQF"), writes=K("HSQA"))
        else:
            E("vector", lambda V: V.tensor_tensor(out=HSQA[:, :], in0=HSQA[:, :],
                                                  in1=SQF[:, :], op=OP.add),
              reads=K("SQF", "HSQA"), writes=K("HSQA"))

    def mm_hsum(TE):
        for j in range(2):
            i = TE.matmul(out=PS[3][:, j * 512:(j + 1) * 512], lhsT=ONESF[:, :],
                          rhs=HSQA[:, j * 512:(j + 1) * 512], start=True, stop=True)
        return i
    E("tensor", mm_hsum, reads=K("HSQA", "ONESF"), writes=["PS3"])
    E("vector", lambda V: V.tensor_scalar(out=HS1[:, :], in0=PS[3][:, 0:TO],
                                          scalar1=1.0 / INTER, scalar2=None, op0=OP.mult),
      reads=["PS3"], writes=K("HS1"))
    E("vector", lambda V: V.tensor_tensor(out=HS2[:, :], in0=SCG[:, :], in1=SCG[:, :],
                                          op=OP.mult),
      reads=K("SCG"), writes=K("HS2"))
    E("vector", lambda V: V.tensor_tensor(out=HS1[:, :], in0=HS1[:, :], in1=HS2[:, :],
                                          op=OP.mult),
      reads=K("HS2", "HS1"), writes=K("HS1"))
    E("scalar", lambda S: S.activation(out=HS2[:, :], in_=HS1[:, :], func=AF.Ln,
                                       bias=epsv[:, 0:1]),
      reads=K("HS1", "epsv"), writes=K("HS2"))
    E("scalar", lambda S: S.activation(out=HS1[:, :], in_=HS2[:, :], func=AF.Exp, scale=-0.5),
      reads=K("HS2"), writes=K("HS1"))
    compact_max(MMH, ["MMH"], MCOMP, "MCOMP")
    bcast_cols(MCOMP, "MCOMP", HS2, ["HS2"], 0)
    E("vector", lambda V: V.tensor_tensor(out=YH1[:, :], in0=SCG[:, :], in1=HS2[:, :],
                                          op=OP.mult),
      reads=K("SCG", "HS2"), writes=K("YH1"))
    E("vector", lambda V: V.tensor_tensor(out=YH1[:, :], in0=YH1[:, :], in1=HS1[:, :],
                                          op=OP.mult),
      reads=K("HS1", "YH1"), writes=K("YH1"))
    E("vector", lambda V: V.tensor_scalar(out=YH2[:, :], in0=YH1[:, :], scalar1=EPS,
                                          scalar2=None, op0=OP.max),
      reads=K("YH1"), writes=K("YH2"))
    E("vector", lambda V: V.reciprocal(out=YH1[:, :], in_=YH2[:, :]),
      reads=K("YH2"), writes=K("YH1"))
    E("vector", lambda V: V.tensor_tensor(out=CCH[:, :], in0=HS1[:, :], in1=YH1[:, :],
                                          op=OP.mult),
      reads=K("HS1", "YH1"), writes=K("CCH"))
    E("vector", lambda V: V.tensor_tensor(out=CCH[:, :], in0=CCH[:, :], in1=SCG[:, :],
                                          op=OP.mult),
      reads=K("SCG", "CCH"), writes=K("CCH"))
    E("vector", lambda V: V.tensor_scalar(out=CCH[:, :], in0=CCH[:, :], scalar1=127.0,
                                          scalar2=None, op0=OP.mult),
      reads=K("CCH"), writes=K("CCH"))
    E("vector", lambda V: V.tensor_scalar(out=SCD[:, :], in0=YH2[:, :], scalar1=SCAL[:, 1:2],
                                          scalar2=0.5 / 127.0, op0=OP.mult, op1=OP.mult),
      reads=K("YH2", "SCAL"), writes=K("SCD"))
    tapk("cch", CCH, ["CCH"])
    tapk("scd", SCD, ["SCD"])


    pb_units = []

    def _pbA(jp):
        E("sync", lambda S, jp=jp: S.dma_start(out=HB[:, :], in_=hw_tile(jp)),
          reads=["h_scr"], writes=K("HB"), sem="hr0", inc=16)
        E("vector", lambda V, jp=jp: V.scalar_tensor_tensor(
            out=LYH1[:, :], in0=HB[:, :], scalar=g2s[:, jp:jp + 1], in1=CCH[:, :],
            op0=OP.mult, op1=OP.mult),
          reads=K("HB", "g2s", "CCH"), writes=K("LYH1"))
        E("vector", lambda V: V.tensor_copy(out=LQD[:, :], in_=LYH1[:, :]),
          reads=K("LYH1"), writes=K("LQD"))

    def _pbS(i):
        m = (64.0, 32.0, 16.0, 8.0, 4.0, 2.0, 1.0)[i]
        E("scalar", lambda S, i=i: S.activation(out=LQC[:, :], in_=LQD[:, :],
                                                func=AF.Sign,
                                                bias=LADB[:, 2 * i + 1:2 * i + 2]),
          reads=K("LQD", "LADB"), writes=K("LQC"))
        E("scalar", lambda S, i=i: S.activation(out=LQF[:, :], in_=LQD[:, :],
                                                func=AF.Sign,
                                                bias=LADB[:, 2 * i:2 * i + 1]),
          reads=K("LQD", "LADB"), writes=K("LQF"))
        E("vector", lambda V: V.tensor_tensor(out=LQC[:, :], in0=LQC[:, :],
                                              in1=LQF[:, :], op=OP.add),
          reads=K("LQC", "LQF"), writes=K("LQC"))
        E("vector", lambda V, m=m: V.scalar_tensor_tensor(
            out=LQD[:, :], in0=LQC[:, :], scalar=-0.5 * m, in1=LQD[:, :],
            op0=OP.mult, op1=OP.add),
          reads=K("LQC", "LQD"), writes=K("LQD"))

    def _pbC(jp):
        b = jp % 2
        xt = XQHT[b]
        xtn = f"XQHT{b}"
        E("vector", lambda V: V.tensor_tensor(out=LQF[:, :], in0=LYH1[:, :], in1=LQD[:, :],
                                              op=OP.subtract),
          reads=K("LYH1", "LQD"), writes=K("LQF"))
        E("vector", lambda V, xt=xt: V.tensor_scalar(out=xt[:, :], in0=LQF[:, :],
                                                     scalar1=-128.0, scalar2=127.0,
                                                     op0=OP.max, op1=OP.min),
          reads=K("LQF"), writes=K(xtn))
        E("gpsimd", lambda G, jp=jp, xt=xt: G.dma_start(
            out=xqh_scr[jp * 128:(jp + 1) * 128, :], in_=xt[:, :]),
          reads=K(xtn), writes=["xqh_scr"], sem=f"xqo{b}", inc=16)

    for jp in range(NHC):
        pb_units.append(lambda jp=jp: _pbA(jp))
        for i in range(7):
            pb_units.append(lambda i=i: _pbS(i))
        pb_units.append(lambda jp=jp: _pbC(jp))
    pb_state = [0]

    def pb_pump(n):
        while pb_state[0] < len(pb_units) and n > 0:
            pb_units[pb_state[0]]()
            pb_state[0] += 1
            n -= 1

    for ci, (dst, srcp, nm) in enumerate([(cosT, cosTin, "cosT"), (sinT, sinTin, "sinT")]):
        E("sync", lambda S, d=dst, s=srcp: S.dma_start(out=d[:, :], in_=s),
          writes=K(nm), sem=f"ct{ci}", inc=16)

    for g in range(H // 2):
        gb = g % 2
        E("gpsimd", lambda G, g=g, gb=gb: G.dma_start(
            out=WVG[gb][:, :], in_=wv_b[g * 128:(g + 1) * 128, :]),
          writes=K(f"WVG{gb}"), sem=f"wv{gb}", inc=16)
        for tt in range(16):
            psn = 2 + (tt % 2)

            def mm_vt(TE, tt=tt, gb=gb, psn=psn):
                for dc in range(NDC):
                    TE.ldweights(XN16[:, dc * T + tt * 128: dc * T + (tt + 1) * 128])
                    i = TE.matmul(out=PS[psn][:, 0:256],
                                  lhsT=XN16[:, dc * T + tt * 128: dc * T + (tt + 1) * 128],
                                  rhs=WVG[gb][:, dc * 256:(dc + 1) * 256],
                                  start=(dc == 0), stop=(dc == NDC - 1))
                return i
            E("tensor", mm_vt, reads=K("XN16", f"WVG{gb}"), writes=[f"PS{psn}"])
            E("vector", lambda V, tt=tt, psn=psn: V.tensor_copy(
                out=VT16[:, tt * 128:(tt + 1) * 128], in_=PS[psn][:, 0:128]),
              reads=[f"PS{psn}"], writes=K("VT16"))
            E("vector", lambda V, tt=tt, psn=psn: V.tensor_copy(
                out=VT16[:, T + tt * 128: T + (tt + 1) * 128], in_=PS[psn][:, 128:256]),
              reads=[f"PS{psn}"], writes=K("VT16"))
            pb_pump(1)

        for hh in range(2):
            head = 2 * g + hh
            E("gpsimd", lambda G, head=head: G.dma_start(
                out=WQK[0][:, :], in_=wqkv_b[head * 128:(head + 1) * 128, :]),
              writes=K("WQK0"), sem="wq0", inc=16)
            E("gpsimd", lambda G, head=head: G.dma_start(
                out=WQK[1][:, :],
                in_=wqkv_b[(16 + head) * 128:(16 + head + 1) * 128, :]),
              writes=K("WQK1"), sem="wk0", inc=16)

            def mm_q(TE):
                for dc in range(NDC):
                    TE.ldweights(WQK[0][:, dc * 128:(dc + 1) * 128])
                    for j in range(2):
                        i = TE.matmul(out=PS[0][:, j * 512:(j + 1) * 512],
                                      lhsT=WQK[0][:, dc * 128:(dc + 1) * 128],
                                      rhs=XN16[:, dc * T + j * 512: dc * T + (j + 1) * 512],
                                      start=(dc == 0), stop=(dc == NDC - 1))
                return i
            E("tensor", mm_q, reads=K("WQK0", "XN16"), writes=["PS0"])

            def mm_k(TE):
                for dc in range(NDC):
                    TE.ldweights(WQK[1][:, dc * 128:(dc + 1) * 128])
                    for half in range(2):
                        for j in range(2):
                            i = TE.matmul(
                                out=PS[1 + half][:, j * 512:(j + 1) * 512],
                                lhsT=WQK[1][:, dc * 128:(dc + 1) * 128],
                                rhs=XN16[:, dc * T + half * TO + j * 512:
                                         dc * T + half * TO + (j + 1) * 512],
                                start=(dc == 0), stop=(dc == NDC - 1))
                return i
            E("tensor", mm_k, reads=K("WQK1", "XN16"), writes=["PS1", "PS2"])

            def rope_half(psn, dst, dstnm, dst_c0, n, tab_c0):
                E("vector", lambda V: V.tensor_copy(out=RTa[:, 0:n], in_=PS[psn][0:64, 0:n]),
                  reads=[f"PS{psn}"], writes=K("RTa"))
                E("vector", lambda V: V.tensor_copy(out=RTb[:, 0:n], in_=PS[psn][64:128, 0:n]),
                  reads=[f"PS{psn}"], writes=K("RTb"))
                E("vector", lambda V: V.tensor_tensor(
                    out=RTc[:, 0:n], in0=RTa[:, 0:n],
                    in1=cosT[0:64, tab_c0:tab_c0 + n], op=OP.mult),
                  reads=K("RTa", "cosT"), writes=K("RTc"))
                E("vector", lambda V: V.tensor_tensor(
                    out=RTd[:, 0:n], in0=RTb[:, 0:n],
                    in1=sinT[0:64, tab_c0:tab_c0 + n], op=OP.mult),
                  reads=K("RTb", "sinT"), writes=K("RTd"))
                E("vector", lambda V: V.tensor_tensor(
                    out=dst[0:64, dst_c0:dst_c0 + n], in0=RTc[:, 0:n], in1=RTd[:, 0:n],
                    op=OP.subtract),
                  reads=K("RTc", "RTd"), writes=K(dstnm))
                E("vector", lambda V: V.tensor_tensor(
                    out=RTc[:, 0:n], in0=RTb[:, 0:n],
                    in1=cosT[0:64, tab_c0:tab_c0 + n], op=OP.mult),
                  reads=K("RTb", "cosT"), writes=K("RTc"))
                E("vector", lambda V: V.tensor_tensor(
                    out=RTd[:, 0:n], in0=RTa[:, 0:n],
                    in1=sinT[0:64, tab_c0:tab_c0 + n], op=OP.mult),
                  reads=K("RTa", "sinT"), writes=K("RTd"))
                E("vector", lambda V: V.tensor_tensor(
                    out=dst[64:128, dst_c0:dst_c0 + n], in0=RTc[:, 0:n], in1=RTd[:, 0:n],
                    op=OP.add),
                  reads=K("RTc", "RTd"), writes=K(dstnm))

            rope_half(0, Q16, "Q16", 0, TO, 0)
            rope_half(1, K16, "K16", 0, TO, 0)
            rope_half(2, K16, "K16", TO, TO, TO)

            E("scalar", lambda S: S.activation(out=SQH[:, 0:TO], in_=Q16[:, :], func=AF.Square),
              reads=K("Q16"), writes=K("SQH"))

            def mm_mq(TE):
                for j in range(2):
                    TE.ldweights(ones16[:, :])
                    i = TE.matmul(out=PS[3][:, j * 512:(j + 1) * 512], lhsT=ones16[:, :],
                                  rhs=SQH[:, j * 512:(j + 1) * 512], start=True, stop=True)
                return i
            E("tensor", mm_mq, reads=K("SQH", "ones16"), writes=["PS3"])
            E("vector", lambda V: V.reduce_max(out=MQ[:, :], in_=PS[3][:, 0:TO], axis=AX.X),
              reads=["PS3"], writes=K("MQ"))
            E("scalar", lambda S: S.activation(out=SQH[:, :], in_=K16[:, :], func=AF.Square),
              reads=K("K16"), writes=K("SQH"))

            def mm_mk(TE):
                for half in range(2):
                    for j in range(2):
                        TE.ldweights(ones16[:, :])
                        i = TE.matmul(out=PS[3][:, j * 512:(j + 1) * 512], lhsT=ones16[:, :],
                                      rhs=SQH[:, half * TO + j * 512: half * TO + (j + 1) * 512],
                                      start=(half == 0), stop=(half == 1))
                return i
            E("tensor", mm_mk, reads=K("SQH", "ones16"), writes=["PS3"])
            E("vector", lambda V: V.reduce_max(out=MK[:, :], in_=PS[3][:, 0:TO], axis=AX.X),
              reads=["PS3"], writes=K("MK"))
            E("vector", lambda V: V.tensor_tensor(out=MQ[:, :], in0=MQ[:, :], in1=MK[:, :],
                                                  op=OP.mult),
              reads=K("MK", "MQ"), writes=K("MQ"))
            E("scalar", lambda S: S.activation(out=MK[:, :], in_=MQ[:, :], func=AF.Sqrt),
              reads=K("MQ"), writes=K("MK"))
            E("vector", lambda V: V.tensor_scalar(out=NEGM[:, :], in0=MK[:, :], scalar1=-RS,
                                                  scalar2=None, op0=OP.mult),
              reads=K("MK"), writes=K("NEGM"))

            def mm_s(TE, kt):
                TE.ldweights(K16[:, kt * 128:(kt + 1) * 128])
                for j in range(2):
                    i = TE.matmul(out=PS[1 + (kt % 2)][:, j * 512:(j + 1) * 512],
                                  lhsT=K16[:, kt * 128:(kt + 1) * 128],
                                  rhs=Q16[:, j * 512:(j + 1) * 512], start=True, stop=True)
                return i
            E("tensor", lambda TE: mm_s(TE, 0), reads=K("K16", "Q16"), writes=["PS1"])
            for kt in range(16):
                psn = 1 + (kt % 2)
                pb = PB[kt % 2]
                pbn = f"PB{kt % 2}"
                E("scalar", lambda S, psn=psn, pb=pb: S.activation(
                    out=pb[:, :], in_=PS[psn][:, 0:TO], func=AF.Exp, bias=NEGM[:, 0:1], scale=RS),
                  reads=[f"PS{psn}"] + K("NEGM"), writes=K(pbn))
                if kt < 15:
                    E("tensor", lambda TE, kt=kt: mm_s(TE, kt + 1), reads=K("K16", "Q16"),
                      writes=[f"PS{1 + ((kt + 1) % 2)}"])

                def mm_sum(TE, kt=kt, pb=pb):
                    TE.ldweights(ones16[:, :])
                    for j in range(2):
                        i = TE.matmul(out=PS[3][:, j * 512:(j + 1) * 512], lhsT=ones16[:, :],
                                      rhs=pb[:, j * 512:(j + 1) * 512],
                                      start=(kt == 0), stop=(kt == 15))
                    return i
                E("tensor", mm_sum, reads=K(pbn, "ones16"), writes=["PS3"])

                def mm_o(TE, kt=kt, hh=hh, pb=pb):
                    TE.ldweights(VT16[:, hh * T + kt * 128: hh * T + (kt + 1) * 128])
                    for j in range(2):
                        i = TE.matmul(out=PS[0][:, j * 512:(j + 1) * 512],
                                      lhsT=VT16[:, hh * T + kt * 128: hh * T + (kt + 1) * 128],
                                      rhs=pb[:, j * 512:(j + 1) * 512],
                                      start=(kt == 0), stop=(kt == 15))
                    return i
                E("tensor", mm_o, reads=K(pbn, "VT16"), writes=["PS0"])
                pb_pump(1)

            E("vector", lambda V: V.reciprocal(out=RECB[:, :], in_=PS[3][:, 0:TO]),
              reads=["PS3"], writes=K("RECB"))
            E("vector", lambda V, head=head: V.tensor_tensor(
                out=O16[:, head * TO:(head + 1) * TO], in0=PS[0][:, 0:TO], in1=RECB[:, :],
                op=OP.mult),
              reads=["PS0"] + K("RECB"), writes=K("O16"))
            pb_pump(2)
    tapk("o16", O16, ["O16"])

    pb_pump(10 ** 9)
    for hc in range(40):
        E("sync", lambda S, hc=hc: S.dma_start(out=XQHC0[:, hc * TO:(hc + 1) * TO],
                                               in_=xqh_scr[hc * 128:(hc + 1) * 128, :]),
          reads=["xqh_scr"], writes=K("XQHC0"), sem=f"xl{hc % 2}", inc=16)

    for dco in range(NDC):
        wb = dco % 2
        E("gpsimd", lambda G, dc=dco, wb=wb: G.dma_start(
            out=WOB[wb][:, :], in_=wout_b[dc * 128:(dc + 1) * 128, :]),
          writes=K(f"WOB{wb}"), sem=f"wo{wb}", inc=16)

        def mm_sp(TE, wb=wb):
            for oc in range(NDC):
                TE.ldweights(WOB[wb][:, oc * 128:(oc + 1) * 128])
                for j in range(2):
                    i = TE.matmul(out=PS[0][:, j * 512:(j + 1) * 512],
                                  lhsT=WOB[wb][:, oc * 128:(oc + 1) * 128],
                                  rhs=O16[:, oc * TO + j * 512: oc * TO + (j + 1) * 512],
                                  start=(oc == 0), stop=(oc == NDC - 1))
            return i
        E("tensor", mm_sp, reads=K(f"WOB{wb}", "O16"), writes=["PS0"])
        E("vector", lambda V, dc=dco: V.tensor_copy(out=S16[:, dc * TO:(dc + 1) * TO],
                                                    in_=PS[0][:, 0:TO]),
          reads=["PS0"], writes=K("S16"))
    tapk("s16", S16, ["S16"])
    for hc in range(40, NHC):
        E("sync", lambda S, hc=hc: S.dma_start(
            out=XQHC1[:, (hc - 40) * TO:(hc - 39) * TO],
            in_=xqh_scr[hc * 128:(hc + 1) * 128, :]),
          reads=["xqh_scr"], writes=K("XQHC1"), sem=f"xl{2 + hc % 2}", inc=16)

    def xqh_res(hc):
        if hc < 40:
            return XQHC0[:, hc * TO:(hc + 1) * TO], "XQHC0"
        return XQHC1[:, (hc - 40) * TO:(hc - 39) * TO], "XQHC1"

    for dc in range(NDC):
        b = dc % 2
        for hf in range(2):
            E("sync", lambda S, dc=dc, hf=hf: S.dma_start(
                out=DWC[hf][:, :],
                in_=dwA[:, dc * 5632 + hf * 2816: dc * 5632 + (hf + 1) * 2816]),
              writes=K(f"DWC{hf}"), sem=f"dwc{hf}", inc=16)
            for o, w in ((0, 1024), (1024, 1024), (2048, 768)):
                E("scalar", lambda S, hf=hf, o=o, w=w: S.activation(
                    out=UD1[:, 0:w], in_=DWC[hf][:, o:o + w], func=AF.Sign,
                    scale=SCAL[:, 3:4], bias=HLF[:, 1:2]),
                  reads=K(f"DWC{hf}", "SCAL", "HLF"), writes=K("UD1"))
                E("scalar", lambda S, hf=hf, o=o, w=w: S.activation(
                    out=UD2[:, 0:w], in_=DWC[hf][:, o:o + w], func=AF.Sign,
                    scale=SCAL[:, 3:4], bias=HLF[:, 0:1]),
                  reads=K(f"DWC{hf}", "SCAL", "HLF"), writes=K("UD2"))
                E("vector", lambda V, b=b, hf=hf, o=o, w=w: V.tensor_tensor(
                    out=WQDC[b][:, hf * 2816 + o: hf * 2816 + o + w],
                    in0=UD1[:, 0:w], in1=UD2[:, 0:w], op=OP.add),
                  reads=K("UD1", "UD2"), writes=K(f"WQDC{b}"))

        def mm_dn(TE, b=b):
            for hc in range(NHC):
                xt_, _ = xqh_res(hc)
                TE.ldweights(WQDC[b][:, hc * 128:(hc + 1) * 128])
                for j in range(2):
                    i = TE.matmul(out=PS[b][:, j * 512:(j + 1) * 512],
                                  lhsT=WQDC[b][:, hc * 128:(hc + 1) * 128],
                                  rhs=xt_[:, j * 512:(j + 1) * 512],
                                  start=(hc == 0), stop=(hc == NHC - 1))
            return i
        E("tensor", mm_dn, reads=K(f"WQDC{b}", "XQHC0", "XQHC1"), writes=[f"PS{b}"])
        E("sync", lambda S, dc=dc: S.dma_start(out=XR[:, :], in_=xchunk(dc)[:, 0:TO]),
          writes=K("XR"), sem="xr", inc=16)
        E("vector", lambda V, b=b: V.tensor_tensor(out=OBA[:, :], in0=PS[b][:, 0:TO],
                                                   in1=SCD[:, :], op=OP.mult),
          reads=[f"PS{b}"] + K("SCD"), writes=K("OBA"))
        E("scalar", lambda S: S.activation(out=OBB[:, :], in_=OBA[:, :], func=AF.Silu),
          reads=K("OBA"), writes=K("OBB"))
        E("gpsimd", lambda G, dc=dc: G.tensor_tensor(out=OBB[:, :], in0=OBB[:, :],
                                                     in1=S16[:, dc * TO:(dc + 1) * TO],
                                                     op=OP.mult),
          reads=K("S16", "OBB"), writes=K("OBB"))
        E("vector", lambda V: V.tensor_tensor(out=OBA[:, :], in0=OBB[:, :], in1=XR[:, :],
                                              op=OP.add),
          reads=K("XR", "OBB"), writes=K("OBA"))
        E("gpsimd", lambda G, dc=dc: G.dma_start(
            out=outT[dc * 128:(dc + 1) * 128, :], in_=OBA[:, :]),
          reads=K("OBA"), sem=f"ost{dc % 2}", inc=16)

    p.final_wait("gpsimd", "ost0", 16 * (NDC // 2))
    p.final_wait("gpsimd", "ost1", 16 * (NDC // 2))
    p.realize()
    return nc, taps


_BUILD_CACHE = {}


def _get_nc():
    key = tuple(sorted(DEBUG_TAPS))
    if key not in _BUILD_CACHE:
        _BUILD_CACHE[key] = build()
    return _BUILD_CACHE[key]


def stage_inputs(L0_attn_x, L0_attn_w_qkv, L0_attn_w_out, L0_ffn_gate_w, L0_ffn_down_w,
                 L0_ffn_gate_norm_w, L0_ffn_down_norm_w):
    x = np.asarray(L0_attn_x, np.float32)
    wqkvT = np.asarray(L0_attn_w_qkv, np.float32).T
    wqkv_b = np.ascontiguousarray(
        wqkvT[:, :2 * D].reshape(NDC, 128, 32, 128).transpose(2, 1, 0, 3).reshape(32 * 128, NDC * 128))
    wv_b = np.ascontiguousarray(
        wqkvT[:, 2 * D:].reshape(NDC, 128, 8, 256).transpose(2, 1, 0, 3).reshape(8 * 128, NDC * 256))
    woutT = np.asarray(L0_attn_w_out, np.float32).T
    wout_b = np.ascontiguousarray(
        woutT.reshape(NDC, 128, NDC, 128).transpose(2, 1, 0, 3).reshape(NDC * 128, NDC * 128))
    gwT = np.asarray(L0_ffn_gate_w, np.float32).T
    idx = []
    for j in range(NHC):
        idx.extend(range(j * 128, (j + 1) * 128))
        idx.extend(range(INTER + j * 128, INTER + (j + 1) * 128))
    gwTp = gwT[:, idx]
    gw_b = np.ascontiguousarray(
        gwTp.reshape(NDC, 128, NHC, 256).transpose(2, 1, 0, 3).reshape(NHC * 128, NDC * 256))
    dwTc = np.asarray(L0_ffn_down_w, np.float32).T
    dwA = np.ascontiguousarray(
        dwTc.reshape(NHC, 128, NDC, 128).transpose(1, 2, 0, 3).reshape(128, NDC * NHC * 128))
    g1t = np.ascontiguousarray(np.asarray(L0_ffn_gate_norm_w, np.float32).reshape(NDC, 128).T)
    g2t = np.ascontiguousarray(np.asarray(L0_ffn_down_norm_w, np.float32).reshape(NHC, 128).T)
    onesb16 = np.ones((128, 128), ml_dtypes.bfloat16)
    identf = np.eye(128, dtype=np.float32)
    cvec = np.tile(np.array([[EPS, 0.0]], np.float32), (128, 1))
    cosE, sinE = _rope_tables_f32()
    in_maps = []
    for c in range(8):
        pair, half = c // 2, c % 2
        xp = x[pair].T
        perm = np.concatenate([np.arange(half * TO, (half + 1) * TO),
                               np.arange((1 - half) * TO, (2 - half) * TO)])
        in_maps.append(dict(
            xT=np.ascontiguousarray(xp[:, perm]),
            wqkv_b=wqkv_b, wv_b=wv_b, wout_b=wout_b, gw_b=gw_b, dwA=dwA, g1t=g1t, g2t=g2t,
            onesb16=onesb16, identf=identf,
            cosTin=np.ascontiguousarray(cosE[perm].T),
            sinTin=np.ascontiguousarray(sinE[perm].T),
            cvec=cvec,
        ))
    return in_maps


def kernel(L0_attn_x, L0_attn_w_qkv, L0_attn_w_out, L0_ffn_gate_w, L0_ffn_down_w,
           L0_ffn_gate_norm_w, L0_ffn_down_norm_w):
    in_maps = stage_inputs(L0_attn_x, L0_attn_w_qkv, L0_attn_w_out, L0_ffn_gate_w,
                           L0_ffn_down_w, L0_ffn_gate_norm_w, L0_ffn_down_norm_w)
    nc, taps = _get_nc()
    res = run_bass_kernel_spmd(nc, in_maps, core_ids=list(range(8)))
    out = np.empty((4, T, D), np.float32)
    for c in range(8):
        pair, half = c // 2, c % 2
        out[pair, half * TO:(half + 1) * TO, :] = np.asarray(res.results[c]["outT"]).T
    if DEBUG_TAPS:
        kernel.tap_results = [
            {("tap_" + n): np.asarray(r.get("tap_" + n)) for n in taps} for r in res.results]
    return out


# version 17
# speedup vs baseline: 1.5844x; 1.0462x over previous
import math
import numpy as np
import ml_dtypes
import concourse.bass as bass
import concourse.mybir as mybir
from concourse.bass_utils import run_bass_kernel_spmd

F32 = mybir.dt.float32
BF16 = mybir.dt.bfloat16
I32 = mybir.dt.int32
AF = mybir.ActivationFunctionType
OP = mybir.AluOpType
AX = mybir.AxisListType

D = 2048
T = 2048
TO = 1024
H = 16
INTER = 5632
NDC = D // 128
NHC = INTER // 128
EPS = 1e-5
RS = 1.0 / math.sqrt(128.0)
LN1E4 = math.log(10000.0)
TWO_PI = 2.0 * math.pi
C1 = 6.28125
C2 = TWO_PI - C1
PI_F32 = float(np.float32(np.pi))
PI_LO = float(np.nextafter(np.float32(np.pi), np.float32(0.0)))

ENGINES = ("sync", "vector", "scalar", "tensor", "gpsimd")
ESEM = {"vector": "dv", "scalar": "ac", "tensor": "pe", "gpsimd": "gp", "sync": "sp"}

DEBUG_TAPS = set()

_INV_FREQ_BITS = np.array([1065353216,1063104471,1061157137,1059470816,1058010522,1056527311,1054337179,1052440603,1050798235,1049376004,1047712803,1045579757,1043732615,1042133056,1040747895,1038909398,1036831949,1035032952,1033475085,1032126029,1030116802,1028093503,1026341397,1024824136,1023510243,1021334738,1019364175,1017657739,1016180025,1014779187,1012562930,1010643729,1008981770,1007542573,1005959600,1003801109,1001931932,1000313292,998911607,997151246,995049016,993228559,991652109,990286959,988353832,986306396,984533390,982998030,981668463,979567074,977573004,975846212,974350870,973033388,970790693,968848599,967166816,965710450,964208661,962024420,960132947,958494998,957076594,955395298], dtype=np.uint32)


_TABLE_CACHE = {}


def _rope_tables_f32():
    if "t" in _TABLE_CACHE:
        return _TABLE_CACHE["t"]
    import os
    import subprocess
    import sys
    import tempfile
    inv = _INV_FREQ_BITS.view(np.float32)
    t = np.arange(T, dtype=np.float32)
    fr = np.einsum("i,j->ij", t, inv).astype(np.float32)
    emb = np.concatenate([fr, fr], axis=-1)
    cos = sin = None
    try:
        with tempfile.TemporaryDirectory() as td:
            np.save(os.path.join(td, "emb.npy"), emb)
            prog = ("import numpy as np, jax.numpy as jnp, sys, os;"
                    "td = sys.argv[1];"
                    "e = jnp.asarray(np.load(os.path.join(td, 'emb.npy')));"
                    "np.save(os.path.join(td, 'cos.npy'), np.asarray(jnp.cos(e)));"
                    "np.save(os.path.join(td, 'sin.npy'), np.asarray(jnp.sin(e)));"
                    "print('TBLOK')")
            env = {k: v for k, v in os.environ.items() if k != "PYTHONPATH"}
            env["JAX_PLATFORMS"] = "cpu"
            r = subprocess.run([sys.executable, "-c", prog, td], env=env,
                               capture_output=True, text=True, timeout=600)
            if "TBLOK" in r.stdout:
                cos = np.load(os.path.join(td, "cos.npy")).astype(np.float32)
                sin = np.load(os.path.join(td, "sin.npy")).astype(np.float32)
    except Exception:
        cos = None
    if cos is None:
        cos = np.cos(emb.astype(np.float64)).astype(np.float32)
        sin = np.sin(emb.astype(np.float64)).astype(np.float32)
    _TABLE_CACHE["t"] = (cos, sin)
    return cos, sin


class Prog:
    def __init__(self, nc, ctx):
        self.nc = nc
        self.ctx = ctx
        self.ops = {e: [] for e in ENGINES}
        self.sem_handles = {}
        self.counts = {}
        self.writers = {}
        self.readers = {}
        self.waited = {e: {} for e in ENGINES}

    def sem(self, name):
        if name not in self.sem_handles:
            self.sem_handles[name] = self.ctx.enter_context(self.nc.semaphore(name))
            self.counts[name] = 0
        return self.sem_handles[name]

    def _wait(self, eng, sname, val):
        if val <= 0 or self.waited[eng].get(sname, 0) >= val:
            return
        self.waited[eng][sname] = val
        h = self.sem_handles[sname]
        self.ops[eng].append(lambda E, h=h, v=val: E.wait_ge(h, v))

    def emit(self, eng, fn, reads=(), writes=(), sem=None, inc=1):
        for k in reads:
            for s, c in self.writers.get(k, {}).items():
                self._wait(eng, s, c)
        for k in writes:
            for s, c in self.writers.get(k, {}).items():
                self._wait(eng, s, c)
            for s, c in self.readers.get(k, {}).items():
                self._wait(eng, s, c)
        sname = sem or ESEM[eng]
        self.sem(sname)
        self.counts[sname] += inc
        cnt = self.counts[sname]
        h = self.sem_handles[sname]

        def run(E, fn=fn, h=h, inc=inc):
            fn(E).then_inc(h, inc)
        self.ops[eng].append(run)
        for k in writes:
            self.writers.setdefault(k, {})[sname] = cnt
            self.readers.pop(k, None)
        for k in reads:
            self.readers.setdefault(k, {})[sname] = cnt
        return sname, cnt

    def final_wait(self, eng, sname, val):
        h = self.sem(sname)
        self.ops[eng].append(lambda E, h=h, v=val: E.wait_ge(h, v))

    def realize(self):
        with self.nc.Block() as block:
            @block.sync
            def _(sync):
                for f in self.ops["sync"]:
                    f(sync)

            @block.vector
            def _(vector):
                for f in self.ops["vector"]:
                    f(vector)

            @block.scalar
            def _(scalar):
                for f in self.ops["scalar"]:
                    f(scalar)

            @block.tensor
            def _(tensor):
                for f in self.ops["tensor"]:
                    f(tensor)

            @block.gpsimd
            def _(gpsimd):
                for f in self.ops["gpsimd"]:
                    f(gpsimd)


def build():
    import contextlib
    nc = bass.Bass("TRN2", target_bir_lowering=False, debug=False)
    ctx = contextlib.ExitStack()
    p = Prog(nc, ctx)
    E = p.emit

    xT = nc.dram_tensor("xT", [D, T], F32, kind="ExternalInput").ap()
    wqkv_b = nc.dram_tensor("wqkv_b", [32 * 128, NDC * 128], F32, kind="ExternalInput").ap()
    wv_b = nc.dram_tensor("wv_b", [8 * 128, NDC * 256], F32, kind="ExternalInput").ap()
    wout_b = nc.dram_tensor("wout_b", [NDC * 128, NDC * 128], F32, kind="ExternalInput").ap()
    gw_b = nc.dram_tensor("gw_b", [NHC * 128, NDC * 256], F32, kind="ExternalInput").ap()
    dwA = nc.dram_tensor("dwA", [128, NDC * NHC * 128], F32, kind="ExternalInput").ap()
    g1t = nc.dram_tensor("g1t", [128, NDC], F32, kind="ExternalInput").ap()
    g2t = nc.dram_tensor("g2t", [128, NHC], F32, kind="ExternalInput").ap()
    onesb16 = nc.dram_tensor("onesb16", [128, 128], BF16, kind="ExternalInput").ap()
    identf = nc.dram_tensor("identf", [128, 128], F32, kind="ExternalInput").ap()
    cosTin = nc.dram_tensor("cosTin", [128, T], F32, kind="ExternalInput").ap()
    sinTin = nc.dram_tensor("sinTin", [128, T], F32, kind="ExternalInput").ap()
    cvec = nc.dram_tensor("cvec", [128, 2], F32, kind="ExternalInput").ap()

    outT = nc.dram_tensor("outT", [D, TO], F32, kind="ExternalOutput").ap()
    h_scr = nc.dram_tensor("h_scr", [INTER, TO], F32, kind="Internal").ap()
    xqh_scr = nc.dram_tensor("xqh_scr", [INTER, TO], BF16, kind="Internal").ap()
    sc_scr = nc.dram_tensor("sc_scr", [1, 4], F32, kind="Internal").ap()

    akey = {}

    def place(name, shape, dt, off, keys):
        t = nc.alloc_sbuf_tensor_at(name, shape, dt, offset=off)
        akey[name] = (keys,) if isinstance(keys, str) else tuple(keys)
        return t

    Z0 = 16544
    Z1 = Z0 + 81920
    Z2 = Z1 + 32768
    Z3 = Z2 + 32768
    Z4 = Z3 + 32768
    Z5 = Z4 + 24576
    XN16 = place("XN16", [128, NDC * T], BF16, Z0, "z0a")
    cosT = place("cosT", [128, T], F32, Z0 + 65536, "z0b")
    sinT = place("sinT", [128, T], F32, Z0 + 73728, "z0c")
    GWF = [place("GWF0", [128, NDC * 256], F32, Z1, ("z1a", "z1b")),
           place("GWF1", [128, NDC * 256], F32, Z1 + 16384,
                 ("z1c", "z1d", "z1e", "z1f"))]
    WQ16 = [place("WQ16_0", [128, NDC * 256], BF16, Z4, ("z4a", "z4b")),
            place("WQ16_1", [128, NDC * 256], BF16, Z4 + 16384, ("z4e", "z4f"))]
    U1 = place("U1", [128, 2048], F32, Z0 + 65536, "z0b")
    U2 = place("U2", [128, 2048], F32, Z0 + 73728, "z0c")
    RSTD = place("RSTD", [128, T], F32, Z1, "z1a")
    MUSC = place("MUSC", [128, T], F32, Z1 + 8192, "z1b")
    RRMS = place("RRMS", [128, TO], F32, Z1 + 16384, "z1c")
    CG = place("CG", [128, TO], F32, Z1 + 20480, "z1d")
    MM1 = place("MM1", [128, TO], F32, Z1 + 24576, "z1e")
    K16 = place("K16", [128, T], BF16, Z1, "z1a")
    VT16 = place("VT16", [128, 2 * T], BF16, Z1 + 4096, ("z1a", "z1b"))
    Q16 = place("Q16", [128, TO], BF16, Z1 + 12288, "z1b")
    SQH = place("SQH", [128, T], BF16, Z1 + 14336, ("z1b", "z1c"))
    RECB = place("RECB", [128, TO], F32, Z1 + 18432, ("z1c", "z1d"))
    LNS = place("LNS", [128, TO], F32, Z1 + 22528, ("z1d", "z1e"))
    PB = [place("PB0", [128, TO], BF16, Z1 + 26624, "z1e"),
          place("PB1", [128, TO], BF16, Z1 + 28672, "z1f")]
    S16 = place("S16", [128, NDC * TO], BF16, Z1,
                ("z1a", "z1b", "z1c", "z1d", "z1e", "z1f"))
    XB = place("XB", [128, T], F32, Z2, "z2a")
    XB1 = place("XB1", [128, T], F32, Z3, ("z3a",))
    SQB = place("SQB", [128, T], F32, Z2 + 8192, "z2b")
    TMP1 = place("TMP1", [128, T], F32, Z2 + 16384, "z2c")
    TMP2 = place("TMP2", [128, T], F32, Z2 + 24576, "z2d")
    QT = place("QT", [128, TO], F32, Z2 + 8192, "z2b")
    QI = place("QI", [128, TO], I32, Z2 + 12288, "z2b")
    QC = place("QC", [128, TO], F32, Z2 + 16384, "z2c")
    QN = place("QN", [128, TO], F32, Z2 + 20480, "z2c")
    QF = place("QF", [128, TO], F32, Z2 + 24576, "z2d")
    QD = place("QD", [128, TO], F32, Z2 + 28672, "z2d")
    WQK = [place("WQK0", [128, NDC * 128], BF16, Z2, "z2a"),
           place("WQK1", [128, NDC * 128], BF16, Z2 + 4096, "z2a")]
    LQD = place("LQD", [128, TO], F32, Z2 + 8192, "z2b")
    LQC = place("LQC", [128, TO], F32, Z2 + 12288, "z2b")
    WVG = [place("WVG0", [128, NDC * 256], BF16, Z2 + 16384, "z2c"),
           place("WVG1", [128, NDC * 256], BF16, Z2 + 24576, "z2d")]
    WOB = [place("WOB0", [128, NDC * 128], BF16, Z2, "z2a"),
           place("WOB1", [128, NDC * 128], BF16, Z2 + 4096, "z2a")]
    EV1 = place("EV1", [128, TO], F32, Z2, "z2a")
    EV2 = place("EV2", [128, TO], F32, Z2 + 4096, "z2a")
    EV3 = [place("EV3a", [128, TO], F32, Z2 + 8192, "z2b"),
           place("EV3b", [128, TO], F32, Z2 + 12288, "z2b")]
    HB = place("HB", [128, TO], F32, Z4 + 4096, "z4b")
    LYH1 = place("LYH1", [128, TO], F32, Z4 + 8192, "z4c")
    LQF = place("LQF", [128, TO], F32, Z4 + 12288, "z4d")
    YH1 = place("YH1", [128, TO], F32, Z2 + 8192, "z2b")
    YH2 = place("YH2", [128, TO], F32, Z2 + 12288, "z2b")
    HS1 = place("HS1", [128, TO], F32, Z2 + 16384, "z2c")
    HS2 = place("HS2", [128, TO], F32, Z2 + 20480, "z2c")
    YH2p = place("YH2p", [128, TO], F32, Z2 + 16384, "z2c")
    HSQA = place("HSQA", [128, TO], F32, Z2 + 20480, "z2c")
    SQF = place("SQF", [128, TO], F32, Z2 + 24576, "z2d")
    XQHC0 = place("XQHC0", [128, 40 * TO], BF16, Z0, ("z0a", "z0b", "z0c"))
    XQHC1 = place("XQHC1", [128, 4 * TO], BF16, Z3, "z3a")
    DWC = [place("DWC0", [128, 2816], F32, Z2, ("z2a", "z2b")),
           place("DWC1", [128, 2816], F32, Z2 + 11264, ("z2b", "z2c"))]
    UD1 = place("UD1", [128, 1024], F32, Z2 + 22528, ("z2c", "z2d"))
    UD2 = place("UD2", [128, 1024], F32, Z2 + 26624, "z2d")
    WQDC = [place("WQDC0", [128, 5632], BF16, Z4, ("z4a", "z4b", "z4c")),
            place("WQDC1", [128, 5632], BF16, Z3 + 20480, "z3b")]
    OBA = place("OBA", [128, TO], F32, Z4 + 12288, "z4d")
    OBB = place("OBB", [128, TO], F32, Z4 + 16384, "z4e")
    XR = place("XR", [128, TO], F32, Z3 + 16384, "z3b")
    XQ = place("XQ", [128, NDC * TO], BF16, Z3, ("z3a", "z3b"))
    O16 = place("O16", [128, H * TO], BF16, Z3, ("z3a", "z3b"))
    SQ16 = place("SQ16", [128, T], BF16, Z4, "z4a")
    XB16 = place("XB16", [128, T], BF16, Z4 + 4096, "z4b")
    WRB = [place("WRB0", [128, 2048], BF16, Z4 + 8192, "z4c"),
           place("WRB1", [128, 2048], BF16, Z4 + 12288, "z4d")]
    RTa = place("RTa", [64, TO], F32, Z4 + 16384, "z4e")
    RTb = place("RTb", [64, TO], F32, Z1 + 22528, ("z1d", "z1e"))
    RTc = place("RTc", [64, TO], F32, Z1 + 18432, ("z1c", "z1d"))
    RTd = place("RTd", [64, TO], F32, Z1 + 14336, ("z1b", "z1c"))
    SCG = place("SCG", [128, TO], F32, Z4 + 8192, "z4c")
    MMH = place("MMH", [128, TO], F32, Z4 + 12288, "z4d")
    CCH = place("CCH", [128, TO], F32, Z4, "z4a")
    SCD = place("SCD", [128, TO], F32, Z4 + 20480, "z4f")
    z5 = [Z5]

    def small(name, shape, dt=F32):
        t = place(name, shape, dt, z5[0], name)
        sz = shape[1] * {F32: 4, BF16: 2, I32: 4}[dt]
        z5[0] += (sz + 31) // 32 * 32
        return t

    g1s = small("g1s", [128, NDC])
    g2s = small("g2s", [128, NHC])
    ones16 = small("ones16", [128, 128], BF16)
    idf = small("idf", [128, 128])
    epsv = small("epsv", [128, 2])
    SCAL = small("SCAL", [128, 4])
    SCLT = small("SCLT", [1, 4])
    MCOMP = small("MCOMP", [128, 8])
    MQ = small("MQ", [128, 1])
    MK = small("MK", [128, 1])
    NEGM = small("NEGM", [128, 1])
    WACC = small("WACC", [128, 2])
    WS1 = small("WS1", [128, 1])
    WS2 = small("WS2", [128, 1])
    WS1D = small("WS1D", [128, 1])
    WS2D = small("WS2D", [128, 1])
    BCH = small("BCH", [128, 128])
    HLF = small("HLF", [128, 2])
    ONESF = small("ONESF", [128, 128])
    LADB = small("LADB", [128, 14])
    DEN_G = small("DEN_G", [128, TO])
    XQHT = [place("XQHT0", [128, TO], BF16, z5[0] - 4096, "DEN_G"),
            place("XQHT1", [128, TO], BF16, z5[0] - 2048, "DEN_G")]
    assert z5[0] <= 229376, z5[0]

    PS = [nc.alloc_psum_tensor(f"PS{i}", [128, TO], F32) for i in range(4)]

    def K(*names):
        out = []
        for n in names:
            out.extend(akey.get(n, (n,)))
        return out

    taps = []

    def tapk(name, tile, keys, rows=128, cols=None):
        if name not in DEBUG_TAPS:
            return
        c = cols if cols is not None else tile.shape[1]
        dt = tile.dtype
        o = nc.dram_tensor("tap_" + name, [rows, c], dt, kind="ExternalOutput").ap()
        taps.append(name)
        E("gpsimd", lambda G, o=o, t=tile, r=rows, c=c: G.dma_start(out=o, in_=t[0:r, 0:c]),
          reads=K(*keys), sem=f"tp{len(taps)}", inc=16)

    for ci, (dst, srcp, nm) in enumerate([(g1s, g1t, "g1s"), (g2s, g2t, "g2s"),
                                          (ones16, onesb16, "ones16"), (idf, identf, "idf"),
                                          (epsv, cvec, "epsv")]):
        E("sync", lambda S, d=dst, s=srcp: S.dma_start(out=d[:, :], in_=s),
          writes=K(nm), sem=f"cs{ci}", inc=16)

    for nm in ("WS1", "WS2", "WS1D", "WS2D"):
        t = {"WS1": WS1, "WS2": WS2, "WS1D": WS1D, "WS2D": WS2D}[nm]
        E("vector", lambda V, t=t: V.memset(t[:, :], 0.0), writes=K(nm))
    gw_flat = gw_b.rearrange("a (b c) -> (a b) c", b=2)
    wred_list = [(gw_flat[rg * 128:(rg + 1) * 128, :], 2048) for rg in range(NHC * 2)]
    wred_list += [(dwA[:, rg * 2048:(rg + 1) * 2048], 2048) for rg in range(NHC)]
    NGW = NHC * 2

    for idxw, (apw, w) in enumerate(wred_list):
        b = idxw % 2
        E("gpsimd", lambda G, b=b, apw=apw, w=w: G.dma_start(out=WRB[b][:, 0:w], in_=apw),
          writes=K(f"WRB{b}"), sem=f"wr{b}", inc=16)
        awt = SQ16 if b == 0 else XB16
        awk = "SQ16" if b == 0 else "XB16"
        E("scalar", lambda S, b=b, w=w, awt=awt: S.activation(out=awt[:, 0:w],
                                                              in_=WRB[b][:, 0:w],
                                                              func=AF.Abs),
          reads=K(f"WRB{b}"), writes=K(awk))
        isgw = idxw < NGW
        ps_lo = 0 if isgw else 2
        st = (idxw == 0) if isgw else (idxw == NGW)
        sp = (idxw == NGW - 1) if isgw else (idxw == len(wred_list) - 1)

        def mm_wred(TE, w=w, ps_lo=ps_lo, st=st, sp=sp, awt=awt):
            nj = (w + 511) // 512
            for j in range(nj):
                wj = min(512, w - j * 512)
                TE.ldweights(ones16[:, :])
                i = TE.matmul(out=PS[ps_lo + j // 2][:, (j % 2) * 512:(j % 2) * 512 + wj],
                              lhsT=ones16[:, :], rhs=awt[:, j * 512:j * 512 + wj],
                              start=st, stop=sp)
            return i
        E("tensor", mm_wred, reads=K(awk, "ones16"),
          writes=[f"PS{ps_lo}", f"PS{ps_lo + 1}"])
    E("vector", lambda V: V.reduce_sum(out=WS1[:, :], in_=PS[0][:, :], axis=AX.X),
      reads=["PS0"], writes=K("WS1"))
    E("vector", lambda V: V.reduce_sum(out=WS2[:, :], in_=PS[1][:, :], axis=AX.X),
      reads=["PS1"], writes=K("WS2"))
    E("vector", lambda V: V.reduce_sum(out=WS1D[:, :], in_=PS[2][:, :], axis=AX.X),
      reads=["PS2"], writes=K("WS1D"))
    E("vector", lambda V: V.reduce_sum(out=WS2D[:, :], in_=PS[3][:, :], axis=AX.X),
      reads=["PS3"], writes=K("WS2D"))
    E("vector", lambda V: V.tensor_tensor(out=WACC[:, 0:1], in0=WS1[:, :], in1=WS2[:, :],
                                          op=OP.add),
      reads=K("WS1", "WS2"), writes=K("WACC"))
    E("vector", lambda V: V.tensor_tensor(out=WACC[:, 1:2], in0=WS1D[:, :], in1=WS2D[:, :],
                                          op=OP.add),
      reads=K("WS1D", "WS2D"), writes=K("WACC"))
    E("vector", lambda V: V.tensor_scalar(out=SCAL[:, 0:1], in0=WACC[:, 0:1],
                                          scalar1=1.0 / (D * 2 * INTER), scalar2=EPS,
                                          op0=OP.mult, op1=OP.max),
      reads=K("WACC"), writes=K("SCAL"))
    E("vector", lambda V: V.tensor_scalar(out=SCAL[:, 1:2], in0=WACC[:, 1:2],
                                          scalar1=1.0 / (INTER * D), scalar2=EPS,
                                          op0=OP.mult, op1=OP.max),
      reads=K("WACC"), writes=K("SCAL"))
    E("vector", lambda V: V.reciprocal(out=SCAL[:, 2:3], in_=SCAL[:, 0:1]),
      reads=K("SCAL"), writes=K("SCAL"))
    E("vector", lambda V: V.reciprocal(out=SCAL[:, 3:4], in_=SCAL[:, 1:2]),
      reads=K("SCAL"), writes=K("SCAL"))

    E("vector", lambda V: V.memset(HLF[:, 0:1], 0.5), writes=K("HLF"))
    E("vector", lambda V: V.memset(ONESF[:, :], 1.0), writes=K("ONESF"))
    E("vector", lambda V: V.memset(HLF[:, 1:2], -0.5), writes=K("HLF"))
    for i, m in enumerate((64.0, 32.0, 16.0, 8.0, 4.0, 2.0, 1.0)):
        E("vector", lambda V, i=i, m=m: V.memset(LADB[:, 2 * i:2 * i + 1], m - 0.5),
          writes=K("LADB"))
        E("vector", lambda V, i=i, m=m: V.memset(LADB[:, 2 * i + 1:2 * i + 2], -(m - 0.5)),
          writes=K("LADB"))

    def xchunk(dc):
        return xT[dc * 128:(dc + 1) * 128, :]

    for dc in range(NDC):
        xb = XB if dc % 2 == 0 else XB1
        xbn = "XB" if dc % 2 == 0 else "XB1"
        E("sync", lambda S, dc=dc, xb=xb: S.dma_start(out=xb[:, :], in_=xchunk(dc)),
          writes=K(xbn), sem=f"xb{dc % 2}", inc=16)
        E("scalar", lambda S, xb=xb: S.activation(out=SQB[:, :], in_=xb[:, :], func=AF.Square),
          reads=K(xbn), writes=K("SQB"))
        E("vector", lambda V, xb=xb: V.tensor_copy(out=XB16[:, :], in_=xb[:, :]),
          reads=K(xbn), writes=K("XB16"))
        E("vector", lambda V: V.tensor_copy(out=SQ16[:, :], in_=SQB[:, :]),
          reads=K("SQB"), writes=K("SQ16"))
        st, sp = dc == 0, dc == NDC - 1

        def mm_stats(TE, st=st, sp=sp):
            for half in range(2):
                for j in range(2):
                    TE.ldweights(ones16[:, :])
                    TE.matmul(out=PS[half][:, 512 * j:512 * (j + 1)], lhsT=ones16[:, :],
                              rhs=XB16[:, half * TO + 512 * j: half * TO + 512 * (j + 1)],
                              start=st, stop=sp)
                    TE.ldweights(ones16[:, :])
                    i = TE.matmul(out=PS[2 + half][:, 512 * j:512 * (j + 1)], lhsT=ones16[:, :],
                                  rhs=SQ16[:, half * TO + 512 * j: half * TO + 512 * (j + 1)],
                                  start=st, stop=sp)
            return i
        E("tensor", mm_stats, reads=K("XB16", "SQ16", "ones16"),
          writes=["PS0", "PS1", "PS2", "PS3"])

    MUt, VARt = XB, SQB
    for half in range(2):
        E("vector", lambda V, h=half: V.tensor_scalar(out=MUt[:, h * TO:(h + 1) * TO],
                                                      in0=PS[h][:, :], scalar1=1.0 / D,
                                                      scalar2=None, op0=OP.mult),
          reads=[f"PS{half}"], writes=K("XB"))
        E("vector", lambda V, h=half: V.tensor_scalar(out=VARt[:, h * TO:(h + 1) * TO],
                                                      in0=PS[2 + h][:, :], scalar1=1.0 / D,
                                                      scalar2=None, op0=OP.mult),
          reads=[f"PS{2 + half}"], writes=K("SQB"))
    E("vector", lambda V: V.tensor_tensor(out=TMP1[:, :], in0=MUt[:, :], in1=MUt[:, :], op=OP.mult),
      reads=K("XB"), writes=K("TMP1"))
    E("vector", lambda V: V.tensor_tensor(out=VARt[:, :], in0=VARt[:, :], in1=TMP1[:, :],
                                          op=OP.subtract),
      reads=K("TMP1", "SQB"), writes=K("SQB"))
    E("scalar", lambda S: S.activation(out=TMP1[:, :], in_=VARt[:, :], func=AF.Ln,
                                       bias=epsv[:, 0:1]),
      reads=K("SQB", "epsv"), writes=K("TMP1"))
    E("scalar", lambda S: S.activation(out=RSTD[:, :], in_=TMP1[:, :], func=AF.Exp, scale=-0.5),
      reads=K("TMP1"), writes=K("RSTD"))
    E("vector", lambda V: V.tensor_tensor(out=MUSC[:, :], in0=MUt[:, :], in1=RSTD[:, :],
                                          op=OP.mult),
      reads=K("XB", "RSTD"), writes=K("MUSC"))
    E("vector", lambda V: V.tensor_scalar(out=TMP1[:, 0:TO], in0=VARt[:, 0:TO], scalar1=EPS,
                                          scalar2=None, op0=OP.add),
      reads=K("SQB"), writes=K("TMP1"))
    E("vector", lambda V: V.reciprocal(out=MM1[:, :], in_=TMP1[:, 0:TO]),
      reads=K("TMP1"), writes=K("MM1"))
    E("vector", lambda V: V.tensor_tensor(out=TMP2[:, 0:TO], in0=VARt[:, 0:TO], in1=MM1[:, :],
                                          op=OP.mult),
      reads=K("SQB", "MM1"), writes=K("TMP2"))
    E("scalar", lambda S: S.activation(out=TMP1[:, 0:TO], in_=TMP2[:, 0:TO], func=AF.Ln,
                                       bias=epsv[:, 0:1]),
      reads=K("TMP2", "epsv"), writes=K("TMP1"))
    E("scalar", lambda S: S.activation(out=RRMS[:, :], in_=TMP1[:, 0:TO], func=AF.Exp, scale=-0.5),
      reads=K("TMP1"), writes=K("RRMS"))
    tapk("rstd", RSTD, ["RSTD"])
    tapk("rrms", RRMS, ["RRMS"])

    for dc in range(NDC):
        xb = XB if dc % 2 == 0 else XB1
        xbn = "XB" if dc % 2 == 0 else "XB1"
        E("sync", lambda S, dc=dc, xb=xb: S.dma_start(out=xb[:, :], in_=xchunk(dc)),
          writes=K(xbn), sem=f"xb{dc % 2}", inc=16)
        E("vector", lambda V, xb=xb: V.tensor_tensor(out=TMP1[:, :], in0=xb[:, :], in1=RSTD[:, :],
                                              op=OP.mult),
          reads=K(xbn, "RSTD"), writes=K("TMP1"))
        E("vector", lambda V: V.tensor_tensor(out=TMP2[:, :], in0=TMP1[:, :], in1=MUSC[:, :],
                                              op=OP.subtract),
          reads=K("TMP1", "MUSC"), writes=K("TMP2"))
        E("vector", lambda V, dc=dc: V.tensor_copy(out=XN16[:, dc * T:(dc + 1) * T],
                                                   in_=TMP2[:, :]),
          reads=K("TMP2"), writes=K("XN16"))
        E("vector", lambda V, dc=dc: V.tensor_scalar(out=QT[:, :], in0=TMP2[:, 0:TO],
                                                     scalar1=g1s[:, dc:dc + 1], scalar2=None,
                                                     op0=OP.mult),
          reads=K("TMP2", "g1s"), writes=K("QT"))
        E("scalar", lambda S: S.activation(out=QF[:, :], in_=QT[:, :], func=AF.Abs),
          reads=K("QT"), writes=K("QF"))
        if dc == 0:
            E("vector", lambda V: V.tensor_copy(out=MM1[:, :], in_=QF[:, :]),
              reads=K("QF"), writes=K("MM1"))
        else:
            E("vector", lambda V: V.tensor_tensor(out=MM1[:, :], in0=MM1[:, :], in1=QF[:, :],
                                                  op=OP.max),
              reads=K("QF", "MM1"), writes=K("MM1"))
    tapk("xn16", XN16, ["XN16"])

    def compact_max(src, srckeys, dst, dstnm):
        for blk in range(8):
            E("tensor", lambda TE, blk=blk: TE.transpose(
                out=PS[0][:, 0:128], in_=src[:, blk * 128:(blk + 1) * 128], identity=idf[:, :]),
              reads=K(*srckeys) + K("idf"), writes=["PS0"])
            E("vector", lambda V, blk=blk: V.reduce_max(out=dst[:, blk:blk + 1],
                                                        in_=PS[0][:, 0:128], axis=AX.X),
              reads=["PS0"], writes=K(dstnm))

    def bcast_cols(srccomp, srcnm, dst, dstkeys, colbase, nblk=8):
        for blk in range(nblk):
            E("vector", lambda V, blk=blk: V.tensor_copy(
                out=BCH[:, :], in_=srccomp[:, blk:blk + 1].broadcast_to([128, 128])),
              reads=K(srcnm), writes=K("BCH"))
            E("tensor", lambda TE: TE.transpose(out=PS[0][:, 0:128], in_=BCH[:, :],
                                                identity=idf[:, :]),
              reads=K("BCH", "idf"), writes=["PS0"])
            E("vector", lambda V, blk=blk: V.tensor_copy(
                out=dst[:, colbase + blk * 128: colbase + (blk + 1) * 128], in_=PS[0][:, 0:128]),
              reads=["PS0"], writes=K(*dstkeys))

    compact_max(MM1, ["MM1"], MCOMP, "MCOMP")
    bcast_cols(MCOMP, "MCOMP", TMP1, ["TMP1"], 0)
    E("vector", lambda V: V.tensor_tensor(out=QT[:, :], in0=RRMS[:, :], in1=TMP1[:, 0:TO],
                                          op=OP.mult),
      reads=K("RRMS", "TMP1"), writes=K("QT"))
    E("vector", lambda V: V.tensor_scalar(out=DEN_G[:, :], in0=QT[:, :], scalar1=EPS,
                                          scalar2=None, op0=OP.max),
      reads=K("QT"), writes=K("DEN_G"))
    E("vector", lambda V: V.reciprocal(out=QF[:, :], in_=DEN_G[:, :]),
      reads=K("DEN_G"), writes=K("QF"))
    E("vector", lambda V: V.tensor_tensor(out=QT[:, :], in0=RRMS[:, :], in1=QF[:, :], op=OP.mult),
      reads=K("RRMS", "QF"), writes=K("QT"))
    E("vector", lambda V: V.tensor_scalar(out=CG[:, :], in0=QT[:, :], scalar1=127.0,
                                          scalar2=None, op0=OP.mult),
      reads=K("QT"), writes=K("CG"))
    tapk("cg", CG, ["CG"])
    tapk("deng", DEN_G, ["DEN_G"])

    def cast_correct(src_t, srckeys, dst, dstkeys, dst_c0):
        E("vector", lambda V: V.tensor_copy(out=QI[:, :], in_=src_t[:, 0:TO]),
          reads=K(*srckeys), writes=K("QI"))
        E("vector", lambda V: V.tensor_scalar(out=QF[:, :], in0=QI[:, :], scalar1=-128.0,
                                              scalar2=127.0, op0=OP.max, op1=OP.min),
          reads=K("QI"), writes=K("QF"))
        E("gpsimd", lambda G: G.tensor_tensor(out=QD[:, :], in0=src_t[:, 0:TO], in1=QF[:, :],
                                              op=OP.subtract),
          reads=K(*srckeys) + K("QF"), writes=K("QD"))
        E("vector", lambda V: V.tensor_scalar(out=QC[:, :], in0=QD[:, :], scalar1=0.5,
                                              scalar2=None, op0=OP.is_ge),
          reads=K("QD"), writes=K("QC"))
        E("vector", lambda V: V.scalar_tensor_tensor(out=QN[:, :], in0=QD[:, :], scalar=-0.5,
                                                     in1=QC[:, :], op0=OP.is_le, op1=OP.subtract),
          reads=K("QD", "QC"), writes=K("QN"))
        E("gpsimd", lambda G: G.tensor_tensor(out=dst[:, dst_c0:dst_c0 + TO], in0=QF[:, :],
                                              in1=QN[:, :], op=OP.subtract),
          reads=K("QF", "QN"), writes=K(*dstkeys))

    def ladder_round(src_t, srckeys, dst, dstkeys, dst_c0, eng2="vector"):
        E("vector", lambda V: V.tensor_copy(out=QD[:, :], in_=src_t[:, 0:TO]),
          reads=K(*srckeys), writes=K("QD"))
        for i, m in enumerate((64.0, 32.0, 16.0, 8.0, 4.0, 2.0, 1.0)):
            E(eng2, lambda X, m=m: X.tensor_scalar(out=QC[:, :], in0=QD[:, :], scalar1=m - 0.5,
                                                   scalar2=None, op0=OP.is_ge),
              reads=K("QD"), writes=K("QC"))
            E("vector", lambda X, m=m: X.scalar_tensor_tensor(out=QC[:, :], in0=QD[:, :],
                                                              scalar=-(m - 0.5), in1=QC[:, :],
                                                              op0=OP.is_le, op1=OP.subtract),
              reads=K("QD", "QC"), writes=K("QC"))
            E("vector", lambda X, m=m: X.scalar_tensor_tensor(out=QD[:, :], in0=QC[:, :], scalar=m,
                                                              in1=QD[:, :], op0=OP.mult, op1=OP.add),
              reads=K("QC", "QD"), writes=K("QD"))
        E("vector", lambda V: V.tensor_tensor(out=QF[:, :], in0=src_t[:, 0:TO], in1=QD[:, :],
                                              op=OP.subtract),
          reads=K(*srckeys) + K("QD"), writes=K("QF"))
        E("vector", lambda V: V.tensor_scalar(out=dst[:, dst_c0:dst_c0 + TO], in0=QF[:, :],
                                              scalar1=-128.0, scalar2=127.0, op0=OP.max,
                                              op1=OP.min),
          reads=K("QF"), writes=K(*dstkeys))

    for dc in range(NDC):
        E("sync", lambda S, dc=dc: S.dma_start(out=XB[:, 0:TO], in_=xchunk(dc)[:, 0:TO]),
          writes=K("XB"), sem="xb", inc=16)
        E("vector", lambda V: V.tensor_tensor(out=TMP1[:, 0:TO], in0=XB[:, 0:TO],
                                              in1=RSTD[:, 0:TO], op=OP.mult),
          reads=K("XB", "RSTD"), writes=K("TMP1"))
        E("vector", lambda V: V.tensor_tensor(out=TMP2[:, 0:TO], in0=TMP1[:, 0:TO],
                                              in1=MUSC[:, 0:TO], op=OP.subtract),
          reads=K("TMP1", "MUSC"), writes=K("TMP2"))
        E("vector", lambda V, dc=dc: V.tensor_scalar(out=QT[:, :], in0=TMP2[:, 0:TO],
                                                     scalar1=g1s[:, dc:dc + 1], scalar2=None,
                                                     op0=OP.mult),
          reads=K("TMP2", "g1s"), writes=K("QT"))
        E("vector", lambda V: V.tensor_tensor(out=QT[:, :], in0=QT[:, :], in1=CG[:, :],
                                              op=OP.mult),
          reads=K("CG", "QT"), writes=K("QT"))
        cast_correct(QT, ["QT"], XQ, ["XQ"], dc * TO)
    tapk("xq", XQ, ["XQ"])

    E("vector", lambda V: V.tensor_scalar(out=SCG[:, :], in0=DEN_G[:, :], scalar1=SCAL[:, 0:1],
                                          scalar2=0.5 / 127.0, op0=OP.mult, op1=OP.mult),
      reads=K("DEN_G", "SCAL"), writes=K("SCG"))

    def hw_tile(jp):
        return h_scr[jp * 128:(jp + 1) * 128, :]

    def gate_load_quant(jp):
        gb = jp % 2
        E("sync", lambda S, jp=jp, gb=gb: S.dma_start(
            out=GWF[gb][:, :], in_=gw_b[jp * 128:(jp + 1) * 128, :]),
          writes=K(f"GWF{gb}"), sem=f"gw{gb}", inc=16)
        for hf in range(2):
            E("scalar", lambda S, gb=gb, hf=hf: S.activation(
                out=U1[:, :], in_=GWF[gb][:, hf * 2048:(hf + 1) * 2048],
                func=AF.Sign, scale=SCAL[:, 2:3], bias=HLF[:, 1:2]),
              reads=K(f"GWF{gb}", "SCAL", "HLF"), writes=K("U1"))
            E("scalar", lambda S, gb=gb, hf=hf: S.activation(
                out=U2[:, :], in_=GWF[gb][:, hf * 2048:(hf + 1) * 2048],
                func=AF.Sign, scale=SCAL[:, 2:3], bias=HLF[:, 0:1]),
              reads=K(f"GWF{gb}", "SCAL", "HLF"), writes=K("U2"))
            E("vector", lambda V, gb=gb, hf=hf: V.tensor_tensor(
                out=WQ16[gb][:, hf * 2048:(hf + 1) * 2048], in0=U1[:, :], in1=U2[:, :],
                op=OP.add),
              reads=K("U1", "U2"), writes=K(f"WQ16_{gb}"))

    gate_load_quant(0)
    for jp in range(NHC):
        gb = jp % 2
        psg, psgn = PS[2 * gb], f"PS{2 * gb}"
        psu, psun = PS[2 * gb + 1], f"PS{2 * gb + 1}"
        ev = EV3[gb]
        evn = "EV3a" if gb == 0 else "EV3b"
        if jp + 1 < NHC:
            gate_load_quant(jp + 1)

        def mm_gate(TE, gb=gb, psg=psg, psu=psu):
            for dc in range(NDC):
                TE.ldweights(WQ16[gb][:, dc * 256: dc * 256 + 128])
                for j in range(2):
                    TE.matmul(out=psg[:, j * 512:(j + 1) * 512],
                              lhsT=WQ16[gb][:, dc * 256: dc * 256 + 128],
                              rhs=XQ[:, dc * TO + j * 512: dc * TO + (j + 1) * 512],
                              start=(dc == 0), stop=(dc == NDC - 1))
                TE.ldweights(WQ16[gb][:, dc * 256 + 128:(dc + 1) * 256])
                for j in range(2):
                    i = TE.matmul(out=psu[:, j * 512:(j + 1) * 512],
                                  lhsT=WQ16[gb][:, dc * 256 + 128:(dc + 1) * 256],
                                  rhs=XQ[:, dc * TO + j * 512: dc * TO + (j + 1) * 512],
                                  start=(dc == 0), stop=(dc == NDC - 1))
            return i
        E("tensor", mm_gate, reads=K(f"WQ16_{gb}", "XQ"), writes=[psgn, psun])
        E("vector", lambda V, psg=psg: V.tensor_tensor(out=EV1[:, :], in0=psg[:, 0:TO],
                                                       in1=SCG[:, :], op=OP.mult),
          reads=[psgn] + K("SCG"), writes=K("EV1"))
        E("scalar", lambda S: S.activation(out=EV2[:, :], in_=EV1[:, :], func=AF.Silu),
          reads=K("EV1"), writes=K("EV2"))
        E("vector", lambda V, psu=psu, ev=ev: V.tensor_tensor(out=ev[:, :], in0=EV2[:, :],
                                                              in1=psu[:, 0:TO], op=OP.mult),
          reads=[psun] + K("EV2"), writes=K(evn))
        E("gpsimd", lambda G, jp=jp, ev=ev: G.dma_start(out=hw_tile(jp), in_=ev[:, :]),
          reads=K(evn), writes=["h_scr"], sem=f"hw{jp % 2}", inc=16)
        E("scalar", lambda S, jp=jp, ev=ev: S.activation(out=YH2p[:, :], in_=ev[:, :],
                                                         func=AF.Abs,
                                                         scale=g2s[:, jp:jp + 1]),
          reads=K(evn, "g2s"), writes=K("YH2p"))
        if jp == 0:
            E("vector", lambda V: V.tensor_copy(out=MMH[:, :], in_=YH2p[:, :]),
              reads=K("YH2p"), writes=K("MMH"))
        else:
            E("vector", lambda V: V.tensor_tensor(out=MMH[:, :], in0=MMH[:, :],
                                                  in1=YH2p[:, :], op=OP.max),
              reads=K("YH2p", "MMH"), writes=K("MMH"))
        E("scalar", lambda S, ev=ev: S.activation(out=SQF[:, :], in_=ev[:, :],
                                                  func=AF.Square),
          reads=K(evn), writes=K("SQF"))
        if jp == 0:
            E("vector", lambda V: V.tensor_copy(out=HSQA[:, :], in_=SQF[:, :]),
              reads=K("SQF"), writes=K("HSQA"))
        else:
            E("vector", lambda V: V.tensor_tensor(out=HSQA[:, :], in0=HSQA[:, :],
                                                  in1=SQF[:, :], op=OP.add),
              reads=K("SQF", "HSQA"), writes=K("HSQA"))

    def mm_hsum(TE):
        for j in range(2):
            i = TE.matmul(out=PS[3][:, j * 512:(j + 1) * 512], lhsT=ONESF[:, :],
                          rhs=HSQA[:, j * 512:(j + 1) * 512], start=True, stop=True)
        return i
    E("tensor", mm_hsum, reads=K("HSQA", "ONESF"), writes=["PS3"])
    E("vector", lambda V: V.tensor_scalar(out=HS1[:, :], in0=PS[3][:, 0:TO],
                                          scalar1=1.0 / INTER, scalar2=None, op0=OP.mult),
      reads=["PS3"], writes=K("HS1"))
    E("vector", lambda V: V.tensor_tensor(out=HS2[:, :], in0=SCG[:, :], in1=SCG[:, :],
                                          op=OP.mult),
      reads=K("SCG"), writes=K("HS2"))
    E("vector", lambda V: V.tensor_tensor(out=HS1[:, :], in0=HS1[:, :], in1=HS2[:, :],
                                          op=OP.mult),
      reads=K("HS2", "HS1"), writes=K("HS1"))
    E("scalar", lambda S: S.activation(out=HS2[:, :], in_=HS1[:, :], func=AF.Ln,
                                       bias=epsv[:, 0:1]),
      reads=K("HS1", "epsv"), writes=K("HS2"))
    E("scalar", lambda S: S.activation(out=HS1[:, :], in_=HS2[:, :], func=AF.Exp, scale=-0.5),
      reads=K("HS2"), writes=K("HS1"))
    compact_max(MMH, ["MMH"], MCOMP, "MCOMP")
    bcast_cols(MCOMP, "MCOMP", HS2, ["HS2"], 0)
    E("vector", lambda V: V.tensor_tensor(out=YH1[:, :], in0=SCG[:, :], in1=HS2[:, :],
                                          op=OP.mult),
      reads=K("SCG", "HS2"), writes=K("YH1"))
    E("vector", lambda V: V.tensor_tensor(out=YH1[:, :], in0=YH1[:, :], in1=HS1[:, :],
                                          op=OP.mult),
      reads=K("HS1", "YH1"), writes=K("YH1"))
    E("vector", lambda V: V.tensor_scalar(out=YH2[:, :], in0=YH1[:, :], scalar1=EPS,
                                          scalar2=None, op0=OP.max),
      reads=K("YH1"), writes=K("YH2"))
    E("vector", lambda V: V.reciprocal(out=YH1[:, :], in_=YH2[:, :]),
      reads=K("YH2"), writes=K("YH1"))
    E("vector", lambda V: V.tensor_tensor(out=CCH[:, :], in0=HS1[:, :], in1=YH1[:, :],
                                          op=OP.mult),
      reads=K("HS1", "YH1"), writes=K("CCH"))
    E("vector", lambda V: V.tensor_tensor(out=CCH[:, :], in0=CCH[:, :], in1=SCG[:, :],
                                          op=OP.mult),
      reads=K("SCG", "CCH"), writes=K("CCH"))
    E("vector", lambda V: V.tensor_scalar(out=CCH[:, :], in0=CCH[:, :], scalar1=127.0,
                                          scalar2=None, op0=OP.mult),
      reads=K("CCH"), writes=K("CCH"))
    E("vector", lambda V: V.tensor_scalar(out=SCD[:, :], in0=YH2[:, :], scalar1=SCAL[:, 1:2],
                                          scalar2=0.5 / 127.0, op0=OP.mult, op1=OP.mult),
      reads=K("YH2", "SCAL"), writes=K("SCD"))
    tapk("cch", CCH, ["CCH"])
    tapk("scd", SCD, ["SCD"])


    pb_units = []

    def _pbA(jp):
        E("sync", lambda S, jp=jp: S.dma_start(out=HB[:, :], in_=hw_tile(jp)),
          reads=["h_scr"], writes=K("HB"), sem="hr0", inc=16)
        E("vector", lambda V, jp=jp: V.scalar_tensor_tensor(
            out=LYH1[:, :], in0=HB[:, :], scalar=g2s[:, jp:jp + 1], in1=CCH[:, :],
            op0=OP.mult, op1=OP.mult),
          reads=K("HB", "g2s", "CCH"), writes=K("LYH1"))
        E("vector", lambda V: V.tensor_copy(out=LQD[:, :], in_=LYH1[:, :]),
          reads=K("LYH1"), writes=K("LQD"))

    def _pbS(i):
        m = (64.0, 32.0, 16.0, 8.0, 4.0, 2.0, 1.0)[i]
        E("scalar", lambda S, i=i: S.activation(out=LQC[:, :], in_=LQD[:, :],
                                                func=AF.Sign,
                                                bias=LADB[:, 2 * i + 1:2 * i + 2]),
          reads=K("LQD", "LADB"), writes=K("LQC"))
        E("scalar", lambda S, i=i: S.activation(out=LQF[:, :], in_=LQD[:, :],
                                                func=AF.Sign,
                                                bias=LADB[:, 2 * i:2 * i + 1]),
          reads=K("LQD", "LADB"), writes=K("LQF"))
        E("vector", lambda V: V.tensor_tensor(out=LQC[:, :], in0=LQC[:, :],
                                              in1=LQF[:, :], op=OP.add),
          reads=K("LQC", "LQF"), writes=K("LQC"))
        E("vector", lambda V, m=m: V.scalar_tensor_tensor(
            out=LQD[:, :], in0=LQC[:, :], scalar=-0.5 * m, in1=LQD[:, :],
            op0=OP.mult, op1=OP.add),
          reads=K("LQC", "LQD"), writes=K("LQD"))

    def _pbC(jp):
        b = jp % 2
        xt = XQHT[b]
        xtn = f"XQHT{b}"
        E("vector", lambda V: V.tensor_tensor(out=LQF[:, :], in0=LYH1[:, :], in1=LQD[:, :],
                                              op=OP.subtract),
          reads=K("LYH1", "LQD"), writes=K("LQF"))
        E("vector", lambda V, xt=xt: V.tensor_scalar(out=xt[:, :], in0=LQF[:, :],
                                                     scalar1=-128.0, scalar2=127.0,
                                                     op0=OP.max, op1=OP.min),
          reads=K("LQF"), writes=K(xtn))
        E("gpsimd", lambda G, jp=jp, xt=xt: G.dma_start(
            out=xqh_scr[jp * 128:(jp + 1) * 128, :], in_=xt[:, :]),
          reads=K(xtn), writes=["xqh_scr"], sem=f"xqo{b}", inc=16)

    for jp in range(NHC):
        pb_units.append(lambda jp=jp: _pbA(jp))
        for i in range(7):
            pb_units.append(lambda i=i: _pbS(i))
        pb_units.append(lambda jp=jp: _pbC(jp))
    pb_state = [0]

    def pb_pump(n):
        while pb_state[0] < len(pb_units) and n > 0:
            pb_units[pb_state[0]]()
            pb_state[0] += 1
            n -= 1

    for ci, (dst, srcp, nm) in enumerate([(cosT, cosTin, "cosT"), (sinT, sinTin, "sinT")]):
        E("sync", lambda S, d=dst, s=srcp: S.dma_start(out=d[:, :], in_=s),
          writes=K(nm), sem=f"ct{ci}", inc=16)

    for g in range(H // 2):
        gb = g % 2
        E("gpsimd", lambda G, g=g, gb=gb: G.dma_start(
            out=WVG[gb][:, :], in_=wv_b[g * 128:(g + 1) * 128, :]),
          writes=K(f"WVG{gb}"), sem=f"wv{gb}", inc=16)
        for tt in range(16):
            psn = 2 + (tt % 2)

            def mm_vt(TE, tt=tt, gb=gb, psn=psn):
                for dc in range(NDC):
                    TE.ldweights(XN16[:, dc * T + tt * 128: dc * T + (tt + 1) * 128])
                    i = TE.matmul(out=PS[psn][:, 0:256],
                                  lhsT=XN16[:, dc * T + tt * 128: dc * T + (tt + 1) * 128],
                                  rhs=WVG[gb][:, dc * 256:(dc + 1) * 256],
                                  start=(dc == 0), stop=(dc == NDC - 1))
                return i
            E("tensor", mm_vt, reads=K("XN16", f"WVG{gb}"), writes=[f"PS{psn}"])
            E("vector", lambda V, tt=tt, psn=psn: V.tensor_copy(
                out=VT16[:, tt * 128:(tt + 1) * 128], in_=PS[psn][:, 0:128]),
              reads=[f"PS{psn}"], writes=K("VT16"))
            E("vector", lambda V, tt=tt, psn=psn: V.tensor_copy(
                out=VT16[:, T + tt * 128: T + (tt + 1) * 128], in_=PS[psn][:, 128:256]),
              reads=[f"PS{psn}"], writes=K("VT16"))
            pb_pump(1)

        for hh in range(2):
            head = 2 * g + hh
            E("gpsimd", lambda G, head=head: G.dma_start(
                out=WQK[0][:, :], in_=wqkv_b[head * 128:(head + 1) * 128, :]),
              writes=K("WQK0"), sem="wq0", inc=16)
            E("gpsimd", lambda G, head=head: G.dma_start(
                out=WQK[1][:, :],
                in_=wqkv_b[(16 + head) * 128:(16 + head + 1) * 128, :]),
              writes=K("WQK1"), sem="wk0", inc=16)

            def mm_q(TE):
                for dc in range(NDC):
                    TE.ldweights(WQK[0][:, dc * 128:(dc + 1) * 128])
                    for j in range(2):
                        i = TE.matmul(out=PS[0][:, j * 512:(j + 1) * 512],
                                      lhsT=WQK[0][:, dc * 128:(dc + 1) * 128],
                                      rhs=XN16[:, dc * T + j * 512: dc * T + (j + 1) * 512],
                                      start=(dc == 0), stop=(dc == NDC - 1))
                return i
            E("tensor", mm_q, reads=K("WQK0", "XN16"), writes=["PS0"])

            def mm_k(TE):
                for dc in range(NDC):
                    TE.ldweights(WQK[1][:, dc * 128:(dc + 1) * 128])
                    for half in range(2):
                        for j in range(2):
                            i = TE.matmul(
                                out=PS[1 + half][:, j * 512:(j + 1) * 512],
                                lhsT=WQK[1][:, dc * 128:(dc + 1) * 128],
                                rhs=XN16[:, dc * T + half * TO + j * 512:
                                         dc * T + half * TO + (j + 1) * 512],
                                start=(dc == 0), stop=(dc == NDC - 1))
                return i
            E("tensor", mm_k, reads=K("WQK1", "XN16"), writes=["PS1", "PS2"])

            def rope_half(psn, dst, dstnm, dst_c0, n, tab_c0):
                E("vector", lambda V: V.tensor_copy(out=RTa[:, 0:n], in_=PS[psn][0:64, 0:n]),
                  reads=[f"PS{psn}"], writes=K("RTa"))
                E("vector", lambda V: V.tensor_copy(out=RTb[:, 0:n], in_=PS[psn][64:128, 0:n]),
                  reads=[f"PS{psn}"], writes=K("RTb"))
                E("vector", lambda V: V.tensor_tensor(
                    out=RTc[:, 0:n], in0=RTa[:, 0:n],
                    in1=cosT[0:64, tab_c0:tab_c0 + n], op=OP.mult),
                  reads=K("RTa", "cosT"), writes=K("RTc"))
                E("vector", lambda V: V.tensor_tensor(
                    out=RTd[:, 0:n], in0=RTb[:, 0:n],
                    in1=sinT[0:64, tab_c0:tab_c0 + n], op=OP.mult),
                  reads=K("RTb", "sinT"), writes=K("RTd"))
                E("vector", lambda V: V.tensor_tensor(
                    out=dst[0:64, dst_c0:dst_c0 + n], in0=RTc[:, 0:n], in1=RTd[:, 0:n],
                    op=OP.subtract),
                  reads=K("RTc", "RTd"), writes=K(dstnm))
                E("vector", lambda V: V.tensor_tensor(
                    out=RTc[:, 0:n], in0=RTb[:, 0:n],
                    in1=cosT[0:64, tab_c0:tab_c0 + n], op=OP.mult),
                  reads=K("RTb", "cosT"), writes=K("RTc"))
                E("vector", lambda V: V.tensor_tensor(
                    out=RTd[:, 0:n], in0=RTa[:, 0:n],
                    in1=sinT[0:64, tab_c0:tab_c0 + n], op=OP.mult),
                  reads=K("RTa", "sinT"), writes=K("RTd"))
                E("vector", lambda V: V.tensor_tensor(
                    out=dst[64:128, dst_c0:dst_c0 + n], in0=RTc[:, 0:n], in1=RTd[:, 0:n],
                    op=OP.add),
                  reads=K("RTc", "RTd"), writes=K(dstnm))

            rope_half(0, Q16, "Q16", 0, TO, 0)
            rope_half(1, K16, "K16", 0, TO, 0)
            rope_half(2, K16, "K16", TO, TO, TO)

            E("scalar", lambda S: S.activation(out=SQH[:, 0:TO], in_=Q16[:, :], func=AF.Square),
              reads=K("Q16"), writes=K("SQH"))

            def mm_mq(TE):
                for j in range(2):
                    TE.ldweights(ones16[:, :])
                    i = TE.matmul(out=PS[3][:, j * 512:(j + 1) * 512], lhsT=ones16[:, :],
                                  rhs=SQH[:, j * 512:(j + 1) * 512], start=True, stop=True)
                return i
            E("tensor", mm_mq, reads=K("SQH", "ones16"), writes=["PS3"])
            E("vector", lambda V: V.reduce_max(out=MQ[:, :], in_=PS[3][:, 0:TO], axis=AX.X),
              reads=["PS3"], writes=K("MQ"))
            E("scalar", lambda S: S.activation(out=SQH[:, :], in_=K16[:, :], func=AF.Square),
              reads=K("K16"), writes=K("SQH"))

            def mm_mk(TE):
                for half in range(2):
                    for j in range(2):
                        TE.ldweights(ones16[:, :])
                        i = TE.matmul(out=PS[3][:, j * 512:(j + 1) * 512], lhsT=ones16[:, :],
                                      rhs=SQH[:, half * TO + j * 512: half * TO + (j + 1) * 512],
                                      start=(half == 0), stop=(half == 1))
                return i
            E("tensor", mm_mk, reads=K("SQH", "ones16"), writes=["PS3"])
            E("vector", lambda V: V.reduce_max(out=MK[:, :], in_=PS[3][:, 0:TO], axis=AX.X),
              reads=["PS3"], writes=K("MK"))
            E("vector", lambda V: V.tensor_tensor(out=MQ[:, :], in0=MQ[:, :], in1=MK[:, :],
                                                  op=OP.mult),
              reads=K("MK", "MQ"), writes=K("MQ"))
            E("scalar", lambda S: S.activation(out=MK[:, :], in_=MQ[:, :], func=AF.Sqrt),
              reads=K("MQ"), writes=K("MK"))
            E("vector", lambda V: V.tensor_scalar(out=NEGM[:, :], in0=MK[:, :], scalar1=-RS,
                                                  scalar2=None, op0=OP.mult),
              reads=K("MK"), writes=K("NEGM"))

            def mm_s(TE, kt):
                TE.ldweights(K16[:, kt * 128:(kt + 1) * 128])
                for j in range(2):
                    i = TE.matmul(out=PS[1 + (kt % 2)][:, j * 512:(j + 1) * 512],
                                  lhsT=K16[:, kt * 128:(kt + 1) * 128],
                                  rhs=Q16[:, j * 512:(j + 1) * 512], start=True, stop=True)
                return i
            E("tensor", lambda TE: mm_s(TE, 0), reads=K("K16", "Q16"), writes=["PS1"])
            for kt in range(16):
                psn = 1 + (kt % 2)
                pb = PB[kt % 2]
                pbn = f"PB{kt % 2}"
                E("scalar", lambda S, psn=psn, pb=pb: S.activation(
                    out=pb[:, :], in_=PS[psn][:, 0:TO], func=AF.Exp, bias=NEGM[:, 0:1], scale=RS),
                  reads=[f"PS{psn}"] + K("NEGM"), writes=K(pbn))
                if kt < 15:
                    E("tensor", lambda TE, kt=kt: mm_s(TE, kt + 1), reads=K("K16", "Q16"),
                      writes=[f"PS{1 + ((kt + 1) % 2)}"])

                def mm_sum(TE, kt=kt, pb=pb):
                    TE.ldweights(ones16[:, :])
                    for j in range(2):
                        i = TE.matmul(out=PS[3][:, j * 512:(j + 1) * 512], lhsT=ones16[:, :],
                                      rhs=pb[:, j * 512:(j + 1) * 512],
                                      start=(kt == 0), stop=(kt == 15))
                    return i
                E("tensor", mm_sum, reads=K(pbn, "ones16"), writes=["PS3"])

                def mm_o(TE, kt=kt, hh=hh, pb=pb):
                    TE.ldweights(VT16[:, hh * T + kt * 128: hh * T + (kt + 1) * 128])
                    for j in range(2):
                        i = TE.matmul(out=PS[0][:, j * 512:(j + 1) * 512],
                                      lhsT=VT16[:, hh * T + kt * 128: hh * T + (kt + 1) * 128],
                                      rhs=pb[:, j * 512:(j + 1) * 512],
                                      start=(kt == 0), stop=(kt == 15))
                    return i
                E("tensor", mm_o, reads=K(pbn, "VT16"), writes=["PS0"])
                pb_pump(1)

            E("vector", lambda V: V.reciprocal(out=RECB[:, :], in_=PS[3][:, 0:TO]),
              reads=["PS3"], writes=K("RECB"))
            E("vector", lambda V, head=head: V.tensor_tensor(
                out=O16[:, head * TO:(head + 1) * TO], in0=PS[0][:, 0:TO], in1=RECB[:, :],
                op=OP.mult),
              reads=["PS0"] + K("RECB"), writes=K("O16"))
            pb_pump(2)
    tapk("o16", O16, ["O16"])

    pb_pump(10 ** 9)
    for hc in range(40):
        E("sync", lambda S, hc=hc: S.dma_start(out=XQHC0[:, hc * TO:(hc + 1) * TO],
                                               in_=xqh_scr[hc * 128:(hc + 1) * 128, :]),
          reads=["xqh_scr"], writes=K("XQHC0"), sem=f"xl{hc % 2}", inc=16)

    for dco in range(NDC):
        wb = dco % 2
        E("gpsimd", lambda G, dc=dco, wb=wb: G.dma_start(
            out=WOB[wb][:, :], in_=wout_b[dc * 128:(dc + 1) * 128, :]),
          writes=K(f"WOB{wb}"), sem=f"wo{wb}", inc=16)

        def mm_sp(TE, wb=wb):
            for oc in range(NDC):
                TE.ldweights(WOB[wb][:, oc * 128:(oc + 1) * 128])
                for j in range(2):
                    i = TE.matmul(out=PS[0][:, j * 512:(j + 1) * 512],
                                  lhsT=WOB[wb][:, oc * 128:(oc + 1) * 128],
                                  rhs=O16[:, oc * TO + j * 512: oc * TO + (j + 1) * 512],
                                  start=(oc == 0), stop=(oc == NDC - 1))
            return i
        E("tensor", mm_sp, reads=K(f"WOB{wb}", "O16"), writes=["PS0"])
        E("vector", lambda V, dc=dco: V.tensor_copy(out=S16[:, dc * TO:(dc + 1) * TO],
                                                    in_=PS[0][:, 0:TO]),
          reads=["PS0"], writes=K("S16"))
    tapk("s16", S16, ["S16"])
    for hc in range(40, NHC):
        E("sync", lambda S, hc=hc: S.dma_start(
            out=XQHC1[:, (hc - 40) * TO:(hc - 39) * TO],
            in_=xqh_scr[hc * 128:(hc + 1) * 128, :]),
          reads=["xqh_scr"], writes=K("XQHC1"), sem=f"xl{2 + hc % 2}", inc=16)

    def xqh_res(hc):
        if hc < 40:
            return XQHC0[:, hc * TO:(hc + 1) * TO], "XQHC0"
        return XQHC1[:, (hc - 40) * TO:(hc - 39) * TO], "XQHC1"

    def down_load_quant(dc):
        b = dc % 2
        for hf in range(2):
            E("sync", lambda S, dc=dc, hf=hf: S.dma_start(
                out=DWC[hf][:, :],
                in_=dwA[:, dc * 5632 + hf * 2816: dc * 5632 + (hf + 1) * 2816]),
              writes=K(f"DWC{hf}"), sem=f"dwc{hf}", inc=16)
            for o, w in ((0, 1024), (1024, 1024), (2048, 768)):
                E("scalar", lambda S, hf=hf, o=o, w=w: S.activation(
                    out=UD1[:, 0:w], in_=DWC[hf][:, o:o + w], func=AF.Sign,
                    scale=SCAL[:, 3:4], bias=HLF[:, 1:2]),
                  reads=K(f"DWC{hf}", "SCAL", "HLF"), writes=K("UD1"))
                E("scalar", lambda S, hf=hf, o=o, w=w: S.activation(
                    out=UD2[:, 0:w], in_=DWC[hf][:, o:o + w], func=AF.Sign,
                    scale=SCAL[:, 3:4], bias=HLF[:, 0:1]),
                  reads=K(f"DWC{hf}", "SCAL", "HLF"), writes=K("UD2"))
                E("vector", lambda V, b=b, hf=hf, o=o, w=w: V.tensor_tensor(
                    out=WQDC[b][:, hf * 2816 + o: hf * 2816 + o + w],
                    in0=UD1[:, 0:w], in1=UD2[:, 0:w], op=OP.add),
                  reads=K("UD1", "UD2"), writes=K(f"WQDC{b}"))

    down_load_quant(0)
    for dc in range(NDC):
        b = dc % 2
        if dc + 1 < NDC:
            down_load_quant(dc + 1)

        def mm_dn(TE, b=b):
            for hc in range(NHC):
                xt_, _ = xqh_res(hc)
                TE.ldweights(WQDC[b][:, hc * 128:(hc + 1) * 128])
                for j in range(2):
                    i = TE.matmul(out=PS[b][:, j * 512:(j + 1) * 512],
                                  lhsT=WQDC[b][:, hc * 128:(hc + 1) * 128],
                                  rhs=xt_[:, j * 512:(j + 1) * 512],
                                  start=(hc == 0), stop=(hc == NHC - 1))
            return i
        E("tensor", mm_dn, reads=K(f"WQDC{b}", "XQHC0", "XQHC1"), writes=[f"PS{b}"])
        E("sync", lambda S, dc=dc: S.dma_start(out=XR[:, :], in_=xchunk(dc)[:, 0:TO]),
          writes=K("XR"), sem="xr", inc=16)
        E("vector", lambda V, b=b: V.tensor_tensor(out=OBA[:, :], in0=PS[b][:, 0:TO],
                                                   in1=SCD[:, :], op=OP.mult),
          reads=[f"PS{b}"] + K("SCD"), writes=K("OBA"))
        E("scalar", lambda S: S.activation(out=OBB[:, :], in_=OBA[:, :], func=AF.Silu),
          reads=K("OBA"), writes=K("OBB"))
        E("gpsimd", lambda G, dc=dc: G.tensor_tensor(out=OBB[:, :], in0=OBB[:, :],
                                                     in1=S16[:, dc * TO:(dc + 1) * TO],
                                                     op=OP.mult),
          reads=K("S16", "OBB"), writes=K("OBB"))
        E("vector", lambda V: V.tensor_tensor(out=OBA[:, :], in0=OBB[:, :], in1=XR[:, :],
                                              op=OP.add),
          reads=K("XR", "OBB"), writes=K("OBA"))
        E("gpsimd", lambda G, dc=dc: G.dma_start(
            out=outT[dc * 128:(dc + 1) * 128, :], in_=OBA[:, :]),
          reads=K("OBA"), sem=f"ost{dc % 2}", inc=16)

    p.final_wait("gpsimd", "ost0", 16 * (NDC // 2))
    p.final_wait("gpsimd", "ost1", 16 * (NDC // 2))
    p.realize()
    return nc, taps


_BUILD_CACHE = {}


def _get_nc():
    key = tuple(sorted(DEBUG_TAPS))
    if key not in _BUILD_CACHE:
        _BUILD_CACHE[key] = build()
    return _BUILD_CACHE[key]


def stage_inputs(L0_attn_x, L0_attn_w_qkv, L0_attn_w_out, L0_ffn_gate_w, L0_ffn_down_w,
                 L0_ffn_gate_norm_w, L0_ffn_down_norm_w):
    x = np.asarray(L0_attn_x, np.float32)
    wqkvT = np.asarray(L0_attn_w_qkv, np.float32).T
    wqkv_b = np.ascontiguousarray(
        wqkvT[:, :2 * D].reshape(NDC, 128, 32, 128).transpose(2, 1, 0, 3).reshape(32 * 128, NDC * 128))
    wv_b = np.ascontiguousarray(
        wqkvT[:, 2 * D:].reshape(NDC, 128, 8, 256).transpose(2, 1, 0, 3).reshape(8 * 128, NDC * 256))
    woutT = np.asarray(L0_attn_w_out, np.float32).T
    wout_b = np.ascontiguousarray(
        woutT.reshape(NDC, 128, NDC, 128).transpose(2, 1, 0, 3).reshape(NDC * 128, NDC * 128))
    gwT = np.asarray(L0_ffn_gate_w, np.float32).T
    idx = []
    for j in range(NHC):
        idx.extend(range(j * 128, (j + 1) * 128))
        idx.extend(range(INTER + j * 128, INTER + (j + 1) * 128))
    gwTp = gwT[:, idx]
    gw_b = np.ascontiguousarray(
        gwTp.reshape(NDC, 128, NHC, 256).transpose(2, 1, 0, 3).reshape(NHC * 128, NDC * 256))
    dwTc = np.asarray(L0_ffn_down_w, np.float32).T
    dwA = np.ascontiguousarray(
        dwTc.reshape(NHC, 128, NDC, 128).transpose(1, 2, 0, 3).reshape(128, NDC * NHC * 128))
    g1t = np.ascontiguousarray(np.asarray(L0_ffn_gate_norm_w, np.float32).reshape(NDC, 128).T)
    g2t = np.ascontiguousarray(np.asarray(L0_ffn_down_norm_w, np.float32).reshape(NHC, 128).T)
    onesb16 = np.ones((128, 128), ml_dtypes.bfloat16)
    identf = np.eye(128, dtype=np.float32)
    cvec = np.tile(np.array([[EPS, 0.0]], np.float32), (128, 1))
    cosE, sinE = _rope_tables_f32()
    in_maps = []
    for c in range(8):
        pair, half = c // 2, c % 2
        xp = x[pair].T
        perm = np.concatenate([np.arange(half * TO, (half + 1) * TO),
                               np.arange((1 - half) * TO, (2 - half) * TO)])
        in_maps.append(dict(
            xT=np.ascontiguousarray(xp[:, perm]),
            wqkv_b=wqkv_b, wv_b=wv_b, wout_b=wout_b, gw_b=gw_b, dwA=dwA, g1t=g1t, g2t=g2t,
            onesb16=onesb16, identf=identf,
            cosTin=np.ascontiguousarray(cosE[perm].T),
            sinTin=np.ascontiguousarray(sinE[perm].T),
            cvec=cvec,
        ))
    return in_maps


def kernel(L0_attn_x, L0_attn_w_qkv, L0_attn_w_out, L0_ffn_gate_w, L0_ffn_down_w,
           L0_ffn_gate_norm_w, L0_ffn_down_norm_w):
    in_maps = stage_inputs(L0_attn_x, L0_attn_w_qkv, L0_attn_w_out, L0_ffn_gate_w,
                           L0_ffn_down_w, L0_ffn_gate_norm_w, L0_ffn_down_norm_w)
    nc, taps = _get_nc()
    res = run_bass_kernel_spmd(nc, in_maps, core_ids=list(range(8)))
    out = np.empty((4, T, D), np.float32)
    for c in range(8):
        pair, half = c // 2, c % 2
        out[pair, half * TO:(half + 1) * TO, :] = np.asarray(res.results[c]["outT"]).T
    if DEBUG_TAPS:
        kernel.tap_results = [
            {("tap_" + n): np.asarray(r.get("tap_" + n)) for n in taps} for r in res.results]
    return out


# version 18
# speedup vs baseline: 1.5919x; 1.0047x over previous
import math
import numpy as np
import ml_dtypes
import concourse.bass as bass
import concourse.mybir as mybir
from concourse.bass_utils import run_bass_kernel_spmd

F32 = mybir.dt.float32
BF16 = mybir.dt.bfloat16
I32 = mybir.dt.int32
AF = mybir.ActivationFunctionType
OP = mybir.AluOpType
AX = mybir.AxisListType

D = 2048
T = 2048
TO = 1024
H = 16
INTER = 5632
NDC = D // 128
NHC = INTER // 128
EPS = 1e-5
RS = 1.0 / math.sqrt(128.0)
LN1E4 = math.log(10000.0)
TWO_PI = 2.0 * math.pi
C1 = 6.28125
C2 = TWO_PI - C1
PI_F32 = float(np.float32(np.pi))
PI_LO = float(np.nextafter(np.float32(np.pi), np.float32(0.0)))

ENGINES = ("sync", "vector", "scalar", "tensor", "gpsimd")
ESEM = {"vector": "dv", "scalar": "ac", "tensor": "pe", "gpsimd": "gp", "sync": "sp"}

DEBUG_TAPS = set()

_INV_FREQ_BITS = np.array([1065353216,1063104471,1061157137,1059470816,1058010522,1056527311,1054337179,1052440603,1050798235,1049376004,1047712803,1045579757,1043732615,1042133056,1040747895,1038909398,1036831949,1035032952,1033475085,1032126029,1030116802,1028093503,1026341397,1024824136,1023510243,1021334738,1019364175,1017657739,1016180025,1014779187,1012562930,1010643729,1008981770,1007542573,1005959600,1003801109,1001931932,1000313292,998911607,997151246,995049016,993228559,991652109,990286959,988353832,986306396,984533390,982998030,981668463,979567074,977573004,975846212,974350870,973033388,970790693,968848599,967166816,965710450,964208661,962024420,960132947,958494998,957076594,955395298], dtype=np.uint32)


_TABLE_CACHE = {}


def _rope_tables_f32():
    if "t" in _TABLE_CACHE:
        return _TABLE_CACHE["t"]
    import os
    import subprocess
    import sys
    import tempfile
    inv = _INV_FREQ_BITS.view(np.float32)
    t = np.arange(T, dtype=np.float32)
    fr = np.einsum("i,j->ij", t, inv).astype(np.float32)
    emb = np.concatenate([fr, fr], axis=-1)
    cos = sin = None
    try:
        with tempfile.TemporaryDirectory() as td:
            np.save(os.path.join(td, "emb.npy"), emb)
            prog = ("import numpy as np, jax.numpy as jnp, sys, os;"
                    "td = sys.argv[1];"
                    "e = jnp.asarray(np.load(os.path.join(td, 'emb.npy')));"
                    "np.save(os.path.join(td, 'cos.npy'), np.asarray(jnp.cos(e)));"
                    "np.save(os.path.join(td, 'sin.npy'), np.asarray(jnp.sin(e)));"
                    "print('TBLOK')")
            env = {k: v for k, v in os.environ.items() if k != "PYTHONPATH"}
            env["JAX_PLATFORMS"] = "cpu"
            r = subprocess.run([sys.executable, "-c", prog, td], env=env,
                               capture_output=True, text=True, timeout=600)
            if "TBLOK" in r.stdout:
                cos = np.load(os.path.join(td, "cos.npy")).astype(np.float32)
                sin = np.load(os.path.join(td, "sin.npy")).astype(np.float32)
    except Exception:
        cos = None
    if cos is None:
        cos = np.cos(emb.astype(np.float64)).astype(np.float32)
        sin = np.sin(emb.astype(np.float64)).astype(np.float32)
    _TABLE_CACHE["t"] = (cos, sin)
    return cos, sin


class Prog:
    def __init__(self, nc, ctx):
        self.nc = nc
        self.ctx = ctx
        self.ops = {e: [] for e in ENGINES}
        self.sem_handles = {}
        self.counts = {}
        self.writers = {}
        self.readers = {}
        self.waited = {e: {} for e in ENGINES}

    def sem(self, name):
        if name not in self.sem_handles:
            self.sem_handles[name] = self.ctx.enter_context(self.nc.semaphore(name))
            self.counts[name] = 0
        return self.sem_handles[name]

    def _wait(self, eng, sname, val):
        if val <= 0 or self.waited[eng].get(sname, 0) >= val:
            return
        self.waited[eng][sname] = val
        h = self.sem_handles[sname]
        self.ops[eng].append(lambda E, h=h, v=val: E.wait_ge(h, v))

    def emit(self, eng, fn, reads=(), writes=(), sem=None, inc=1):
        for k in reads:
            for s, c in self.writers.get(k, {}).items():
                self._wait(eng, s, c)
        for k in writes:
            for s, c in self.writers.get(k, {}).items():
                self._wait(eng, s, c)
            for s, c in self.readers.get(k, {}).items():
                self._wait(eng, s, c)
        sname = sem or ESEM[eng]
        self.sem(sname)
        self.counts[sname] += inc
        cnt = self.counts[sname]
        h = self.sem_handles[sname]

        def run(E, fn=fn, h=h, inc=inc):
            fn(E).then_inc(h, inc)
        self.ops[eng].append(run)
        for k in writes:
            self.writers.setdefault(k, {})[sname] = cnt
            self.readers.pop(k, None)
        for k in reads:
            self.readers.setdefault(k, {})[sname] = cnt
        return sname, cnt

    def final_wait(self, eng, sname, val):
        h = self.sem(sname)
        self.ops[eng].append(lambda E, h=h, v=val: E.wait_ge(h, v))

    def realize(self):
        with self.nc.Block() as block:
            @block.sync
            def _(sync):
                for f in self.ops["sync"]:
                    f(sync)

            @block.vector
            def _(vector):
                for f in self.ops["vector"]:
                    f(vector)

            @block.scalar
            def _(scalar):
                for f in self.ops["scalar"]:
                    f(scalar)

            @block.tensor
            def _(tensor):
                for f in self.ops["tensor"]:
                    f(tensor)

            @block.gpsimd
            def _(gpsimd):
                for f in self.ops["gpsimd"]:
                    f(gpsimd)


def build():
    import contextlib
    nc = bass.Bass("TRN2", target_bir_lowering=False, debug=False)
    ctx = contextlib.ExitStack()
    p = Prog(nc, ctx)
    E = p.emit

    xT = nc.dram_tensor("xT", [D, T], F32, kind="ExternalInput").ap()
    wqkv_b = nc.dram_tensor("wqkv_b", [32 * 128, NDC * 128], F32, kind="ExternalInput").ap()
    wv_b = nc.dram_tensor("wv_b", [8 * 128, NDC * 256], F32, kind="ExternalInput").ap()
    wout_b = nc.dram_tensor("wout_b", [NDC * 128, NDC * 128], F32, kind="ExternalInput").ap()
    gw_b = nc.dram_tensor("gw_b", [NHC * 128, NDC * 256], F32, kind="ExternalInput").ap()
    dwA = nc.dram_tensor("dwA", [128, NDC * NHC * 128], F32, kind="ExternalInput").ap()
    g1t = nc.dram_tensor("g1t", [128, NDC], F32, kind="ExternalInput").ap()
    g2t = nc.dram_tensor("g2t", [128, NHC], F32, kind="ExternalInput").ap()
    onesb16 = nc.dram_tensor("onesb16", [128, 128], BF16, kind="ExternalInput").ap()
    identf = nc.dram_tensor("identf", [128, 128], F32, kind="ExternalInput").ap()
    cosTin = nc.dram_tensor("cosTin", [128, T], F32, kind="ExternalInput").ap()
    sinTin = nc.dram_tensor("sinTin", [128, T], F32, kind="ExternalInput").ap()
    cvec = nc.dram_tensor("cvec", [128, 2], F32, kind="ExternalInput").ap()

    outT = nc.dram_tensor("outT", [D, TO], F32, kind="ExternalOutput").ap()
    h_scr = nc.dram_tensor("h_scr", [INTER, TO], F32, kind="Internal").ap()
    xqh_scr = nc.dram_tensor("xqh_scr", [INTER, TO], BF16, kind="Internal").ap()
    sc_scr = nc.dram_tensor("sc_scr", [1, 4], F32, kind="Internal").ap()

    akey = {}

    def place(name, shape, dt, off, keys):
        t = nc.alloc_sbuf_tensor_at(name, shape, dt, offset=off)
        akey[name] = (keys,) if isinstance(keys, str) else tuple(keys)
        return t

    Z0 = 16544
    Z1 = Z0 + 81920
    Z2 = Z1 + 32768
    Z3 = Z2 + 32768
    Z4 = Z3 + 32768
    Z5 = Z4 + 24576
    XN16 = place("XN16", [128, NDC * T], BF16, Z0, "z0a")
    cosT = place("cosT", [128, T], F32, Z0 + 65536, "z0b")
    sinT = place("sinT", [128, T], F32, Z0 + 73728, "z0c")
    GWF = [place("GWF0", [128, NDC * 256], F32, Z1, ("z1a", "z1b")),
           place("GWF1", [128, NDC * 256], F32, Z1 + 16384,
                 ("z1c", "z1d", "z1e", "z1f"))]
    WQ16 = [place("WQ16_0", [128, NDC * 256], BF16, Z4, ("z4a", "z4b")),
            place("WQ16_1", [128, NDC * 256], BF16, Z4 + 16384, ("z4e", "z4f"))]
    U1 = place("U1", [128, 2048], F32, Z0 + 65536, "z0b")
    U2 = place("U2", [128, 2048], F32, Z0 + 73728, "z0c")
    RSTD = place("RSTD", [128, T], F32, Z1, "z1a")
    MUSC = place("MUSC", [128, T], F32, Z1 + 8192, "z1b")
    RRMS = place("RRMS", [128, TO], F32, Z1 + 16384, "z1c")
    CG = place("CG", [128, TO], F32, Z1 + 20480, "z1d")
    MM1 = place("MM1", [128, TO], F32, Z1 + 24576, "z1e")
    K16 = place("K16", [128, T], BF16, Z1, "z1a")
    VT16 = place("VT16", [128, 2 * T], BF16, Z1 + 4096, ("z1a", "z1b"))
    Q16 = place("Q16", [128, TO], BF16, Z1 + 12288, "z1b")
    SQH = place("SQH", [128, T], BF16, Z1 + 14336, ("z1b", "z1c"))
    RECB = place("RECB", [128, TO], F32, Z1 + 18432, ("z1c", "z1d"))
    LNS = place("LNS", [128, TO], F32, Z1 + 22528, ("z1d", "z1e"))
    PB = [place("PB0", [128, TO], BF16, Z1 + 26624, "z1e"),
          place("PB1", [128, TO], BF16, Z1 + 28672, "z1f")]
    S16 = place("S16", [128, NDC * TO], BF16, Z1,
                ("z1a", "z1b", "z1c", "z1d", "z1e", "z1f"))
    XB = place("XB", [128, T], F32, Z2, "z2a")
    XB1 = place("XB1", [128, T], F32, Z3, ("z3a",))
    SQB = place("SQB", [128, T], F32, Z2 + 8192, "z2b")
    TMP1 = place("TMP1", [128, T], F32, Z2 + 16384, "z2c")
    TMP2 = place("TMP2", [128, T], F32, Z2 + 24576, "z2d")
    QT = place("QT", [128, TO], F32, Z2 + 8192, "z2b")
    QI = place("QI", [128, TO], I32, Z2 + 12288, "z2b")
    QC = place("QC", [128, TO], F32, Z2 + 16384, "z2c")
    QN = place("QN", [128, TO], F32, Z2 + 20480, "z2c")
    QF = place("QF", [128, TO], F32, Z2 + 24576, "z2d")
    QD = place("QD", [128, TO], F32, Z2 + 28672, "z2d")
    WQK = [place("WQK0", [128, NDC * 128], BF16, Z2, "z2a"),
           place("WQK1", [128, NDC * 128], BF16, Z2 + 4096, "z2a")]
    LQD = place("LQD", [128, TO], F32, Z2 + 8192, "z2b")
    LQC = place("LQC", [128, TO], F32, Z2 + 12288, "z2b")
    WVG = [place("WVG0", [128, NDC * 256], BF16, Z2 + 16384, "z2c"),
           place("WVG1", [128, NDC * 256], BF16, Z2 + 24576, "z2d")]
    WOB = [place("WOB0", [128, NDC * 128], BF16, Z2, "z2a"),
           place("WOB1", [128, NDC * 128], BF16, Z2 + 4096, "z2a")]
    EV1 = place("EV1", [128, TO], F32, Z2, "z2a")
    EV2 = place("EV2", [128, TO], F32, Z2 + 4096, "z2a")
    EV3 = [place("EV3a", [128, TO], F32, Z2 + 8192, "z2b"),
           place("EV3b", [128, TO], F32, Z2 + 12288, "z2b")]
    HB = place("HB", [128, TO], F32, Z4 + 4096, "z4b")
    LYH1 = place("LYH1", [128, TO], F32, Z4 + 8192, "z4c")
    LQF = place("LQF", [128, TO], F32, Z4 + 12288, "z4d")
    YH1 = place("YH1", [128, TO], F32, Z2 + 8192, "z2b")
    YH2 = place("YH2", [128, TO], F32, Z2 + 12288, "z2b")
    HS1 = place("HS1", [128, TO], F32, Z2 + 16384, "z2c")
    HS2 = place("HS2", [128, TO], F32, Z2 + 20480, "z2c")
    YH2p = place("YH2p", [128, TO], F32, Z2 + 16384, "z2c")
    HSQA = place("HSQA", [128, TO], F32, Z2 + 20480, "z2c")
    SQF = place("SQF", [128, TO], F32, Z2 + 24576, "z2d")
    XQHC0 = place("XQHC0", [128, 40 * TO], BF16, Z0, ("z0a", "z0b", "z0c"))
    XQHC1 = place("XQHC1", [128, 4 * TO], BF16, Z3, "z3a")
    DWC = [place("DWC0", [128, 2816], F32, Z2, ("z2a", "z2b")),
           place("DWC1", [128, 2816], F32, Z2 + 11264, ("z2b", "z2c"))]
    UD1 = place("UD1", [128, 1024], F32, Z2 + 22528, ("z2c", "z2d"))
    UD2 = place("UD2", [128, 1024], F32, Z2 + 26624, "z2d")
    WQDC = [place("WQDC0", [128, 5632], BF16, Z4, ("z4a", "z4b", "z4c")),
            place("WQDC1", [128, 5632], BF16, Z3 + 20480, "z3b")]
    OBA = place("OBA", [128, TO], F32, Z4 + 12288, "z4d")
    OBB = place("OBB", [128, TO], F32, Z4 + 16384, "z4e")
    XR = place("XR", [128, TO], F32, Z3 + 16384, "z3b")
    XQ = place("XQ", [128, NDC * TO], BF16, Z3, ("z3a", "z3b"))
    O16 = place("O16", [128, H * TO], BF16, Z3, ("z3a", "z3b"))
    SQ16 = place("SQ16", [128, T], BF16, Z4, "z4a")
    XB16 = place("XB16", [128, T], BF16, Z4 + 4096, "z4b")
    WRB = [place("WRB0", [128, 2048], BF16, Z4 + 8192, "z4c"),
           place("WRB1", [128, 2048], BF16, Z4 + 12288, "z4d")]
    RTa = place("RTa", [64, TO], F32, Z4 + 16384, "z4e")
    RTb = place("RTb", [64, TO], F32, Z1 + 22528, ("z1d", "z1e"))
    RTc = place("RTc", [64, TO], F32, Z1 + 18432, ("z1c", "z1d"))
    RTd = place("RTd", [64, TO], F32, Z1 + 14336, ("z1b", "z1c"))
    SCG = place("SCG", [128, TO], F32, Z4 + 8192, "z4c")
    MMH = place("MMH", [128, TO], F32, Z4 + 12288, "z4d")
    CCH = place("CCH", [128, TO], F32, Z4, "z4a")
    SCD = place("SCD", [128, TO], F32, Z4 + 20480, "z4f")
    z5 = [Z5]

    def small(name, shape, dt=F32):
        t = place(name, shape, dt, z5[0], name)
        sz = shape[1] * {F32: 4, BF16: 2, I32: 4}[dt]
        z5[0] += (sz + 31) // 32 * 32
        return t

    g1s = small("g1s", [128, NDC])
    g2s = small("g2s", [128, NHC])
    ones16 = small("ones16", [128, 128], BF16)
    idf = small("idf", [128, 128])
    epsv = small("epsv", [128, 2])
    SCAL = small("SCAL", [128, 4])
    SCLT = small("SCLT", [1, 4])
    MCOMP = small("MCOMP", [128, 8])
    MQ = small("MQ", [128, 1])
    MK = small("MK", [128, 1])
    NEGM = small("NEGM", [128, 1])
    WACC = small("WACC", [128, 2])
    WS1 = small("WS1", [128, 1])
    WS2 = small("WS2", [128, 1])
    WS1D = small("WS1D", [128, 1])
    WS2D = small("WS2D", [128, 1])
    BCH = small("BCH", [128, 128])
    HLF = small("HLF", [128, 2])
    ONESF = small("ONESF", [128, 128])
    LADB = small("LADB", [128, 14])
    DEN_G = small("DEN_G", [128, TO])
    XQHT = [place("XQHT0", [128, TO], BF16, z5[0] - 4096, "DEN_G"),
            place("XQHT1", [128, TO], BF16, z5[0] - 2048, "DEN_G")]
    assert z5[0] <= 229376, z5[0]

    PS = [nc.alloc_psum_tensor(f"PS{i}", [128, TO], F32) for i in range(4)]

    def K(*names):
        out = []
        for n in names:
            out.extend(akey.get(n, (n,)))
        return out

    taps = []

    def tapk(name, tile, keys, rows=128, cols=None):
        if name not in DEBUG_TAPS:
            return
        c = cols if cols is not None else tile.shape[1]
        dt = tile.dtype
        o = nc.dram_tensor("tap_" + name, [rows, c], dt, kind="ExternalOutput").ap()
        taps.append(name)
        E("gpsimd", lambda G, o=o, t=tile, r=rows, c=c: G.dma_start(out=o, in_=t[0:r, 0:c]),
          reads=K(*keys), sem=f"tp{len(taps)}", inc=16)

    for ci, (dst, srcp, nm) in enumerate([(g1s, g1t, "g1s"), (g2s, g2t, "g2s"),
                                          (ones16, onesb16, "ones16"), (idf, identf, "idf"),
                                          (epsv, cvec, "epsv")]):
        E("sync", lambda S, d=dst, s=srcp: S.dma_start(out=d[:, :], in_=s),
          writes=K(nm), sem=f"cs{ci}", inc=16)

    for nm in ("WS1", "WS2", "WS1D", "WS2D"):
        t = {"WS1": WS1, "WS2": WS2, "WS1D": WS1D, "WS2D": WS2D}[nm]
        E("vector", lambda V, t=t: V.memset(t[:, :], 0.0), writes=K(nm))
    gw_flat = gw_b.rearrange("a (b c) -> (a b) c", b=2)
    wred_list = [(gw_flat[rg * 128:(rg + 1) * 128, :], 2048) for rg in range(NHC * 2)]
    wred_list += [(dwA[:, rg * 2048:(rg + 1) * 2048], 2048) for rg in range(NHC)]
    NGW = NHC * 2

    for idxw, (apw, w) in enumerate(wred_list):
        b = idxw % 2
        E("gpsimd", lambda G, b=b, apw=apw, w=w: G.dma_start(out=WRB[b][:, 0:w], in_=apw),
          writes=K(f"WRB{b}"), sem=f"wr{b}", inc=16)
        awt = SQ16 if b == 0 else XB16
        awk = "SQ16" if b == 0 else "XB16"
        E("scalar", lambda S, b=b, w=w, awt=awt: S.activation(out=awt[:, 0:w],
                                                              in_=WRB[b][:, 0:w],
                                                              func=AF.Abs),
          reads=K(f"WRB{b}"), writes=K(awk))
        isgw = idxw < NGW
        ps_lo = 0 if isgw else 2
        st = (idxw == 0) if isgw else (idxw == NGW)
        sp = (idxw == NGW - 1) if isgw else (idxw == len(wred_list) - 1)

        def mm_wred(TE, w=w, ps_lo=ps_lo, st=st, sp=sp, awt=awt):
            nj = (w + 511) // 512
            for j in range(nj):
                wj = min(512, w - j * 512)
                TE.ldweights(ones16[:, :])
                i = TE.matmul(out=PS[ps_lo + j // 2][:, (j % 2) * 512:(j % 2) * 512 + wj],
                              lhsT=ones16[:, :], rhs=awt[:, j * 512:j * 512 + wj],
                              start=st, stop=sp)
            return i
        E("tensor", mm_wred, reads=K(awk, "ones16"),
          writes=[f"PS{ps_lo}", f"PS{ps_lo + 1}"])
    E("vector", lambda V: V.reduce_sum(out=WS1[:, :], in_=PS[0][:, :], axis=AX.X),
      reads=["PS0"], writes=K("WS1"))
    E("vector", lambda V: V.reduce_sum(out=WS2[:, :], in_=PS[1][:, :], axis=AX.X),
      reads=["PS1"], writes=K("WS2"))
    E("vector", lambda V: V.reduce_sum(out=WS1D[:, :], in_=PS[2][:, :], axis=AX.X),
      reads=["PS2"], writes=K("WS1D"))
    E("vector", lambda V: V.reduce_sum(out=WS2D[:, :], in_=PS[3][:, :], axis=AX.X),
      reads=["PS3"], writes=K("WS2D"))
    E("vector", lambda V: V.tensor_tensor(out=WACC[:, 0:1], in0=WS1[:, :], in1=WS2[:, :],
                                          op=OP.add),
      reads=K("WS1", "WS2"), writes=K("WACC"))
    E("vector", lambda V: V.tensor_tensor(out=WACC[:, 1:2], in0=WS1D[:, :], in1=WS2D[:, :],
                                          op=OP.add),
      reads=K("WS1D", "WS2D"), writes=K("WACC"))
    E("vector", lambda V: V.tensor_scalar(out=SCAL[:, 0:1], in0=WACC[:, 0:1],
                                          scalar1=1.0 / (D * 2 * INTER), scalar2=EPS,
                                          op0=OP.mult, op1=OP.max),
      reads=K("WACC"), writes=K("SCAL"))
    E("vector", lambda V: V.tensor_scalar(out=SCAL[:, 1:2], in0=WACC[:, 1:2],
                                          scalar1=1.0 / (INTER * D), scalar2=EPS,
                                          op0=OP.mult, op1=OP.max),
      reads=K("WACC"), writes=K("SCAL"))
    E("vector", lambda V: V.reciprocal(out=SCAL[:, 2:3], in_=SCAL[:, 0:1]),
      reads=K("SCAL"), writes=K("SCAL"))
    E("vector", lambda V: V.reciprocal(out=SCAL[:, 3:4], in_=SCAL[:, 1:2]),
      reads=K("SCAL"), writes=K("SCAL"))

    E("vector", lambda V: V.memset(HLF[:, 0:1], 0.5), writes=K("HLF"))
    E("vector", lambda V: V.memset(ONESF[:, :], 1.0), writes=K("ONESF"))
    E("vector", lambda V: V.memset(HLF[:, 1:2], -0.5), writes=K("HLF"))
    for i, m in enumerate((64.0, 32.0, 16.0, 8.0, 4.0, 2.0, 1.0)):
        E("vector", lambda V, i=i, m=m: V.memset(LADB[:, 2 * i:2 * i + 1], m - 0.5),
          writes=K("LADB"))
        E("vector", lambda V, i=i, m=m: V.memset(LADB[:, 2 * i + 1:2 * i + 2], -(m - 0.5)),
          writes=K("LADB"))

    def xchunk(dc):
        return xT[dc * 128:(dc + 1) * 128, :]

    for dc in range(NDC):
        xb = XB if dc % 2 == 0 else XB1
        xbn = "XB" if dc % 2 == 0 else "XB1"
        E("sync", lambda S, dc=dc, xb=xb: S.dma_start(out=xb[:, :], in_=xchunk(dc)),
          writes=K(xbn), sem=f"xb{dc % 2}", inc=16)
        E("scalar", lambda S, xb=xb: S.activation(out=SQB[:, :], in_=xb[:, :], func=AF.Square),
          reads=K(xbn), writes=K("SQB"))
        E("vector", lambda V, xb=xb: V.tensor_copy(out=XB16[:, :], in_=xb[:, :]),
          reads=K(xbn), writes=K("XB16"))
        E("vector", lambda V: V.tensor_copy(out=SQ16[:, :], in_=SQB[:, :]),
          reads=K("SQB"), writes=K("SQ16"))
        st, sp = dc == 0, dc == NDC - 1

        def mm_stats(TE, st=st, sp=sp):
            for half in range(2):
                for j in range(2):
                    TE.ldweights(ones16[:, :])
                    TE.matmul(out=PS[half][:, 512 * j:512 * (j + 1)], lhsT=ones16[:, :],
                              rhs=XB16[:, half * TO + 512 * j: half * TO + 512 * (j + 1)],
                              start=st, stop=sp)
                    TE.ldweights(ones16[:, :])
                    i = TE.matmul(out=PS[2 + half][:, 512 * j:512 * (j + 1)], lhsT=ones16[:, :],
                                  rhs=SQ16[:, half * TO + 512 * j: half * TO + 512 * (j + 1)],
                                  start=st, stop=sp)
            return i
        E("tensor", mm_stats, reads=K("XB16", "SQ16", "ones16"),
          writes=["PS0", "PS1", "PS2", "PS3"])

    MUt, VARt = XB, SQB
    for half in range(2):
        E("vector", lambda V, h=half: V.tensor_scalar(out=MUt[:, h * TO:(h + 1) * TO],
                                                      in0=PS[h][:, :], scalar1=1.0 / D,
                                                      scalar2=None, op0=OP.mult),
          reads=[f"PS{half}"], writes=K("XB"))
        E("vector", lambda V, h=half: V.tensor_scalar(out=VARt[:, h * TO:(h + 1) * TO],
                                                      in0=PS[2 + h][:, :], scalar1=1.0 / D,
                                                      scalar2=None, op0=OP.mult),
          reads=[f"PS{2 + half}"], writes=K("SQB"))
    E("vector", lambda V: V.tensor_tensor(out=TMP1[:, :], in0=MUt[:, :], in1=MUt[:, :], op=OP.mult),
      reads=K("XB"), writes=K("TMP1"))
    E("vector", lambda V: V.tensor_tensor(out=VARt[:, :], in0=VARt[:, :], in1=TMP1[:, :],
                                          op=OP.subtract),
      reads=K("TMP1", "SQB"), writes=K("SQB"))
    E("scalar", lambda S: S.activation(out=TMP1[:, :], in_=VARt[:, :], func=AF.Ln,
                                       bias=epsv[:, 0:1]),
      reads=K("SQB", "epsv"), writes=K("TMP1"))
    E("scalar", lambda S: S.activation(out=RSTD[:, :], in_=TMP1[:, :], func=AF.Exp, scale=-0.5),
      reads=K("TMP1"), writes=K("RSTD"))
    E("vector", lambda V: V.tensor_tensor(out=MUSC[:, :], in0=MUt[:, :], in1=RSTD[:, :],
                                          op=OP.mult),
      reads=K("XB", "RSTD"), writes=K("MUSC"))
    E("vector", lambda V: V.tensor_scalar(out=TMP1[:, 0:TO], in0=VARt[:, 0:TO], scalar1=EPS,
                                          scalar2=None, op0=OP.add),
      reads=K("SQB"), writes=K("TMP1"))
    E("vector", lambda V: V.reciprocal(out=MM1[:, :], in_=TMP1[:, 0:TO]),
      reads=K("TMP1"), writes=K("MM1"))
    E("vector", lambda V: V.tensor_tensor(out=TMP2[:, 0:TO], in0=VARt[:, 0:TO], in1=MM1[:, :],
                                          op=OP.mult),
      reads=K("SQB", "MM1"), writes=K("TMP2"))
    E("scalar", lambda S: S.activation(out=TMP1[:, 0:TO], in_=TMP2[:, 0:TO], func=AF.Ln,
                                       bias=epsv[:, 0:1]),
      reads=K("TMP2", "epsv"), writes=K("TMP1"))
    E("scalar", lambda S: S.activation(out=RRMS[:, :], in_=TMP1[:, 0:TO], func=AF.Exp, scale=-0.5),
      reads=K("TMP1"), writes=K("RRMS"))
    tapk("rstd", RSTD, ["RSTD"])
    tapk("rrms", RRMS, ["RRMS"])

    for dc in range(NDC):
        xb = XB if dc % 2 == 0 else XB1
        xbn = "XB" if dc % 2 == 0 else "XB1"
        E("sync", lambda S, dc=dc, xb=xb: S.dma_start(out=xb[:, :], in_=xchunk(dc)),
          writes=K(xbn), sem=f"xb{dc % 2}", inc=16)
        E("vector", lambda V, xb=xb: V.tensor_tensor(out=TMP1[:, :], in0=xb[:, :], in1=RSTD[:, :],
                                              op=OP.mult),
          reads=K(xbn, "RSTD"), writes=K("TMP1"))
        E("vector", lambda V: V.tensor_tensor(out=TMP2[:, :], in0=TMP1[:, :], in1=MUSC[:, :],
                                              op=OP.subtract),
          reads=K("TMP1", "MUSC"), writes=K("TMP2"))
        E("vector", lambda V, dc=dc: V.tensor_copy(out=XN16[:, dc * T:(dc + 1) * T],
                                                   in_=TMP2[:, :]),
          reads=K("TMP2"), writes=K("XN16"))
        E("vector", lambda V, dc=dc: V.tensor_scalar(out=QT[:, :], in0=TMP2[:, 0:TO],
                                                     scalar1=g1s[:, dc:dc + 1], scalar2=None,
                                                     op0=OP.mult),
          reads=K("TMP2", "g1s"), writes=K("QT"))
        E("scalar", lambda S: S.activation(out=QF[:, :], in_=QT[:, :], func=AF.Abs),
          reads=K("QT"), writes=K("QF"))
        if dc == 0:
            E("vector", lambda V: V.tensor_copy(out=MM1[:, :], in_=QF[:, :]),
              reads=K("QF"), writes=K("MM1"))
        else:
            E("vector", lambda V: V.tensor_tensor(out=MM1[:, :], in0=MM1[:, :], in1=QF[:, :],
                                                  op=OP.max),
              reads=K("QF", "MM1"), writes=K("MM1"))
    tapk("xn16", XN16, ["XN16"])

    def compact_max(src, srckeys, dst, dstnm):
        for blk in range(8):
            E("tensor", lambda TE, blk=blk: TE.transpose(
                out=PS[0][:, 0:128], in_=src[:, blk * 128:(blk + 1) * 128], identity=idf[:, :]),
              reads=K(*srckeys) + K("idf"), writes=["PS0"])
            E("vector", lambda V, blk=blk: V.reduce_max(out=dst[:, blk:blk + 1],
                                                        in_=PS[0][:, 0:128], axis=AX.X),
              reads=["PS0"], writes=K(dstnm))

    def bcast_cols(srccomp, srcnm, dst, dstkeys, colbase, nblk=8):
        for blk in range(nblk):
            E("vector", lambda V, blk=blk: V.tensor_copy(
                out=BCH[:, :], in_=srccomp[:, blk:blk + 1].broadcast_to([128, 128])),
              reads=K(srcnm), writes=K("BCH"))
            E("tensor", lambda TE: TE.transpose(out=PS[0][:, 0:128], in_=BCH[:, :],
                                                identity=idf[:, :]),
              reads=K("BCH", "idf"), writes=["PS0"])
            E("vector", lambda V, blk=blk: V.tensor_copy(
                out=dst[:, colbase + blk * 128: colbase + (blk + 1) * 128], in_=PS[0][:, 0:128]),
              reads=["PS0"], writes=K(*dstkeys))

    compact_max(MM1, ["MM1"], MCOMP, "MCOMP")
    bcast_cols(MCOMP, "MCOMP", TMP1, ["TMP1"], 0)
    E("vector", lambda V: V.tensor_tensor(out=QT[:, :], in0=RRMS[:, :], in1=TMP1[:, 0:TO],
                                          op=OP.mult),
      reads=K("RRMS", "TMP1"), writes=K("QT"))
    E("vector", lambda V: V.tensor_scalar(out=DEN_G[:, :], in0=QT[:, :], scalar1=EPS,
                                          scalar2=None, op0=OP.max),
      reads=K("QT"), writes=K("DEN_G"))
    E("vector", lambda V: V.reciprocal(out=QF[:, :], in_=DEN_G[:, :]),
      reads=K("DEN_G"), writes=K("QF"))
    E("vector", lambda V: V.tensor_tensor(out=QT[:, :], in0=RRMS[:, :], in1=QF[:, :], op=OP.mult),
      reads=K("RRMS", "QF"), writes=K("QT"))
    E("vector", lambda V: V.tensor_scalar(out=CG[:, :], in0=QT[:, :], scalar1=127.0,
                                          scalar2=None, op0=OP.mult),
      reads=K("QT"), writes=K("CG"))
    tapk("cg", CG, ["CG"])
    tapk("deng", DEN_G, ["DEN_G"])

    def cast_correct(src_t, srckeys, dst, dstkeys, dst_c0):
        E("vector", lambda V: V.tensor_copy(out=QI[:, :], in_=src_t[:, 0:TO]),
          reads=K(*srckeys), writes=K("QI"))
        E("vector", lambda V: V.tensor_scalar(out=QF[:, :], in0=QI[:, :], scalar1=-128.0,
                                              scalar2=127.0, op0=OP.max, op1=OP.min),
          reads=K("QI"), writes=K("QF"))
        E("gpsimd", lambda G: G.tensor_tensor(out=QD[:, :], in0=src_t[:, 0:TO], in1=QF[:, :],
                                              op=OP.subtract),
          reads=K(*srckeys) + K("QF"), writes=K("QD"))
        E("vector", lambda V: V.tensor_scalar(out=QC[:, :], in0=QD[:, :], scalar1=0.5,
                                              scalar2=None, op0=OP.is_ge),
          reads=K("QD"), writes=K("QC"))
        E("vector", lambda V: V.scalar_tensor_tensor(out=QN[:, :], in0=QD[:, :], scalar=-0.5,
                                                     in1=QC[:, :], op0=OP.is_le, op1=OP.subtract),
          reads=K("QD", "QC"), writes=K("QN"))
        E("gpsimd", lambda G: G.tensor_tensor(out=dst[:, dst_c0:dst_c0 + TO], in0=QF[:, :],
                                              in1=QN[:, :], op=OP.subtract),
          reads=K("QF", "QN"), writes=K(*dstkeys))

    def ladder_round(src_t, srckeys, dst, dstkeys, dst_c0, eng2="vector"):
        E("vector", lambda V: V.tensor_copy(out=QD[:, :], in_=src_t[:, 0:TO]),
          reads=K(*srckeys), writes=K("QD"))
        for i, m in enumerate((64.0, 32.0, 16.0, 8.0, 4.0, 2.0, 1.0)):
            E(eng2, lambda X, m=m: X.tensor_scalar(out=QC[:, :], in0=QD[:, :], scalar1=m - 0.5,
                                                   scalar2=None, op0=OP.is_ge),
              reads=K("QD"), writes=K("QC"))
            E("vector", lambda X, m=m: X.scalar_tensor_tensor(out=QC[:, :], in0=QD[:, :],
                                                              scalar=-(m - 0.5), in1=QC[:, :],
                                                              op0=OP.is_le, op1=OP.subtract),
              reads=K("QD", "QC"), writes=K("QC"))
            E("vector", lambda X, m=m: X.scalar_tensor_tensor(out=QD[:, :], in0=QC[:, :], scalar=m,
                                                              in1=QD[:, :], op0=OP.mult, op1=OP.add),
              reads=K("QC", "QD"), writes=K("QD"))
        E("vector", lambda V: V.tensor_tensor(out=QF[:, :], in0=src_t[:, 0:TO], in1=QD[:, :],
                                              op=OP.subtract),
          reads=K(*srckeys) + K("QD"), writes=K("QF"))
        E("vector", lambda V: V.tensor_scalar(out=dst[:, dst_c0:dst_c0 + TO], in0=QF[:, :],
                                              scalar1=-128.0, scalar2=127.0, op0=OP.max,
                                              op1=OP.min),
          reads=K("QF"), writes=K(*dstkeys))

    for dc in range(NDC):
        E("sync", lambda S, dc=dc: S.dma_start(out=XB[:, 0:TO], in_=xchunk(dc)[:, 0:TO]),
          writes=K("XB"), sem="xb", inc=16)
        E("vector", lambda V: V.tensor_tensor(out=TMP1[:, 0:TO], in0=XB[:, 0:TO],
                                              in1=RSTD[:, 0:TO], op=OP.mult),
          reads=K("XB", "RSTD"), writes=K("TMP1"))
        E("vector", lambda V: V.tensor_tensor(out=TMP2[:, 0:TO], in0=TMP1[:, 0:TO],
                                              in1=MUSC[:, 0:TO], op=OP.subtract),
          reads=K("TMP1", "MUSC"), writes=K("TMP2"))
        E("vector", lambda V, dc=dc: V.tensor_scalar(out=QT[:, :], in0=TMP2[:, 0:TO],
                                                     scalar1=g1s[:, dc:dc + 1], scalar2=None,
                                                     op0=OP.mult),
          reads=K("TMP2", "g1s"), writes=K("QT"))
        E("vector", lambda V: V.tensor_tensor(out=QT[:, :], in0=QT[:, :], in1=CG[:, :],
                                              op=OP.mult),
          reads=K("CG", "QT"), writes=K("QT"))
        cast_correct(QT, ["QT"], XQ, ["XQ"], dc * TO)
    tapk("xq", XQ, ["XQ"])

    E("vector", lambda V: V.tensor_scalar(out=SCG[:, :], in0=DEN_G[:, :], scalar1=SCAL[:, 0:1],
                                          scalar2=0.5 / 127.0, op0=OP.mult, op1=OP.mult),
      reads=K("DEN_G", "SCAL"), writes=K("SCG"))

    def hw_tile(jp):
        return h_scr[jp * 128:(jp + 1) * 128, :]

    def gate_load_quant(jp):
        gb = jp % 2
        E("sync", lambda S, jp=jp, gb=gb: S.dma_start(
            out=GWF[gb][:, :], in_=gw_b[jp * 128:(jp + 1) * 128, :]),
          writes=K(f"GWF{gb}"), sem=f"gw{gb}", inc=16)
        for hf in range(2):
            E("scalar", lambda S, gb=gb, hf=hf: S.activation(
                out=U1[:, :], in_=GWF[gb][:, hf * 2048:(hf + 1) * 2048],
                func=AF.Sign, scale=SCAL[:, 2:3], bias=HLF[:, 1:2]),
              reads=K(f"GWF{gb}", "SCAL", "HLF"), writes=K("U1"))
            E("scalar", lambda S, gb=gb, hf=hf: S.activation(
                out=U2[:, :], in_=GWF[gb][:, hf * 2048:(hf + 1) * 2048],
                func=AF.Sign, scale=SCAL[:, 2:3], bias=HLF[:, 0:1]),
              reads=K(f"GWF{gb}", "SCAL", "HLF"), writes=K("U2"))
            E("vector", lambda V, gb=gb, hf=hf: V.tensor_tensor(
                out=WQ16[gb][:, hf * 2048:(hf + 1) * 2048], in0=U1[:, :], in1=U2[:, :],
                op=OP.add),
              reads=K("U1", "U2"), writes=K(f"WQ16_{gb}"))

    gate_load_quant(0)
    for jp in range(NHC):
        gb = jp % 2
        psg, psgn = PS[2 * gb], f"PS{2 * gb}"
        psu, psun = PS[2 * gb + 1], f"PS{2 * gb + 1}"
        ev = EV3[gb]
        evn = "EV3a" if gb == 0 else "EV3b"
        if jp + 1 < NHC:
            gate_load_quant(jp + 1)

        def mm_gate(TE, gb=gb, psg=psg, psu=psu):
            for dc in range(NDC):
                TE.ldweights(WQ16[gb][:, dc * 256: dc * 256 + 128])
                for j in range(2):
                    TE.matmul(out=psg[:, j * 512:(j + 1) * 512],
                              lhsT=WQ16[gb][:, dc * 256: dc * 256 + 128],
                              rhs=XQ[:, dc * TO + j * 512: dc * TO + (j + 1) * 512],
                              start=(dc == 0), stop=(dc == NDC - 1))
                TE.ldweights(WQ16[gb][:, dc * 256 + 128:(dc + 1) * 256])
                for j in range(2):
                    i = TE.matmul(out=psu[:, j * 512:(j + 1) * 512],
                                  lhsT=WQ16[gb][:, dc * 256 + 128:(dc + 1) * 256],
                                  rhs=XQ[:, dc * TO + j * 512: dc * TO + (j + 1) * 512],
                                  start=(dc == 0), stop=(dc == NDC - 1))
            return i
        E("tensor", mm_gate, reads=K(f"WQ16_{gb}", "XQ"), writes=[psgn, psun])
        E("vector", lambda V, psg=psg: V.tensor_tensor(out=EV1[:, :], in0=psg[:, 0:TO],
                                                       in1=SCG[:, :], op=OP.mult),
          reads=[psgn] + K("SCG"), writes=K("EV1"))
        E("scalar", lambda S: S.activation(out=EV2[:, :], in_=EV1[:, :], func=AF.Silu),
          reads=K("EV1"), writes=K("EV2"))
        E("vector", lambda V, psu=psu, ev=ev: V.tensor_tensor(out=ev[:, :], in0=EV2[:, :],
                                                              in1=psu[:, 0:TO], op=OP.mult),
          reads=[psun] + K("EV2"), writes=K(evn))
        E("gpsimd", lambda G, jp=jp, ev=ev: G.dma_start(out=hw_tile(jp), in_=ev[:, :]),
          reads=K(evn), writes=["h_scr"], sem=f"hw{jp % 2}", inc=16)
        E("scalar", lambda S, jp=jp, ev=ev: S.activation(out=YH2p[:, :], in_=ev[:, :],
                                                         func=AF.Abs,
                                                         scale=g2s[:, jp:jp + 1]),
          reads=K(evn, "g2s"), writes=K("YH2p"))
        if jp == 0:
            E("vector", lambda V: V.tensor_copy(out=MMH[:, :], in_=YH2p[:, :]),
              reads=K("YH2p"), writes=K("MMH"))
        else:
            E("vector", lambda V: V.tensor_tensor(out=MMH[:, :], in0=MMH[:, :],
                                                  in1=YH2p[:, :], op=OP.max),
              reads=K("YH2p", "MMH"), writes=K("MMH"))
        E("scalar", lambda S, ev=ev: S.activation(out=SQF[:, :], in_=ev[:, :],
                                                  func=AF.Square),
          reads=K(evn), writes=K("SQF"))
        if jp == 0:
            E("vector", lambda V: V.tensor_copy(out=HSQA[:, :], in_=SQF[:, :]),
              reads=K("SQF"), writes=K("HSQA"))
        else:
            E("vector", lambda V: V.tensor_tensor(out=HSQA[:, :], in0=HSQA[:, :],
                                                  in1=SQF[:, :], op=OP.add),
              reads=K("SQF", "HSQA"), writes=K("HSQA"))

    def mm_hsum(TE):
        for j in range(2):
            i = TE.matmul(out=PS[3][:, j * 512:(j + 1) * 512], lhsT=ONESF[:, :],
                          rhs=HSQA[:, j * 512:(j + 1) * 512], start=True, stop=True)
        return i
    E("tensor", mm_hsum, reads=K("HSQA", "ONESF"), writes=["PS3"])
    E("vector", lambda V: V.tensor_scalar(out=HS1[:, :], in0=PS[3][:, 0:TO],
                                          scalar1=1.0 / INTER, scalar2=None, op0=OP.mult),
      reads=["PS3"], writes=K("HS1"))
    E("vector", lambda V: V.tensor_tensor(out=HS2[:, :], in0=SCG[:, :], in1=SCG[:, :],
                                          op=OP.mult),
      reads=K("SCG"), writes=K("HS2"))
    E("vector", lambda V: V.tensor_tensor(out=HS1[:, :], in0=HS1[:, :], in1=HS2[:, :],
                                          op=OP.mult),
      reads=K("HS2", "HS1"), writes=K("HS1"))
    E("scalar", lambda S: S.activation(out=HS2[:, :], in_=HS1[:, :], func=AF.Ln,
                                       bias=epsv[:, 0:1]),
      reads=K("HS1", "epsv"), writes=K("HS2"))
    E("scalar", lambda S: S.activation(out=HS1[:, :], in_=HS2[:, :], func=AF.Exp, scale=-0.5),
      reads=K("HS2"), writes=K("HS1"))
    compact_max(MMH, ["MMH"], MCOMP, "MCOMP")
    bcast_cols(MCOMP, "MCOMP", HS2, ["HS2"], 0)
    E("vector", lambda V: V.tensor_tensor(out=YH1[:, :], in0=SCG[:, :], in1=HS2[:, :],
                                          op=OP.mult),
      reads=K("SCG", "HS2"), writes=K("YH1"))
    E("vector", lambda V: V.tensor_tensor(out=YH1[:, :], in0=YH1[:, :], in1=HS1[:, :],
                                          op=OP.mult),
      reads=K("HS1", "YH1"), writes=K("YH1"))
    E("vector", lambda V: V.tensor_scalar(out=YH2[:, :], in0=YH1[:, :], scalar1=EPS,
                                          scalar2=None, op0=OP.max),
      reads=K("YH1"), writes=K("YH2"))
    E("vector", lambda V: V.reciprocal(out=YH1[:, :], in_=YH2[:, :]),
      reads=K("YH2"), writes=K("YH1"))
    E("vector", lambda V: V.tensor_tensor(out=CCH[:, :], in0=HS1[:, :], in1=YH1[:, :],
                                          op=OP.mult),
      reads=K("HS1", "YH1"), writes=K("CCH"))
    E("vector", lambda V: V.tensor_tensor(out=CCH[:, :], in0=CCH[:, :], in1=SCG[:, :],
                                          op=OP.mult),
      reads=K("SCG", "CCH"), writes=K("CCH"))
    E("vector", lambda V: V.tensor_scalar(out=CCH[:, :], in0=CCH[:, :], scalar1=127.0,
                                          scalar2=None, op0=OP.mult),
      reads=K("CCH"), writes=K("CCH"))
    E("vector", lambda V: V.tensor_scalar(out=SCD[:, :], in0=YH2[:, :], scalar1=SCAL[:, 1:2],
                                          scalar2=0.5 / 127.0, op0=OP.mult, op1=OP.mult),
      reads=K("YH2", "SCAL"), writes=K("SCD"))
    tapk("cch", CCH, ["CCH"])
    tapk("scd", SCD, ["SCD"])


    pb_units = []

    def _pbA(jp):
        E("sync", lambda S, jp=jp: S.dma_start(out=HB[:, :], in_=hw_tile(jp)),
          reads=["h_scr"], writes=K("HB"), sem="hr0", inc=16)
        E("vector", lambda V, jp=jp: V.scalar_tensor_tensor(
            out=LYH1[:, :], in0=HB[:, :], scalar=g2s[:, jp:jp + 1], in1=CCH[:, :],
            op0=OP.mult, op1=OP.mult),
          reads=K("HB", "g2s", "CCH"), writes=K("LYH1"))
        E("vector", lambda V: V.tensor_copy(out=LQD[:, :], in_=LYH1[:, :]),
          reads=K("LYH1"), writes=K("LQD"))

    def _pbS(i):
        m = (64.0, 32.0, 16.0, 8.0, 4.0, 2.0, 1.0)[i]
        E("scalar", lambda S, i=i: S.activation(out=LQC[:, :], in_=LQD[:, :],
                                                func=AF.Sign,
                                                bias=LADB[:, 2 * i + 1:2 * i + 2]),
          reads=K("LQD", "LADB"), writes=K("LQC"))
        E("scalar", lambda S, i=i: S.activation(out=LQF[:, :], in_=LQD[:, :],
                                                func=AF.Sign,
                                                bias=LADB[:, 2 * i:2 * i + 1]),
          reads=K("LQD", "LADB"), writes=K("LQF"))
        E("vector", lambda V: V.tensor_tensor(out=LQC[:, :], in0=LQC[:, :],
                                              in1=LQF[:, :], op=OP.add),
          reads=K("LQC", "LQF"), writes=K("LQC"))
        E("vector", lambda V, m=m: V.scalar_tensor_tensor(
            out=LQD[:, :], in0=LQC[:, :], scalar=-0.5 * m, in1=LQD[:, :],
            op0=OP.mult, op1=OP.add),
          reads=K("LQC", "LQD"), writes=K("LQD"))

    def _pbC(jp):
        b = jp % 2
        xt = XQHT[b]
        xtn = f"XQHT{b}"
        E("vector", lambda V: V.tensor_tensor(out=LQF[:, :], in0=LYH1[:, :], in1=LQD[:, :],
                                              op=OP.subtract),
          reads=K("LYH1", "LQD"), writes=K("LQF"))
        E("vector", lambda V, xt=xt: V.tensor_scalar(out=xt[:, :], in0=LQF[:, :],
                                                     scalar1=-128.0, scalar2=127.0,
                                                     op0=OP.max, op1=OP.min),
          reads=K("LQF"), writes=K(xtn))
        E("gpsimd", lambda G, jp=jp, xt=xt: G.dma_start(
            out=xqh_scr[jp * 128:(jp + 1) * 128, :], in_=xt[:, :]),
          reads=K(xtn), writes=["xqh_scr"], sem=f"xqo{b}", inc=16)

    for jp in range(NHC):
        pb_units.append(lambda jp=jp: _pbA(jp))
        for i in range(7):
            pb_units.append(lambda i=i: _pbS(i))
        pb_units.append(lambda jp=jp: _pbC(jp))
    pb_state = [0]

    def pb_pump(n):
        while pb_state[0] < len(pb_units) and n > 0:
            pb_units[pb_state[0]]()
            pb_state[0] += 1
            n -= 1

    for ci, (dst, srcp, nm) in enumerate([(cosT, cosTin, "cosT"), (sinT, sinTin, "sinT")]):
        E("sync", lambda S, d=dst, s=srcp: S.dma_start(out=d[:, :], in_=s),
          writes=K(nm), sem=f"ct{ci}", inc=16)

    for g in range(H // 2):
        gb = g % 2
        E("gpsimd", lambda G, g=g, gb=gb: G.dma_start(
            out=WVG[gb][:, :], in_=wv_b[g * 128:(g + 1) * 128, :]),
          writes=K(f"WVG{gb}"), sem=f"wv{gb}", inc=16)
        for tt in range(16):
            psn = 2 + (tt % 2)

            def mm_vt(TE, tt=tt, gb=gb, psn=psn):
                for dc in range(NDC):
                    TE.ldweights(XN16[:, dc * T + tt * 128: dc * T + (tt + 1) * 128])
                    i = TE.matmul(out=PS[psn][:, 0:256],
                                  lhsT=XN16[:, dc * T + tt * 128: dc * T + (tt + 1) * 128],
                                  rhs=WVG[gb][:, dc * 256:(dc + 1) * 256],
                                  start=(dc == 0), stop=(dc == NDC - 1))
                return i
            E("tensor", mm_vt, reads=K("XN16", f"WVG{gb}"), writes=[f"PS{psn}"])
            E("vector", lambda V, tt=tt, psn=psn: V.tensor_copy(
                out=VT16[:, tt * 128:(tt + 1) * 128], in_=PS[psn][:, 0:128]),
              reads=[f"PS{psn}"], writes=K("VT16"))
            E("vector", lambda V, tt=tt, psn=psn: V.tensor_copy(
                out=VT16[:, T + tt * 128: T + (tt + 1) * 128], in_=PS[psn][:, 128:256]),
              reads=[f"PS{psn}"], writes=K("VT16"))
            pb_pump(1)

        for hh in range(2):
            head = 2 * g + hh
            E("gpsimd", lambda G, head=head: G.dma_start(
                out=WQK[0][:, :], in_=wqkv_b[head * 128:(head + 1) * 128, :]),
              writes=K("WQK0"), sem="wq0", inc=16)
            E("gpsimd", lambda G, head=head: G.dma_start(
                out=WQK[1][:, :],
                in_=wqkv_b[(16 + head) * 128:(16 + head + 1) * 128, :]),
              writes=K("WQK1"), sem="wk0", inc=16)

            def mm_q(TE):
                for dc in range(NDC):
                    TE.ldweights(WQK[0][:, dc * 128:(dc + 1) * 128])
                    for j in range(2):
                        i = TE.matmul(out=PS[0][:, j * 512:(j + 1) * 512],
                                      lhsT=WQK[0][:, dc * 128:(dc + 1) * 128],
                                      rhs=XN16[:, dc * T + j * 512: dc * T + (j + 1) * 512],
                                      start=(dc == 0), stop=(dc == NDC - 1))
                return i
            E("tensor", mm_q, reads=K("WQK0", "XN16"), writes=["PS0"])

            def mm_k(TE):
                for dc in range(NDC):
                    TE.ldweights(WQK[1][:, dc * 128:(dc + 1) * 128])
                    for half in range(2):
                        for j in range(2):
                            i = TE.matmul(
                                out=PS[1 + half][:, j * 512:(j + 1) * 512],
                                lhsT=WQK[1][:, dc * 128:(dc + 1) * 128],
                                rhs=XN16[:, dc * T + half * TO + j * 512:
                                         dc * T + half * TO + (j + 1) * 512],
                                start=(dc == 0), stop=(dc == NDC - 1))
                return i
            E("tensor", mm_k, reads=K("WQK1", "XN16"), writes=["PS1", "PS2"])

            def rope_half(psn, dst, dstnm, dst_c0, n, tab_c0):
                E("scalar", lambda S: S.activation(out=RTa[:, 0:n], in_=PS[psn][0:64, 0:n],
                                                   func=AF.Copy),
                  reads=[f"PS{psn}"], writes=K("RTa"))
                E("scalar", lambda S: S.activation(out=RTb[:, 0:n], in_=PS[psn][64:128, 0:n],
                                                   func=AF.Copy),
                  reads=[f"PS{psn}"], writes=K("RTb"))
                E("vector", lambda V: V.tensor_tensor(
                    out=RTc[:, 0:n], in0=RTa[:, 0:n],
                    in1=cosT[0:64, tab_c0:tab_c0 + n], op=OP.mult),
                  reads=K("RTa", "cosT"), writes=K("RTc"))
                E("vector", lambda V: V.tensor_tensor(
                    out=RTd[:, 0:n], in0=RTb[:, 0:n],
                    in1=sinT[0:64, tab_c0:tab_c0 + n], op=OP.mult),
                  reads=K("RTb", "sinT"), writes=K("RTd"))
                E("vector", lambda V: V.tensor_tensor(
                    out=dst[0:64, dst_c0:dst_c0 + n], in0=RTc[:, 0:n], in1=RTd[:, 0:n],
                    op=OP.subtract),
                  reads=K("RTc", "RTd"), writes=K(dstnm))
                E("vector", lambda V: V.tensor_tensor(
                    out=RTc[:, 0:n], in0=RTb[:, 0:n],
                    in1=cosT[0:64, tab_c0:tab_c0 + n], op=OP.mult),
                  reads=K("RTb", "cosT"), writes=K("RTc"))
                E("vector", lambda V: V.tensor_tensor(
                    out=RTd[:, 0:n], in0=RTa[:, 0:n],
                    in1=sinT[0:64, tab_c0:tab_c0 + n], op=OP.mult),
                  reads=K("RTa", "sinT"), writes=K("RTd"))
                E("vector", lambda V: V.tensor_tensor(
                    out=dst[64:128, dst_c0:dst_c0 + n], in0=RTc[:, 0:n], in1=RTd[:, 0:n],
                    op=OP.add),
                  reads=K("RTc", "RTd"), writes=K(dstnm))

            rope_half(0, Q16, "Q16", 0, TO, 0)
            rope_half(1, K16, "K16", 0, TO, 0)
            rope_half(2, K16, "K16", TO, TO, TO)

            E("scalar", lambda S: S.activation(out=SQH[:, 0:TO], in_=Q16[:, :], func=AF.Square),
              reads=K("Q16"), writes=K("SQH"))

            def mm_mq(TE):
                for j in range(2):
                    TE.ldweights(ones16[:, :])
                    i = TE.matmul(out=PS[3][:, j * 512:(j + 1) * 512], lhsT=ones16[:, :],
                                  rhs=SQH[:, j * 512:(j + 1) * 512], start=True, stop=True)
                return i
            E("tensor", mm_mq, reads=K("SQH", "ones16"), writes=["PS3"])
            E("vector", lambda V: V.reduce_max(out=MQ[:, :], in_=PS[3][:, 0:TO], axis=AX.X),
              reads=["PS3"], writes=K("MQ"))
            E("scalar", lambda S: S.activation(out=SQH[:, :], in_=K16[:, :], func=AF.Square),
              reads=K("K16"), writes=K("SQH"))

            def mm_mk(TE):
                for half in range(2):
                    for j in range(2):
                        TE.ldweights(ones16[:, :])
                        i = TE.matmul(out=PS[3][:, j * 512:(j + 1) * 512], lhsT=ones16[:, :],
                                      rhs=SQH[:, half * TO + j * 512: half * TO + (j + 1) * 512],
                                      start=(half == 0), stop=(half == 1))
                return i
            E("tensor", mm_mk, reads=K("SQH", "ones16"), writes=["PS3"])
            E("vector", lambda V: V.reduce_max(out=MK[:, :], in_=PS[3][:, 0:TO], axis=AX.X),
              reads=["PS3"], writes=K("MK"))
            E("vector", lambda V: V.tensor_tensor(out=MQ[:, :], in0=MQ[:, :], in1=MK[:, :],
                                                  op=OP.mult),
              reads=K("MK", "MQ"), writes=K("MQ"))
            E("scalar", lambda S: S.activation(out=MK[:, :], in_=MQ[:, :], func=AF.Sqrt),
              reads=K("MQ"), writes=K("MK"))
            E("vector", lambda V: V.tensor_scalar(out=NEGM[:, :], in0=MK[:, :], scalar1=-RS,
                                                  scalar2=None, op0=OP.mult),
              reads=K("MK"), writes=K("NEGM"))

            def mm_s(TE, kt):
                TE.ldweights(K16[:, kt * 128:(kt + 1) * 128])
                for j in range(2):
                    i = TE.matmul(out=PS[1 + (kt % 2)][:, j * 512:(j + 1) * 512],
                                  lhsT=K16[:, kt * 128:(kt + 1) * 128],
                                  rhs=Q16[:, j * 512:(j + 1) * 512], start=True, stop=True)
                return i
            E("tensor", lambda TE: mm_s(TE, 0), reads=K("K16", "Q16"), writes=["PS1"])
            for kt in range(16):
                psn = 1 + (kt % 2)
                pb = PB[kt % 2]
                pbn = f"PB{kt % 2}"
                E("scalar", lambda S, psn=psn, pb=pb: S.activation(
                    out=pb[:, :], in_=PS[psn][:, 0:TO], func=AF.Exp, bias=NEGM[:, 0:1], scale=RS),
                  reads=[f"PS{psn}"] + K("NEGM"), writes=K(pbn))
                if kt < 15:
                    E("tensor", lambda TE, kt=kt: mm_s(TE, kt + 1), reads=K("K16", "Q16"),
                      writes=[f"PS{1 + ((kt + 1) % 2)}"])

                def mm_sum(TE, kt=kt, pb=pb):
                    TE.ldweights(ones16[:, :])
                    for j in range(2):
                        i = TE.matmul(out=PS[3][:, j * 512:(j + 1) * 512], lhsT=ones16[:, :],
                                      rhs=pb[:, j * 512:(j + 1) * 512],
                                      start=(kt == 0), stop=(kt == 15))
                    return i
                E("tensor", mm_sum, reads=K(pbn, "ones16"), writes=["PS3"])

                def mm_o(TE, kt=kt, hh=hh, pb=pb):
                    TE.ldweights(VT16[:, hh * T + kt * 128: hh * T + (kt + 1) * 128])
                    for j in range(2):
                        i = TE.matmul(out=PS[0][:, j * 512:(j + 1) * 512],
                                      lhsT=VT16[:, hh * T + kt * 128: hh * T + (kt + 1) * 128],
                                      rhs=pb[:, j * 512:(j + 1) * 512],
                                      start=(kt == 0), stop=(kt == 15))
                    return i
                E("tensor", mm_o, reads=K(pbn, "VT16"), writes=["PS0"])
                pb_pump(1)

            E("vector", lambda V: V.reciprocal(out=RECB[:, :], in_=PS[3][:, 0:TO]),
              reads=["PS3"], writes=K("RECB"))
            E("vector", lambda V, head=head: V.tensor_tensor(
                out=O16[:, head * TO:(head + 1) * TO], in0=PS[0][:, 0:TO], in1=RECB[:, :],
                op=OP.mult),
              reads=["PS0"] + K("RECB"), writes=K("O16"))
            pb_pump(2)
    tapk("o16", O16, ["O16"])

    pb_pump(10 ** 9)
    for hc in range(40):
        E("sync", lambda S, hc=hc: S.dma_start(out=XQHC0[:, hc * TO:(hc + 1) * TO],
                                               in_=xqh_scr[hc * 128:(hc + 1) * 128, :]),
          reads=["xqh_scr"], writes=K("XQHC0"), sem=f"xl{hc % 2}", inc=16)

    for dco in range(NDC):
        wb = dco % 2
        E("gpsimd", lambda G, dc=dco, wb=wb: G.dma_start(
            out=WOB[wb][:, :], in_=wout_b[dc * 128:(dc + 1) * 128, :]),
          writes=K(f"WOB{wb}"), sem=f"wo{wb}", inc=16)

        def mm_sp(TE, wb=wb):
            for oc in range(NDC):
                TE.ldweights(WOB[wb][:, oc * 128:(oc + 1) * 128])
                for j in range(2):
                    i = TE.matmul(out=PS[0][:, j * 512:(j + 1) * 512],
                                  lhsT=WOB[wb][:, oc * 128:(oc + 1) * 128],
                                  rhs=O16[:, oc * TO + j * 512: oc * TO + (j + 1) * 512],
                                  start=(oc == 0), stop=(oc == NDC - 1))
            return i
        E("tensor", mm_sp, reads=K(f"WOB{wb}", "O16"), writes=["PS0"])
        E("vector", lambda V, dc=dco: V.tensor_copy(out=S16[:, dc * TO:(dc + 1) * TO],
                                                    in_=PS[0][:, 0:TO]),
          reads=["PS0"], writes=K("S16"))
    tapk("s16", S16, ["S16"])
    for hc in range(40, NHC):
        E("sync", lambda S, hc=hc: S.dma_start(
            out=XQHC1[:, (hc - 40) * TO:(hc - 39) * TO],
            in_=xqh_scr[hc * 128:(hc + 1) * 128, :]),
          reads=["xqh_scr"], writes=K("XQHC1"), sem=f"xl{2 + hc % 2}", inc=16)

    def xqh_res(hc):
        if hc < 40:
            return XQHC0[:, hc * TO:(hc + 1) * TO], "XQHC0"
        return XQHC1[:, (hc - 40) * TO:(hc - 39) * TO], "XQHC1"

    def down_load_quant(dc):
        b = dc % 2
        for hf in range(2):
            E("sync", lambda S, dc=dc, hf=hf: S.dma_start(
                out=DWC[hf][:, :],
                in_=dwA[:, dc * 5632 + hf * 2816: dc * 5632 + (hf + 1) * 2816]),
              writes=K(f"DWC{hf}"), sem=f"dwc{hf}", inc=16)
            for o, w in ((0, 1024), (1024, 1024), (2048, 768)):
                E("scalar", lambda S, hf=hf, o=o, w=w: S.activation(
                    out=UD1[:, 0:w], in_=DWC[hf][:, o:o + w], func=AF.Sign,
                    scale=SCAL[:, 3:4], bias=HLF[:, 1:2]),
                  reads=K(f"DWC{hf}", "SCAL", "HLF"), writes=K("UD1"))
                E("scalar", lambda S, hf=hf, o=o, w=w: S.activation(
                    out=UD2[:, 0:w], in_=DWC[hf][:, o:o + w], func=AF.Sign,
                    scale=SCAL[:, 3:4], bias=HLF[:, 0:1]),
                  reads=K(f"DWC{hf}", "SCAL", "HLF"), writes=K("UD2"))
                E("vector", lambda V, b=b, hf=hf, o=o, w=w: V.tensor_tensor(
                    out=WQDC[b][:, hf * 2816 + o: hf * 2816 + o + w],
                    in0=UD1[:, 0:w], in1=UD2[:, 0:w], op=OP.add),
                  reads=K("UD1", "UD2"), writes=K(f"WQDC{b}"))

    down_load_quant(0)
    for dc in range(NDC):
        b = dc % 2
        if dc + 1 < NDC:
            down_load_quant(dc + 1)

        def mm_dn(TE, b=b):
            for hc in range(NHC):
                xt_, _ = xqh_res(hc)
                TE.ldweights(WQDC[b][:, hc * 128:(hc + 1) * 128])
                for j in range(2):
                    i = TE.matmul(out=PS[b][:, j * 512:(j + 1) * 512],
                                  lhsT=WQDC[b][:, hc * 128:(hc + 1) * 128],
                                  rhs=xt_[:, j * 512:(j + 1) * 512],
                                  start=(hc == 0), stop=(hc == NHC - 1))
            return i
        E("tensor", mm_dn, reads=K(f"WQDC{b}", "XQHC0", "XQHC1"), writes=[f"PS{b}"])
        E("sync", lambda S, dc=dc: S.dma_start(out=XR[:, :], in_=xchunk(dc)[:, 0:TO]),
          writes=K("XR"), sem="xr", inc=16)
        E("vector", lambda V, b=b: V.tensor_tensor(out=OBA[:, :], in0=PS[b][:, 0:TO],
                                                   in1=SCD[:, :], op=OP.mult),
          reads=[f"PS{b}"] + K("SCD"), writes=K("OBA"))
        E("scalar", lambda S: S.activation(out=OBB[:, :], in_=OBA[:, :], func=AF.Silu),
          reads=K("OBA"), writes=K("OBB"))
        E("gpsimd", lambda G, dc=dc: G.tensor_tensor(out=OBB[:, :], in0=OBB[:, :],
                                                     in1=S16[:, dc * TO:(dc + 1) * TO],
                                                     op=OP.mult),
          reads=K("S16", "OBB"), writes=K("OBB"))
        E("vector", lambda V: V.tensor_tensor(out=OBA[:, :], in0=OBB[:, :], in1=XR[:, :],
                                              op=OP.add),
          reads=K("XR", "OBB"), writes=K("OBA"))
        E("gpsimd", lambda G, dc=dc: G.dma_start(
            out=outT[dc * 128:(dc + 1) * 128, :], in_=OBA[:, :]),
          reads=K("OBA"), sem=f"ost{dc % 2}", inc=16)

    p.final_wait("gpsimd", "ost0", 16 * (NDC // 2))
    p.final_wait("gpsimd", "ost1", 16 * (NDC // 2))
    p.realize()
    return nc, taps


_BUILD_CACHE = {}


def _get_nc():
    key = tuple(sorted(DEBUG_TAPS))
    if key not in _BUILD_CACHE:
        _BUILD_CACHE[key] = build()
    return _BUILD_CACHE[key]


def stage_inputs(L0_attn_x, L0_attn_w_qkv, L0_attn_w_out, L0_ffn_gate_w, L0_ffn_down_w,
                 L0_ffn_gate_norm_w, L0_ffn_down_norm_w):
    x = np.asarray(L0_attn_x, np.float32)
    wqkvT = np.asarray(L0_attn_w_qkv, np.float32).T
    wqkv_b = np.ascontiguousarray(
        wqkvT[:, :2 * D].reshape(NDC, 128, 32, 128).transpose(2, 1, 0, 3).reshape(32 * 128, NDC * 128))
    wv_b = np.ascontiguousarray(
        wqkvT[:, 2 * D:].reshape(NDC, 128, 8, 256).transpose(2, 1, 0, 3).reshape(8 * 128, NDC * 256))
    woutT = np.asarray(L0_attn_w_out, np.float32).T
    wout_b = np.ascontiguousarray(
        woutT.reshape(NDC, 128, NDC, 128).transpose(2, 1, 0, 3).reshape(NDC * 128, NDC * 128))
    gwT = np.asarray(L0_ffn_gate_w, np.float32).T
    idx = []
    for j in range(NHC):
        idx.extend(range(j * 128, (j + 1) * 128))
        idx.extend(range(INTER + j * 128, INTER + (j + 1) * 128))
    gwTp = gwT[:, idx]
    gw_b = np.ascontiguousarray(
        gwTp.reshape(NDC, 128, NHC, 256).transpose(2, 1, 0, 3).reshape(NHC * 128, NDC * 256))
    dwTc = np.asarray(L0_ffn_down_w, np.float32).T
    dwA = np.ascontiguousarray(
        dwTc.reshape(NHC, 128, NDC, 128).transpose(1, 2, 0, 3).reshape(128, NDC * NHC * 128))
    g1t = np.ascontiguousarray(np.asarray(L0_ffn_gate_norm_w, np.float32).reshape(NDC, 128).T)
    g2t = np.ascontiguousarray(np.asarray(L0_ffn_down_norm_w, np.float32).reshape(NHC, 128).T)
    onesb16 = np.ones((128, 128), ml_dtypes.bfloat16)
    identf = np.eye(128, dtype=np.float32)
    cvec = np.tile(np.array([[EPS, 0.0]], np.float32), (128, 1))
    cosE, sinE = _rope_tables_f32()
    in_maps = []
    for c in range(8):
        pair, half = c // 2, c % 2
        xp = x[pair].T
        perm = np.concatenate([np.arange(half * TO, (half + 1) * TO),
                               np.arange((1 - half) * TO, (2 - half) * TO)])
        in_maps.append(dict(
            xT=np.ascontiguousarray(xp[:, perm]),
            wqkv_b=wqkv_b, wv_b=wv_b, wout_b=wout_b, gw_b=gw_b, dwA=dwA, g1t=g1t, g2t=g2t,
            onesb16=onesb16, identf=identf,
            cosTin=np.ascontiguousarray(cosE[perm].T),
            sinTin=np.ascontiguousarray(sinE[perm].T),
            cvec=cvec,
        ))
    return in_maps


def kernel(L0_attn_x, L0_attn_w_qkv, L0_attn_w_out, L0_ffn_gate_w, L0_ffn_down_w,
           L0_ffn_gate_norm_w, L0_ffn_down_norm_w):
    in_maps = stage_inputs(L0_attn_x, L0_attn_w_qkv, L0_attn_w_out, L0_ffn_gate_w,
                           L0_ffn_down_w, L0_ffn_gate_norm_w, L0_ffn_down_norm_w)
    nc, taps = _get_nc()
    res = run_bass_kernel_spmd(nc, in_maps, core_ids=list(range(8)))
    out = np.empty((4, T, D), np.float32)
    for c in range(8):
        pair, half = c // 2, c % 2
        out[pair, half * TO:(half + 1) * TO, :] = np.asarray(res.results[c]["outT"]).T
    if DEBUG_TAPS:
        kernel.tap_results = [
            {("tap_" + n): np.asarray(r.get("tap_" + n)) for n in taps} for r in res.results]
    return out
